# version 11
# baseline (speedup 1.0000x reference)
import numpy as np
import ml_dtypes
from contextlib import ExitStack

import concourse.bass as bass
import concourse.mybir as mybir
from concourse.bass_utils import run_bass_kernel_spmd

F32 = mybir.dt.float32
BF16 = mybir.dt.bfloat16
U32 = mybir.dt.uint32
AF = mybir.ActivationFunctionType
ALU = mybir.AluOpType
AX = mybir.AxisListType

D = 1024
NCH = 8
B, S = 4, 4096
HALF = 2048
MAGIC = 0x5AFE0000

ENG2ATTR = {"t": "tensor", "v": "vector", "s": "scalar", "g": "gpsimd", "q": "sync"}
DMA_LANES = {"q": 6, "g": 4, "s": 2, "v": 2}
RR_LANES = {"g": 2}


class Buf:
    __slots__ = ("name", "w", "r")

    def __init__(self, name=""):
        self.name = name
        self.w = None
        self.r = []


class Entry:
    __slots__ = ("eng", "waits", "fn", "token", "inc")

    def __init__(self, eng, waits, fn, token, inc):
        self.eng = eng
        self.waits = waits
        self.fn = fn
        self.token = token
        self.inc = inc


class PR:

    def __init__(self, nc):
        self.nc = nc
        self.es = ExitStack()
        self.sem = {}
        self.counts = {}
        names = ["t", "v", "s", "g", "q", "pl"]
        for e, n in DMA_LANES.items():
            names += [f"d{e}{i}" for i in range(n)]
        for nm in names:
            self.sem[nm] = self.es.enter_context(nc.semaphore(nm))
            self.counts[nm] = 0
        self._lane_rr = {e: 0 for e in DMA_LANES}
        self.prog = []
        self.wm = {}
        self.pid = {}
        self._regs = {}
        self.fence = []

    def set_fence(self):
        self.fence = [(sm, v) for sm, v in self.counts.items() if v > 0]

    def sb(self, name, shape, dtype):
        es = self.scopes[-1] if getattr(self, "scopes", None) else self.es
        if not hasattr(self, "_names"):
            self._names = {}
        k = self._names.get(name, 0)
        self._names[name] = k + 1
        nm = name if k == 0 else f"{name}_{k}"
        return es.enter_context(self.nc.sbuf_tensor(nm, list(shape), dtype))

    def ps(self, name, shape, dtype=F32):
        return self.es.enter_context(self.nc.psum_tensor(name, list(shape), dtype))

    def push_scope(self):
        if not hasattr(self, "scopes"):
            self.scopes = []
        self.scopes.append(ExitStack())

    def pop_scope(self):
        self.scopes.pop().close()
        self.set_fence()

    def emit_sw(self, eng, fnv, rd=(), wr=(), extra=(), dma=True, rdall=(),
                lane=None):
        extra = list(extra)
        if dma:
            if lane is None:
                lane = self._lane_rr[eng]
                self._lane_rr[eng] = (lane + 1) % RR_LANES.get(eng, DMA_LANES[eng])
            sm = f"d{eng}{lane}"
            if self.counts[sm] > 0:
                extra.append((sm, self.counts[sm]))
        else:
            sm = eng
        waits = self._collect(eng, rd, wr, extra, rdall)
        amt = 16 if dma else 1
        self.counts[sm] += amt
        token = (sm, self.counts[sm])

        swb = getattr(self, "_swb", None)
        if swb is not None and swb[0] == eng:
            swb[2].append((waits, fnv, sm, amt))
        else:
            def fn(e, fnv=fnv, sm=sm, amt=amt):
                last = None
                for cc in e.Switch(self.pid[e], 8):
                    inst = fnv(e, cc)
                    inst.then_inc(self.sem[sm], amt)
                    last = inst
                return last
            self.prog.append(Entry(eng, waits, fn, None, 0))
        for b in rd:
            b.r.append(token)
        for b in wr:
            b.w = token
            b.r = []
        return token

    def sw_begin(self, eng):
        assert getattr(self, "_swb", None) is None
        self._swb = (eng, len(self.prog), [])

    def sw_end(self):
        eng, pos, items = self._swb
        self._swb = None
        for en in self.prog[pos:]:
            assert en.eng != eng, "same-engine entry emitted during sw batch"

        def fn(e, items=items):
            last = None
            for cc in e.Switch(self.pid[e], 8):
                for waits, fnv, sm, amt in items:
                    for s_, v_ in waits:
                        e.wait_ge(self.sem[s_], v_)
                    inst = fnv(e, cc)
                    inst.then_inc(self.sem[sm], amt)
                    last = inst
            return last
        self.prog.append(Entry(eng, [], fn, None, 0))

    def barrier_event(self, ev, flags_ap, evrow, evrowB, write_toks):
        def fnv(e, cc, ev=ev):
            return e.dma_start(
                bass.AP(flags_ap.tensor, ev * 256 + (cc % 2) * 128, [[1, 1], [1, 1]]),
                evrow[0:1, ev:ev + 1])
        tok = self.emit_sw("g", fnv, rd=[evrowB], extra=list(write_toks), dma=True)
        if not hasattr(self, "ev_toks"):
            self.ev_toks = {}
        self.ev_toks[ev] = tok

    def poll_event(self, ev, flags_ap, fb, fbB):
        target = MAGIC + ev + 1

        def fn(e, ev=ev, target=target):
            regs = self._poll_regs
            creg, mn, fi, itr, cnt, tmp = regs
            e.reg_mov(creg, 1)
            e.reg_mov(itr, 0)
            with e.While(creg):
                lane = f"dg{0}"
                e.dma_start(fb[0:1, 0:256],
                            bass.AP(flags_ap.tensor, ev * 256, [[1, 1], [1, 256]])
                            ).then_inc(self.sem["pl"], 16)
                e.reg_add(cnt, cnt, 16)
                e.wait_ge(self.sem["pl"], cnt)
                e.reg_load(mn, fb[0:1, 0:1])
                e.reg_load(fi, fb[0:1, 128:129])
                e.reg_alu(mn, mn, target, ALU.is_equal)
                e.reg_alu(fi, fi, target, ALU.is_equal)
                e.reg_alu(mn, mn, fi, ALU.mult)
                e.reg_add(itr, itr, 1)
                e.reg_alu(creg, mn, 1, ALU.is_lt)
                e.reg_alu(tmp, itr, 2000000, ALU.is_lt)
                e.reg_alu(creg, creg, tmp, ALU.mult)
            return e.engine_nop()
        extra = []
        if hasattr(self, "ev_toks") and ev in self.ev_toks:
            extra.append(self.ev_toks[ev])
        tok = self.emit("g", fn, rd=[fbB], wr=[fbB], extra=extra)
        return tok

    def _collect(self, eng, rd, wr, extra, rdall=()):
        deps = {}
        def add(tok):
            if tok is None:
                return
            sm, val = tok
            if deps.get(sm, -1) < val:
                deps[sm] = val
        for b in rd:
            add(b.w)
        for b in rdall:
            add(b.w)
            for t in b.r:
                add(t)
        for b in wr:
            add(b.w)
            for t in b.r:
                add(t)
        for t in extra:
            add(t)
        for t in self.fence:
            add(t)
        waits = []
        for sm, val in deps.items():
            key = (eng, sm)
            if self.wm.get(key, 0) >= val:
                continue
            self.wm[key] = val
            waits.append((sm, val))
        return waits

    def emit(self, eng, fn, rd=(), wr=(), extra=(), dma=False, inc=True, rdall=(),
             lane=None):
        extra = list(extra)
        if dma:
            if lane is None:
                lane = self._lane_rr[eng]
                self._lane_rr[eng] = (lane + 1) % RR_LANES.get(eng, DMA_LANES[eng])
            sm = f"d{eng}{lane}"
            if self.counts[sm] > 0:
                extra.append((sm, self.counts[sm]))
        waits = self._collect(eng, rd, wr, extra, rdall)
        if inc:
            if not dma:
                sm = eng
            amt = 16 if dma else 1
            self.counts[sm] += amt
            token = (sm, self.counts[sm])
        else:
            token, amt = None, 0
        swb = getattr(self, "_swb", None)
        if dma and inc and swb is not None and swb[0] == eng:
            swb[2].append((waits, (lambda e, cc, fn=fn: fn(e)), sm, amt))
        else:
            self.prog.append(Entry(eng, waits, fn, token, amt))
        if token is not None:
            for b in rd:
                b.r.append(token)
            for b in wr:
                b.w = token
                b.r = []
        return token

    def group(self, eng, fns, rd=(), wr=(), extra=(), dma=False):
        assert not dma
        waits = self._collect(eng, rd, wr, extra)
        for fn in fns[:-1]:
            self.prog.append(Entry(eng, waits, fn, None, 0))
            waits = []
        sm = eng
        amt = 1
        self.counts[sm] += amt
        token = (sm, self.counts[sm])
        self.prog.append(Entry(eng, waits, fns[-1], token, amt))
        for b in rd:
            b.r.append(token)
        for b in wr:
            b.w = token
            b.r = []
        return token

    def mark(self, name):
        if not hasattr(self, "marks"):
            self.marks = []
        self.marks.append((len(self.prog), name))

    def build_block(self):
        nc = self.nc
        used = {e.eng for e in self.prog}
        marks = getattr(self, "marks", [])
        with nc.Block() as block:
            for ename, attr in ENG2ATTR.items():
                if ename not in used:
                    continue

                def mk(ename):
                    def body(eng):
                        self.cur = eng
                        self.pid[eng] = eng.partition_id()
                        if ename == "g" and getattr(self, "need_polls", False):
                            self._poll_reg_ctx = [eng.register(f"plr{i}")
                                                  for i in range(6)]
                            self._poll_regs = [c.__enter__()
                                               for c in self._poll_reg_ctx]
                            cnt0 = self._poll_regs[4]
                            eng.reg_mov(cnt0, 0)
                        mi = 0
                        open_sc = None
                        for idx, e in enumerate(self.prog):
                            while mi < len(marks) and marks[mi][0] <= idx:
                                if open_sc is not None:
                                    nc.leave_named_scope(open_sc[0], open_sc[1],
                                                         False)
                                sid, _ = nc.enter_named_scope(marks[mi][1], False)
                                open_sc = (marks[mi][1], sid)
                                mi += 1
                            if e.eng != ename:
                                continue
                            for sm, val in e.waits:
                                eng.wait_ge(self.sem[sm], val)
                            inst = e.fn(eng)
                            if e.token is not None:
                                inst.then_inc(self.sem[e.token[0]], e.inc)
                        if open_sc is not None:
                            nc.leave_named_scope(open_sc[0], open_sc[1], False)
                    return body

                getattr(block, attr)(mk(ename))

    def close(self):
        self.es.close()



class Stage:
    def __init__(self, pr, NT, xin=None, xout=None, arenas=(), wg_cols=16384,
                 slab_cols=16384, fm_scoped=False):
        self.pr = pr
        self.NT = NT
        self.T = NT * 128
        self.xin = xin
        self.xout = xout
        T = self.T
        self.scr = pr.sb("ln_scr", [128, D], F32)
        self.scrB = Buf("scr")
        self.stat = pr.sb("ln_stat", [128, 8], F32)
        self.statB = Buf("stat")
        self.xt = pr.sb("x_tile", [128, 2 * D], F32)
        self.xtB = [Buf("xt0"), Buf("xt1")]
        self.ytm = pr.sb("y_tm", [128, 2 * D], BF16)
        self.ytmB = [Buf("ytm0"), Buf("ytm1")]
        if not fm_scoped:
            self.alloc_fm()
        self.pb = [pr.ps(f"pb{i}", [128, 512], F32) for i in range(6)]
        self.pbB = [Buf(f"pb{i}") for i in range(6)]
        self.ptb = pr.ps("ptb", [128, 1024], BF16)
        self.ptbB = Buf("ptb")
        self._rot = 0
        self.identf = pr.sb("identf", [128, 128], F32)
        self.identfB = Buf("identf")
        self.ident = pr.sb("ident_bf", [128, 128], BF16)
        self.identB = Buf("identB")
        self.epsc = pr.sb("epsc", [128, 2], F32)
        self.epscB = Buf("epsc")
        if "ffn" in arenas:
            self.alloc_ffn(wg_cols, slab_cols)
        if "src2" in arenas:
            self.alloc_src2()
        if "tern" in arenas:
            self.alloc_tern()

    def alloc_fm(self, xln=True, y=True):
        pr, NT, T = self.pr, self.NT, self.T
        if xln:
            self.xln_fm = pr.sb("xln_fm", [128, NCH * T], BF16)
            self.xlnfB = [Buf(f"xlnf{i}") for i in range(NT)]
        if y:
            self.yfm = pr.sb("y_fm", [128, NCH * T], BF16)
            self.yfB = [Buf(f"yf{i}") for i in range(NT)]

    def alloc_ffn(self, wg_cols=16384, slab_cols=16384):
        pr = self.pr
        self.wg = pr.sb("wg_arena", [128, wg_cols], BF16)
        self.wgB = Buf("wg")
        self.slab = pr.sb("slab_arena", [128, slab_cols], BF16)
        self.slabB = Buf("slab")
        self.dstg = pr.sb("dstg", [128, 2 * 512], BF16)
        self.dstgB = [Buf(f"dstg{i}") for i in range(2)]
        self._dstg_rot = 0

    def alloc_src2(self):
        pr = self.pr
        self.s2t = pr.sb("s2t", [128, 2 * 512], BF16)
        self.s2tB = [Buf("s2t0"), Buf("s2t1")]

    def alloc_tern(self):
        pr = self.pr
        self.wgf = pr.sb("wgf_arena", [128, 4096], F32)
        self.wgfB = Buf("wgf")
        self.tscr = pr.sb("tscr", [128, 4096], BF16)
        self.tscrB = Buf("tscr")
        self.tscr2 = pr.sb("tscr2", [128, 4096], BF16)
        self.tscr2B = Buf("tscr2")

    def bank(self):
        i = self._rot % 6
        self._rot += 1
        return self.pb[i], self.pbB[i]

    def bank4(self):
        i = self._rot % 4
        self._rot += 1
        return self.pb[i], self.pbB[i]

    def load_consts(self, ident_ap):
        pr = self.pr
        pr.emit("q", lambda e: e.dma_start(self.identf[:, :], ident_ap),
                wr=[self.identfB], dma=True)
        pr.emit("v", lambda e: e.tensor_copy(self.ident[:, :], self.identf[:, :]),
                rd=[self.identfB], wr=[self.identB])
        pr.emit("v", lambda e: e.memset(self.epsc[:, :], 1e-5), wr=[self.epscB])

    def _ln_stats(self, tile_ap, rdB):
        pr = self.pr
        st = self.stat
        pr.emit("v", lambda e: e.tensor_scalar(
            self.scr[:, :], tile_ap, 1.0, 0.0, ALU.mult, ALU.add,
            accum_out=st[:, 0:1]), rd=rdB, wr=[self.scrB, self.statB])
        pr.emit("v", lambda e: e.scalar_tensor_tensor(
            self.scr[:, :], tile_ap, 1.0, tile_ap, ALU.mult, ALU.mult,
            accum_out=st[:, 1:2]), rd=rdB, wr=[self.scrB, self.statB])
        pr.emit("v", lambda e: e.tensor_scalar(st[:, 2:3], st[:, 0:1], 1.0 / D,
                                               None, ALU.mult),
                rd=[self.statB], wr=[self.statB])
        pr.emit("v", lambda e: e.tensor_tensor(st[:, 3:4], st[:, 2:3], st[:, 2:3],
                                               ALU.mult), rd=[self.statB], wr=[self.statB])
        pr.emit("v", lambda e: e.tensor_scalar(st[:, 4:5], st[:, 1:2], 1.0 / D,
                                               st[:, 3:4], ALU.mult, ALU.subtract),
                rd=[self.statB], wr=[self.statB])
        pr.emit("s", lambda e: e.activation(st[:, 5:6], st[:, 4:5], AF.Sqrt,
                                            bias=self.epsc[:, 0:1], scale=1.0),
                rd=[self.statB, self.epscB], wr=[self.statB])
        pr.emit("v", lambda e: e.reciprocal(st[:, 6:7], st[:, 5:6]),
                rd=[self.statB], wr=[self.statB])

    def ln_in(self, tiles=None, zero_tiles=(), keep_tm=None):
        pr = self.pr
        tiles = range(self.NT) if tiles is None else tiles
        for i in tiles:
            r = i % 2
            xt = self.xt[:, r * D:(r + 1) * D]
            if i in zero_tiles:
                pr.emit("v", lambda e, xt=xt: e.memset(xt, 0.0), wr=[self.xtB[r]])
            else:
                pr.emit("q", lambda e, i=i, xt=xt, xin=self.xin: e.dma_start(
                    xt, xin[i * 128:(i + 1) * 128, :]),
                    wr=[self.xtB[r]], dma=True)
            self._ln_stats(xt, [self.xtB[r]])
            if keep_tm is None:
                dst_t, dst_B, slot = self.ytm, self.ytmB, r
            else:
                dst_t, dst_B, slot = keep_tm[0], keep_tm[1], i
            yt = dst_t[:, slot * D:(slot + 1) * D]
            pr.emit("v", lambda e, xt=xt, yt=yt: e.tensor_scalar(
                yt, xt, self.stat[:, 2:3], self.stat[:, 6:7],
                ALU.subtract, ALU.mult),
                rd=[self.xtB[r], self.statB], wr=[dst_B[slot]])
            self.tm_to_fm(dst_t, dst_B, slot, self.xln_fm, self.xlnfB, i,
                          f32=keep_tm is not None)

    def tm_to_fm(self, tm, tmB, r, fm, fmB, i, f32=False, fm_stride=None):
        pr = self.pr
        T = self.T if fm_stride is None else fm_stride
        for j0 in range(0, NCH, 4):
            fns = []
            if f32:
                bank, bankB = self.bank()
            for j in range(j0, j0 + 4):
                if f32:
                    def fn(e, r=r, j=j, bank=bank):
                        return e.transpose(
                            bank[:, (j % 4) * 128:(j % 4) * 128 + 128],
                            tm[:, r * D + j * 128:r * D + (j + 1) * 128],
                            self.identf[:, :])
                else:
                    def fn(e, r=r, j=j):
                        return e.transpose(
                            self.ptb[:, (j % 4) * 256:(j % 4) * 256 + 128],
                            tm[:, r * D + j * 128:r * D + (j + 1) * 128],
                            self.ident[:, :])
                fns.append(fn)
            if f32:
                pr.group("t", fns, rd=[tmB[r], self.identfB], wr=[bankB])
                for j in range(j0, j0 + 4):
                    def fn(e, i=i, j=j, bank=bank):
                        return e.copy(fm[:, j * T + i * 128:j * T + (i + 1) * 128],
                                      bank[:, (j % 4) * 128:(j % 4) * 128 + 128])
                    pr.emit("s", fn, rd=[bankB], wr=[fmB[i]])
            else:
                pr.group("t", fns, rd=[tmB[r], self.identB], wr=[self.ptbB])
                for j in range(j0, j0 + 4):
                    def fn(e, i=i, j=j):
                        return e.copy(fm[:, j * T + i * 128:j * T + (i + 1) * 128],
                                      self.ptb[:, (j % 4) * 256:(j % 4) * 256 + 128])
                    pr.emit("s", fn, rd=[self.ptbB], wr=[fmB[i]])

    def ln_out_add_fm(self, y_fm, yfB, tiles=None, xout=None):
        pr = self.pr
        T = self.T
        xout = self.xout if xout is None else xout
        tiles = range(self.NT) if tiles is None else tiles
        toks = []
        for i in tiles:
            r = i % 2
            for j0 in range(0, NCH, 4):
                fns = []
                for j in range(j0, j0 + 4):
                    def fn(e, i=i, j=j):
                        return e.transpose(
                            self.ptb[:, (j % 4) * 256:(j % 4) * 256 + 128],
                            y_fm[:, j * T + i * 128:j * T + (i + 1) * 128],
                            self.ident[:, :])
                    fns.append(fn)
                pr.group("t", fns,
                         rd=[yfB[i] if isinstance(yfB, list) else yfB, self.identB],
                         wr=[self.ptbB])
                for j in range(j0, j0 + 4):
                    def fn(e, r=r, j=j):
                        return e.copy(
                            self.ytm[:, r * D + j * 128:r * D + (j + 1) * 128],
                            self.ptb[:, (j % 4) * 256:(j % 4) * 256 + 128])
                    pr.emit("s", fn, rd=[self.ptbB], wr=[self.ytmB[r]])
            yt = self.ytm[:, r * D:(r + 1) * D]
            self._ln_stats(yt, [self.ytmB[r]])
            pr.emit("v", lambda e, yt=yt: e.tensor_scalar(
                self.scr[:, :], yt, self.stat[:, 2:3], self.stat[:, 6:7],
                ALU.subtract, ALU.mult),
                rd=[self.ytmB[r], self.statB], wr=[self.scrB])
            xt = self.xt[:, r * D:(r + 1) * D]
            pr.emit("q", lambda e, i=i, xt=xt, xin=self.xin: e.dma_start(
                xt, xin[i * 128:(i + 1) * 128, :]),
                wr=[self.xtB[r]], dma=True)
            pr.emit("v", lambda e, xt=xt: e.tensor_tensor(
                xt, xt, self.scr[:, :], ALU.add),
                rd=[self.scrB, self.xtB[r]], wr=[self.xtB[r]])
            toks.append(pr.emit("q", lambda e, i=i, xt=xt: e.dma_start(
                xout[i * 128:(i + 1) * 128, :], xt),
                rd=[self.xtB[r]], dma=True))
        return toks

    def ln_out_add_tm(self, src_t, srcB, tiles=None, xout=None):
        pr = self.pr
        xout = self.xout if xout is None else xout
        tiles = range(self.NT) if tiles is None else tiles
        toks = []
        for i in tiles:
            r = i % 2
            yt = src_t[:, i * D:(i + 1) * D]
            self._ln_stats(yt, [srcB[i]])
            pr.emit("v", lambda e, yt=yt: e.tensor_scalar(
                self.scr[:, :], yt, self.stat[:, 2:3], self.stat[:, 6:7],
                ALU.subtract, ALU.mult),
                rd=[srcB[i], self.statB], wr=[self.scrB])
            xt = self.xt[:, r * D:(r + 1) * D]
            pr.emit("q", lambda e, i=i, xt=xt, xin=self.xin: e.dma_start(
                xt, xin[i * 128:(i + 1) * 128, :]),
                wr=[self.xtB[r]], dma=True)
            pr.emit("v", lambda e, xt=xt: e.tensor_tensor(
                xt, xt, self.scr[:, :], ALU.add),
                rd=[self.scrB, self.xtB[r]], wr=[self.xtB[r]])
            toks.append(pr.emit("q", lambda e, i=i, xt=xt: e.dma_start(
                xout[i * 128:(i + 1) * 128, :], xt),
                rd=[self.xtB[r]], dma=True))
        return toks

    def ffn_evac(self, dst, act, scale_sb, bias_sb, ob, tb, bank, bankB, src2_mul):
        pr = self.pr
        T = self.T
        kw = {}
        sc = 1.0 if scale_sb is None else scale_sb[0][:, ob:ob + 1]
        rdeps = [bankB] + ([scale_sb[1]] if scale_sb else []) + \
                ([bias_sb[1]] if bias_sb else [])
        func = act if act is not None else AF.Identity
        if dst[0] == "sbuf":
            out_ap = dst[1][:, ob * T + tb * 512:ob * T + (tb + 1) * 512]
            outB = [dst[2][t] for t in range(tb * 4, tb * 4 + 4)]
        else:
            s = self._dstg_rot % 2
            self._dstg_rot += 1
            out_ap = self.dstg[:, s * 512:(s + 1) * 512]
            outB = [self.dstgB[s]]
        if bias_sb is not None:
            def fn(e, out_ap=out_ap, bank=bank, sc=sc):
                return e.activation(out_ap, bank[:, 0:512], func,
                                    bias=bias_sb[0][:, ob:ob + 1], scale=sc)
        else:
            def fn(e, out_ap=out_ap, bank=bank, sc=sc):
                return e.activation(out_ap, bank[:, 0:512], func, scale=sc)
        pr.emit("s", fn, rd=rdeps, wr=outB)
        if src2_mul is not None:
            sh, shB = src2_mul[1], src2_mul[2]
            r2 = (ob * 37 + tb) % 2
            s2 = self.s2t[:, r2 * 512:(r2 + 1) * 512]
            pr.emit("q", lambda e, ob=ob, tb=tb, s2=s2: e.dma_start(
                s2, bass.AP(sh.tensor, sh.offset + ob * 128 * T + tb * 512,
                            [[T, 128], [1, 512]])),
                rdall=[shB], wr=[self.s2tB[r2]], dma=True)
            pr.emit("v", lambda e, out_ap=out_ap, s2=s2: e.tensor_tensor(
                out_ap, out_ap, s2, ALU.mult),
                rd=[self.s2tB[r2]] + outB, wr=outB)
        if dst[0] == "dram":
            dh, dhB = dst[1], dst[2]
            tok = pr.emit("q", lambda e, ob=ob, tb=tb, out_ap=out_ap: e.dma_start(
                bass.AP(dh.tensor, dh.offset + ob * 128 * T + tb * 512,
                        [[T, 128], [1, 512]]), out_ap),
                rd=outB, dma=True)
            dhB.r.append(tok)



def mm_dense(st, wsb, wB, xfm, xfB, T, ob_n, ki_n, evac, w_ki_stride=None, psN=512):
    pr = st.pr
    if w_ki_stride is None:
        w_ki_stride = ob_n * 128
    tb_n = T // psN
    for ob in range(ob_n):
        for tb in range(tb_n):
            bank, bankB = st.bank()
            fns = []
            for ki in range(ki_n):
                def fn(e, ob=ob, tb=tb, ki=ki, bank=bank):
                    return e.matmul(bank[:, 0:psN],
                                    wsb[:, ki * w_ki_stride + ob * 128:
                                        ki * w_ki_stride + (ob + 1) * 128],
                                    xfm[:, ki * T + tb * psN:ki * T + (tb + 1) * psN],
                                    start=(ki == 0), stop=(ki == ki_n - 1))
                fns.append(fn)
            rdb = [wB] + [xfB[t] for t in range(tb * psN // 128, (tb + 1) * psN // 128)]
            pr.group("t", fns, rd=rdb, wr=[bankB])
            evac(ob, tb, bank, bankB)


def emit_l0f(st, aps):
    pr = st.pr
    T = st.T
    wbf = pr.sb("l0f_wbf", [128, 8 * 1024], BF16)
    wbfB = Buf("l0f_wbf")
    lup = pr.sb("l0f_lup", [50, 4], BF16); lupB = Buf("lup")
    upxT = pr.sb("l0f_upxT", [50, 1024], BF16); upxTB = Buf("upxT")
    ldn = pr.sb("l0f_ldn", [4, 100], BF16); ldnB = Buf("ldn")
    dax = pr.sb("l0f_dax", [100, 1024], BF16); daxB = Buf("dax")
    u1 = pr.sb("l0f_u1", [4, 1024], BF16); u1B = Buf("u1")
    u2 = pr.sb("l0f_u2", [100, 1024], BF16); u2B = Buf("u2")
    bsb = pr.sb("l0f_b", [128, 8], F32); bsbB = Buf("l0f_b")
    yfm, yfB = st.yfm, st.yfB

    WT = aps["L0_ffn_WT"]
    weng, wdep = wsrc(st, WT)
    pr.emit(weng, lambda e: e.dma_start(
        wbf[:, :], bass.AP(WT.tensor, 0, [[1024, 128], [131072, 8], [1, 1024]])),
        wr=[wbfB], rdall=wdep, dma=True)
    pr.emit("g", lambda e: e.dma_start(lup[:, :], aps["L0_ffn_lora_up"]), wr=[lupB], dma=True)
    pr.emit("g", lambda e: e.dma_start(upxT[:, :], aps["L0_ffn_up_auxT"]), wr=[upxTB], dma=True)
    pr.emit("g", lambda e: e.dma_start(ldn[:, :], aps["L0_ffn_lora_down"]), wr=[ldnB], dma=True)
    pr.emit("g", lambda e: e.dma_start(dax[:, :], aps["L0_ffn_down_aux"]), wr=[daxB], dma=True)
    col_vec_load(pr, aps["L0_ffn_b"], bsb, bsbB, 8)

    bank, bankB = st.bank()
    pr.group("t", [lambda e, bank=bank: e.matmul(
        bank[0:4, 0:512], lup[:, :], upxT[:, 0:512], start=True, stop=True)],
        rd=[lupB, upxTB], wr=[bankB])
    bank2, bank2B = st.bank()
    pr.group("t", [lambda e, bank2=bank2: e.matmul(
        bank2[0:4, 0:512], lup[:, :], upxT[:, 512:1024], start=True, stop=True)],
        rd=[lupB, upxTB], wr=[bank2B])
    pr.emit("s", lambda e, bank=bank: e.copy(u1[0:4, 0:512], bank[0:4, 0:512]),
            rd=[bankB], wr=[u1B])
    pr.emit("s", lambda e, bank2=bank2: e.copy(u1[0:4, 512:1024], bank2[0:4, 0:512]),
            rd=[bank2B], wr=[u1B])
    for nb in range(2):
        bank, bankB = st.bank()
        pr.group("t", [lambda e, nb=nb, bank=bank: e.matmul(
            bank[0:100, 0:512], ldn[:, :], u1[0:4, nb * 512:(nb + 1) * 512],
            start=True, stop=True)], rd=[ldnB, u1B], wr=[bankB])
        pr.emit("s", lambda e, nb=nb, bank=bank: e.copy(
            u2[0:100, nb * 512:(nb + 1) * 512], bank[0:100, 0:512]),
            rd=[bankB], wr=[u2B])
    for mo in range(8):
        for nb in range(2):
            bank, bankB = st.bank()
            pr.group("t", [lambda e, mo=mo, nb=nb, bank=bank: e.matmul(
                bank[:, 0:512], dax[:, mo * 128:(mo + 1) * 128],
                u2[0:100, nb * 512:(nb + 1) * 512], start=True, stop=True)],
                rd=[daxB, u2B], wr=[bankB])
            def fn(e, mo=mo, nb=nb, bank=bank):
                return e.tensor_tensor(
                    wbf[:, mo * 1024 + nb * 512:mo * 1024 + (nb + 1) * 512],
                    wbf[:, mo * 1024 + nb * 512:mo * 1024 + (nb + 1) * 512],
                    bank[:, 0:512], ALU.add)
            pr.emit("v", fn, rd=[bankB, wbfB], wr=[wbfB])

    def evac(ob, tb, bank, bankB):
        def fn(e, ob=ob, tb=tb, bank=bank):
            return e.activation(
                yfm[:, ob * T + tb * 512:ob * T + (tb + 1) * 512],
                bank[:, 0:512], AF.Identity, bias=bsb[:, ob:ob + 1], scale=1.0)
        pr.emit("s", fn, rd=[bankB, bsbB],
                wr=[yfB[t] for t in range(tb * 4, tb * 4 + 4)])

    mm_dense(st, wbf, wbfB, st.xln_fm, st.xlnfB, T, 8, 8, evac, w_ki_stride=1024)
    st.ln_out_add_fm(yfm, yfB)



def mean_abs_scale(st, W_dram, total, name):
    pr = st.pr
    nchunk = total // (128 * 4096)
    acc = pr.sb(f"{name}_acc", [128, nchunk], F32); accB = Buf()
    thr = pr.sb(f"{name}_thr", [128, 2], F32); thrB = Buf()
    onesf = pr.sb(f"{name}_ones", [128, 1], F32); onesB = Buf()
    onesr = pr.sb(f"{name}_onesr", [1, 128], F32); onesrB = Buf()
    tot = pr.sb(f"{name}_tot", [1, nchunk], F32); totB = Buf()
    pr.emit("v", lambda e: e.memset(onesf[:, :], 1.0), wr=[onesB])
    pr.emit("v", lambda e: e.memset(onesr[:, :], 1.0), wr=[onesrB])
    for c in range(nchunk):
        pr.emit("q", lambda e, c=c: e.dma_start(
            st.wgf[:, :], bass.AP(W_dram.tensor, W_dram.offset + c * 128 * 4096,
                                  [[4096, 128], [1, 4096]])),
            wr=[st.wgfB], dma=True)
        pr.emit("v", lambda e, c=c: e.reduce_sum(
            acc[:, c:c + 1], st.wgf[:, :], axis=AX.X, apply_absolute_value=True),
            rd=[st.wgfB], wr=[accB])
    bank, bankB = st.bank()
    pr.group("t", [lambda e, bank=bank: e.matmul(
        bank[0:1, 0:nchunk], onesf[:, 0:1], acc[:, :], start=True, stop=True)],
        rd=[onesB, accB], wr=[bankB])
    pr.emit("s", lambda e, bank=bank: e.copy(tot[0:1, :], bank[0:1, 0:nchunk]),
            rd=[bankB], wr=[totB])
    pr.emit("v", lambda e: e.reduce_sum(tot[0:1, 0:1], tot[0:1, :], axis=AX.X),
            rd=[totB], wr=[totB])
    bank, bankB = st.bank()
    pr.group("t", [lambda e, bank=bank: e.matmul(
        bank[:, 0:1], onesr[0:1, :], tot[0:1, 0:1], start=True, stop=True)],
        rd=[onesrB, totB], wr=[bankB])
    pr.emit("s", lambda e, bank=bank: e.activation(
        thr[:, 0:1], bank[:, 0:1], AF.Identity, scale=0.5 / total), rd=[bankB], wr=[thrB])
    pr.emit("s", lambda e, bank=bank: e.activation(
        thr[:, 1:2], bank[:, 0:1], AF.Identity, scale=-0.5 / total), rd=[bankB], wr=[thrB])
    return thr, thrB


def emit_l3f(st, aps):
    pr = st.pr
    nc = pr.nc
    T = st.T
    b1 = pr.sb("l3f_b1", [128, 32], F32); b1B = Buf()
    b2 = pr.sb("l3f_b2", [128, 8], F32); b2B = Buf()
    col_vec_load(pr, aps["L3_ffn_b1"], b1, b1B, 32)
    col_vec_load(pr, aps["L3_ffn_b2"], b2, b2B, 8)
    thr1 = mean_abs_scale(st, aps["L3_ffn_W1T"], 1024 * 4096, "w1")
    thr2 = mean_abs_scale(st, aps["L3_ffn_W2T"], 4096 * 1024, "w2")
    h = nc.dram_tensor("l3f_h", [4096, T], BF16, kind="Internal")
    hB = Buf("l3f_h")
    ffn_layer(st, aps["L3_ffn_W1T"], 8, 32, ("sbuf", st.xln_fm, st.xlnfB),
              ("dram", h.ap(), hB), AF.Gelu, bias_sb=(b1, b1B), OBG=4,
              tern_thr=thr1)
    ffn_layer(st, aps["L3_ffn_W2T"], 32, 8, ("dram", h.ap(), hB),
              ("sbuf", st.yfm, st.yfB), None, bias_sb=(b2, b2B), OBG=4,
              tern_thr=thr2)
    st.ln_out_add_fm(st.yfm, st.yfB)


def emit_l2f(st, aps):
    pr = st.pr
    nc = pr.nc
    T = st.T
    s1 = pr.sb("l2f_s1", [128, 32], F32); s1B = Buf()
    s2 = pr.sb("l2f_s2", [128, 32], F32); s2B = Buf()
    h1s = pr.sb("l2f_h1s", [128, 32], F32); h1sB = Buf()
    h2s = pr.sb("l2f_h2s", [128, 32], F32); h2sB = Buf()
    s3 = pr.sb("l2f_s3", [128, 8], F32); s3B = Buf()
    col_vec_load(pr, aps["L2_ffn_w1_s"], s1, s1B, 32)
    col_vec_load(pr, aps["L2_ffn_w2_s"], s2, s2B, 32)
    col_vec_load(pr, aps["L2_ffn_h1_s"], h1s, h1sB, 32)
    col_vec_load(pr, aps["L2_ffn_h2_s"], h2s, h2sB, 32)
    col_vec_load(pr, aps["L2_ffn_w3_s"], s3, s3B, 8)
    hA = nc.dram_tensor("l2f_hA", [4096, T], BF16, kind="Internal"); hAB = Buf()
    hBt = nc.dram_tensor("l2f_hB", [4096, T], BF16, kind="Internal"); hBB = Buf()
    hC = nc.dram_tensor("l2f_hC", [4096, T], BF16, kind="Internal"); hCB = Buf()
    ffn_layer(st, aps["L2_ffn_w1_wT"], 8, 32, ("sbuf", st.xln_fm, st.xlnfB),
              ("dram", hA.ap(), hAB), AF.Silu, scale_sb=(s1, s1B), OBG=8)
    ffn_layer(st, aps["L2_ffn_w2_wT"], 8, 32, ("sbuf", st.xln_fm, st.xlnfB),
              ("dram", hBt.ap(), hBB), None, scale_sb=(s2, s2B), OBG=8,
              src2_mul=("dram", hA.ap(), hAB))
    ffn_layer(st, aps["L2_ffn_h1_wT"], 32, 32, ("dram", hBt.ap(), hBB),
              ("dram", hC.ap(), hCB), AF.Gelu, scale_sb=(h1s, h1sB), OBG=4)
    hA2B = Buf()
    ffn_layer(st, aps["L2_ffn_h2_wT"], 32, 32, ("dram", hC.ap(), hCB),
              ("dram", hA.ap(), hA2B), AF.Gelu, scale_sb=(h2s, h2sB), OBG=4)
    ffn_layer(st, aps["L2_ffn_w3_wT"], 32, 8, ("dram", hA.ap(), hA2B),
              ("sbuf", st.yfm, st.yfB), None, scale_sb=(s3, s3B), OBG=4)
    st.ln_out_add_fm(st.yfm, st.yfB)



def build_stage_tl(emitter, wspecs, NT=16, arenas=("ffn", "tern", "src2")):
    nc = bass.Bass("TRN2", target_bir_lowering=False)
    T = NT * 128
    xin = nc.dram_tensor("xin", [T, D], F32, kind="ExternalInput")
    ident = nc.dram_tensor("ident", [128, 128], F32, kind="ExternalInput")
    xout = nc.dram_tensor("xout", [T, D], F32, kind="ExternalOutput")
    aps = {"xin": xin.ap(), "ident": ident.ap(), "xout": xout.ap()}
    for name, shape in wspecs.items():
        aps[name] = nc.dram_tensor(name, list(shape), F32, kind="ExternalInput").ap()
    pr = PR(nc)
    st = Stage(pr, NT, xin=aps["xin"], xout=aps["xout"], arenas=arenas)
    st.load_consts(aps["ident"])
    st.ln_in()
    emitter(st, aps)
    toks = [(sm, v) for sm, v in pr.counts.items() if v > 0]
    pr.emit("g", lambda e: e.engine_nop(), extra=toks)
    pr.build_block()
    pr.close()
    return nc


L0F_WSPECS = {
    "L0_ffn_WT": (1024, 1024), "L0_ffn_b": (1024,),
    "L0_ffn_lora_up": (50, 4), "L0_ffn_lora_down": (4, 100),
    "L0_ffn_down_aux": (100, 1024), "L0_ffn_up_auxT": (50, 1024),
}

L3F_WSPECS = {
    "L3_ffn_W1T": (1024, 4096), "L3_ffn_b1": (4096,),
    "L3_ffn_W2T": (4096, 1024), "L3_ffn_b2": (1024,),
}

L2F_WSPECS = {
    "L2_ffn_w1_wT": (1024, 4096), "L2_ffn_w1_s": (4096,),
    "L2_ffn_w2_wT": (1024, 4096), "L2_ffn_w2_s": (4096,),
    "L2_ffn_h1_wT": (4096, 4096), "L2_ffn_h1_s": (4096,),
    "L2_ffn_h2_wT": (4096, 4096), "L2_ffn_h2_s": (4096,),
    "L2_ffn_w3_wT": (4096, 1024), "L2_ffn_w3_s": (1024,),
}


def host_l0f_weights(I):
    return {
        "L0_ffn_WT": np.ascontiguousarray(I["L0_ffn_W"].T),
        "L0_ffn_b": I["L0_ffn_b"],
        "L0_ffn_lora_up": I["L0_ffn_lora_up"],
        "L0_ffn_lora_down": I["L0_ffn_lora_down"],
        "L0_ffn_down_aux": I["L0_ffn_down_aux"],
        "L0_ffn_up_auxT": np.ascontiguousarray(I["L0_ffn_up_aux"].T),
    }


def host_l3f_weights(I):
    return {
        "L3_ffn_W1T": np.ascontiguousarray(I["L3_ffn_W1"].T),
        "L3_ffn_b1": I["L3_ffn_b1"],
        "L3_ffn_W2T": np.ascontiguousarray(I["L3_ffn_W2"].T),
        "L3_ffn_b2": I["L3_ffn_b2"],
    }


def host_l2f_weights(I):
    return {
        "L2_ffn_w1_wT": np.ascontiguousarray(I["L2_ffn_w1_w"].T),
        "L2_ffn_w1_s": I["L2_ffn_w1_s"],
        "L2_ffn_w2_wT": np.ascontiguousarray(I["L2_ffn_w2_w"].T),
        "L2_ffn_w2_s": I["L2_ffn_w2_s"],
        "L2_ffn_h1_wT": np.ascontiguousarray(I["L2_ffn_h1_w"].T),
        "L2_ffn_h1_s": I["L2_ffn_h1_s"],
        "L2_ffn_h2_wT": np.ascontiguousarray(I["L2_ffn_h2_w"].T),
        "L2_ffn_h2_s": I["L2_ffn_h2_s"],
        "L2_ffn_w3_wT": np.ascontiguousarray(I["L2_ffn_w3_w"].T),
        "L2_ffn_w3_s": I["L2_ffn_w3_s"],
    }


def col_vec_load(pr, dram_ap, sb_tile, buf, n, eng="q"):
    for j in range(n):
        pr.emit(eng, lambda e, j=j: e.dma_start(
            sb_tile[:, j:j + 1], bass.AP(dram_ap.tensor, dram_ap.offset + j * 128,
                                         [[1, 128], [1, 1]])),
            wr=[buf], dma=True)


def wsrc(st, ap):
    if ap.dtype == BF16:
        dep = getattr(st, "wdeps", {}).get(ap.tensor.name)
        return "q", ([dep] if dep is not None else [])
    return "g", []


def ffn_layer(st, W_dram, ki_n, ob_n, src, dst, act, scale_sb=None, bias_sb=None,
              OBG=8, tern_thr=None, src2_mul=None):
    pr = st.pr
    T = st.T
    tbn = T // 512
    wg, wgB = st.wg, st.wgB
    wgf, wgfB = (st.wgf, st.wgfB) if tern_thr is not None else (None, None)
    if src[0] == "sbuf":
        slab, slabB = None, None
    else:
        slab, slabB = st.slab, st.slabB
    src_T = T if len(src) < 4 else src[3]
    if src[0] == "dram_sw":
        pr.sw_begin("q")
    for og0 in range(0, ob_n, OBG):
        obg = min(OBG, ob_n - og0)
        if tern_thr is None:
            weng, wdep = wsrc(st, W_dram)
            nk = 4 if ki_n % 4 == 0 else (2 if ki_n % 2 == 0 else 1)
            kk = ki_n // nk
            wtoks = []
            for c in range(nk):
                def fn(e, og0=og0, obg=obg, c=c, kk=kk):
                    return e.dma_start(
                        wg[:, c * kk * obg * 128:(c + 1) * kk * obg * 128],
                        bass.AP(W_dram.tensor,
                                W_dram.offset + og0 * 128
                                + c * kk * ob_n * 128 * 128,
                                [[ob_n * 128, 128], [ob_n * 128 * 128, kk],
                                 [1, obg * 128]]))
                tok = pr.emit(weng, fn, wr=[wgB] if c == 0 else (),
                              rdall=wdep, dma=True)
                if c > 0:
                    wgB.r.append(tok)
                wtoks.append(tok)
        else:
            wtoks = []
            thr, thrB = tern_thr
            CH = 4096
            kstep = CH // (obg * 128)
            for k0 in range(0, ki_n, kstep):
                kk = min(kstep, ki_n - k0)
                c0 = k0 * obg * 128
                c1 = c0 + kk * obg * 128
                def fn(e, og0=og0, obg=obg, k0=k0, kk=kk):
                    return e.dma_start(
                        wgf[:, 0:kk * obg * 128],
                        bass.AP(W_dram.tensor,
                                W_dram.offset + og0 * 128 + k0 * ob_n * 128 * 128,
                                [[ob_n * 128, 128], [ob_n * 128 * 128, kk],
                                 [1, obg * 128]]))
                pr.emit("q", fn, wr=[wgfB], dma=True)
                pr.emit("v", lambda e, c0=c0, c1=c1: e.tensor_scalar(
                    st.tscr[:, 0:c1 - c0], wgf[:, 0:c1 - c0], thr[:, 0:1], None,
                    ALU.is_ge), rd=[wgfB, thrB], wr=[st.tscrB])
                pr.emit("v", lambda e, c0=c0, c1=c1: e.tensor_scalar(
                    st.tscr2[:, 0:c1 - c0], wgf[:, 0:c1 - c0], thr[:, 1:2], None,
                    ALU.is_le), rd=[wgfB, thrB], wr=[st.tscr2B])
                pr.emit("v", lambda e, c0=c0, c1=c1: e.tensor_tensor(
                    wg[:, c0:c1], st.tscr[:, 0:c1 - c0], st.tscr2[:, 0:c1 - c0],
                    ALU.subtract), rd=[st.tscrB, st.tscr2B], wr=[wgB])
        for tb in range(tbn):
            stoks = []
            if src[0] == "dram_sw":
                sh, shB = src[1], src[2]
                def fnv(e, cc, tb=tb):
                    return e.dma_start(
                        slab[:, 0:ki_n * 512],
                        bass.AP(sh.tensor, sh.offset + (cc % 2) * HALF + tb * 512,
                                [[src_T, 128], [src_T * 128, ki_n], [1, 512]]))
                pr.emit_sw("q", fnv, rdall=[shB], wr=[slabB], dma=True)
                mov, movB = slab, [slabB]
            elif src[0] == "dram":
                sh, shB = src[1], src[2]
                nkc = 4 if ki_n % 4 == 0 else (2 if ki_n % 2 == 0 else 1)
                kk2 = ki_n // nkc
                for c in range(nkc):
                    def fn(e, tb=tb, c=c, kk2=kk2):
                        return e.dma_start(
                            slab[:, c * kk2 * 512:(c + 1) * kk2 * 512],
                            bass.AP(sh.tensor, sh.offset + tb * 512
                                    + c * kk2 * src_T * 128,
                                    [[src_T, 128], [src_T * 128, kk2],
                                     [1, 512]]))
                    tok = pr.emit("q", fn, rdall=[shB],
                                  wr=[slabB] if c == 0 else (), dma=True)
                    if c > 0:
                        slabB.r.append(tok)
                    stoks.append(tok)
                mov, movB = slab, [slabB]
            else:
                mov = src[1]
                movB = [src[2][t] for t in range(tb * 4, tb * 4 + 4)]
            for obl in range(obg):
                ob = og0 + obl
                bank, bankB = st.bank()
                fns = []
                for ki in range(ki_n):
                    if src[0] in ("dram", "dram_sw"):
                        rhs_ap = lambda e, ki=ki: mov[:, ki * 512:(ki + 1) * 512]
                    else:
                        rhs_ap = lambda e, ki=ki, tb=tb: mov[:, ki * T + tb * 512:
                                                            ki * T + tb * 512 + 512]
                    def fn(e, ki=ki, obl=obl, obg=obg, bank=bank, rhs_ap=rhs_ap):
                        return e.matmul(
                            bank[:, 0:512],
                            wg[:, ki * obg * 128 + obl * 128:
                               ki * obg * 128 + (obl + 1) * 128],
                            rhs_ap(e), start=(ki == 0), stop=(ki == ki_n - 1))
                    fns.append(fn)
                pr.group("t", fns, rd=[wgB] + movB, wr=[bankB],
                         extra=list(wtoks) + stoks)
                st.ffn_evac(dst, act, scale_sb, bias_sb, ob, tb, bank, bankB,
                            src2_mul)
    if src[0] == "dram_sw":
        pr.sw_end()



def emit_l1f(st, aps, acc, accB):
    pr = st.pr
    T = st.T
    NT = st.NT
    gsb = pr.sb("l1f_gw", [128, 8 * 16], BF16); gsbB = Buf()
    gbr = pr.sb("l1f_gb", [1, 16], BF16); gbrB = Buf()
    onesr = pr.sb("l1f_ones", [1, 128], BF16); onesrB = Buf()
    rtm = pr.sb("l1f_r", [128, NT * 16], F32); rtmB = Buf()
    wts = pr.sb("l1f_w", [128, NT * 16], F32); wtsB = Buf()
    tv = pr.sb("l1f_tv", [128, NT * 16], F32); tvB = Buf()
    v12 = pr.sb("l1f_v12", [128, 4], F32); v12B = Buf()
    weg = pr.sb("l1f_weg", [128, 8 * 512], BF16); wegB = Buf()
    weu = pr.sb("l1f_weu", [128, 8 * 512], BF16); weuB = Buf()
    wed = pr.sb("l1f_wed", [128, 4 * 1024], BF16); wedB = Buf()
    hg = pr.sb("l1f_hg", [128, 4 * T], BF16)
    hgB = [Buf(f"hg{i}") for i in range(4)]
    af = pr.sb("l1f_act", [128, 4 * T], BF16)
    afB = [Buf(f"af{i}") for i in range(4)]

    pr.emit("g", lambda e: e.dma_start(
        gsb[:, :], bass.AP(aps["L1_ffn_gate_w"].tensor, 0,
                           [[16, 128], [2048, 8], [1, 16]])), wr=[gsbB], dma=True)
    pr.emit("g", lambda e: e.dma_start(gbr[:, :], aps["L1_ffn_gate_b"][None, :]),
            wr=[gbrB], dma=True)
    pr.emit("v", lambda e: e.memset(onesr[:, :], 1.0), wr=[onesrB])

    for i in range(NT):
        bank, bankB = st.bank()
        fns = []
        for ki in range(8):
            def fn(e, i=i, ki=ki, bank=bank, xf=st.xln_fm):
                return e.matmul(bank[:, 0:16],
                                xf[:, ki * T + i * 128:ki * T + (i + 1) * 128],
                                gsb[:, ki * 16:(ki + 1) * 16],
                                start=(ki == 0), stop=False)
            fns.append(fn)
        def fnb(e, bank=bank):
            return e.matmul(bank[:, 0:16], onesr[0:1, :], gbr[0:1, :],
                            start=False, stop=True)
        fns.append(fnb)
        pr.group("t", fns, rd=[st.xlnfB[i], gsbB, onesrB, gbrB], wr=[bankB])
        pr.emit("s", lambda e, i=i, bank=bank: e.activation(
            rtm[:, i * 16:(i + 1) * 16], bank[:, 0:16], AF.Sigmoid),
            rd=[bankB], wr=[rtmB])
        r_ap = rtm[:, i * 16:(i + 1) * 16]
        t_ap = tv[:, i * 16:(i + 1) * 16]
        w_ap = wts[:, i * 16:(i + 1) * 16]
        pr.emit("v", lambda e, r_ap=r_ap: e.reduce_max(v12[:, 0:1], r_ap, axis=AX.X),
                rd=[rtmB], wr=[v12B])
        pr.emit("v", lambda e, r_ap=r_ap, t_ap=t_ap: e.tensor_scalar(
            t_ap, r_ap, v12[:, 0:1], -1e30, ALU.is_ge, ALU.mult),
            rd=[rtmB, v12B], wr=[tvB])
        pr.emit("v", lambda e, r_ap=r_ap, t_ap=t_ap: e.tensor_tensor(
            t_ap, t_ap, r_ap, ALU.add), rd=[rtmB, tvB], wr=[tvB])
        pr.emit("v", lambda e, t_ap=t_ap: e.reduce_max(v12[:, 1:2], t_ap, axis=AX.X),
                rd=[tvB], wr=[v12B])
        pr.emit("v", lambda e: e.tensor_tensor(v12[:, 2:3], v12[:, 0:1], v12[:, 1:2],
                                               ALU.add), rd=[v12B], wr=[v12B])
        pr.emit("v", lambda e: e.reciprocal(v12[:, 3:4], v12[:, 2:3]),
                rd=[v12B], wr=[v12B])
        pr.emit("v", lambda e, r_ap=r_ap, w_ap=w_ap: e.tensor_scalar(
            w_ap, r_ap, v12[:, 1:2], None, ALU.is_ge), rd=[rtmB, v12B], wr=[wtsB])
        pr.emit("v", lambda e, r_ap=r_ap, w_ap=w_ap: e.tensor_tensor(
            w_ap, w_ap, r_ap, ALU.mult), rd=[rtmB, wtsB], wr=[wtsB])
        pr.emit("v", lambda e, w_ap=w_ap: e.tensor_scalar(
            w_ap, w_ap, v12[:, 3:4], None, ALU.mult), rd=[v12B, wtsB], wr=[wtsB])

    wgT = aps["L1_ffn_experts_wg"]
    wuT = aps["L1_ffn_experts_wu"]
    wdT = aps["L1_ffn_experts_wd"]
    for ei in range(-1, 16):
        if ei < 0:
            g_ap = bass.AP(aps["L1_ffn_shared_wg"].tensor, 0, [[512, 128], [65536, 8], [1, 512]])
            u_ap = bass.AP(aps["L1_ffn_shared_wu"].tensor, 0, [[512, 128], [65536, 8], [1, 512]])
            d_ap = bass.AP(aps["L1_ffn_shared_wd"].tensor, 0, [[1024, 128], [131072, 4], [1, 1024]])
        else:
            g_ap = bass.AP(wgT.tensor, ei * 1024 * 512, [[512, 128], [65536, 8], [1, 512]])
            u_ap = bass.AP(wuT.tensor, ei * 1024 * 512, [[512, 128], [65536, 8], [1, 512]])
            d_ap = bass.AP(wdT.tensor, ei * 512 * 1024, [[1024, 128], [131072, 4], [1, 1024]])
        for ap_, dst_, dstB_ in ((g_ap, weg, wegB), (u_ap, weu, weuB),
                                 (d_ap, wed, wedB)):
            weng, wdep = wsrc(st, ap_)
            pr.emit(weng, lambda e, ap_=ap_, dst_=dst_: e.dma_start(dst_[:, :], ap_),
                    wr=[dstB_], rdall=wdep, dma=True)
        for ob in range(4):
            for tb in range(T // 512):
                bank, bankB = st.bank()
                fns = []
                for ki in range(8):
                    def fn(e, ob=ob, tb=tb, ki=ki, bank=bank, xf=st.xln_fm):
                        return e.matmul(bank[:, 0:512],
                                        weg[:, ki * 512 + ob * 128:ki * 512 + (ob + 1) * 128],
                                        xf[:, ki * T + tb * 512:ki * T + (tb + 1) * 512],
                                        start=(ki == 0), stop=(ki == 7))
                    fns.append(fn)
                pr.group("t", fns, rd=[wegB] + [st.xlnfB[t] for t in range(tb * 4, tb * 4 + 4)],
                         wr=[bankB])
                pr.emit("s", lambda e, ob=ob, tb=tb, bank=bank: e.activation(
                    hg[:, ob * T + tb * 512:ob * T + (tb + 1) * 512],
                    bank[:, 0:512], AF.Silu), rd=[bankB], wr=[hgB[ob]])
        for ob in range(4):
            for tb in range(T // 512):
                bank, bankB = st.bank()
                fns = []
                for ki in range(8):
                    def fn(e, ob=ob, tb=tb, ki=ki, bank=bank, xf=st.xln_fm):
                        return e.matmul(bank[:, 0:512],
                                        weu[:, ki * 512 + ob * 128:ki * 512 + (ob + 1) * 128],
                                        xf[:, ki * T + tb * 512:ki * T + (tb + 1) * 512],
                                        start=(ki == 0), stop=(ki == 7))
                    fns.append(fn)
                pr.group("t", fns, rd=[weuB] + [st.xlnfB[t] for t in range(tb * 4, tb * 4 + 4)],
                         wr=[bankB])
                def fn(e, ob=ob, tb=tb, bank=bank):
                    return e.tensor_tensor(
                        af[:, ob * T + tb * 512:ob * T + (tb + 1) * 512],
                        bank[:, 0:512],
                        hg[:, ob * T + tb * 512:ob * T + (tb + 1) * 512], ALU.mult)
                pr.emit("v", fn, rd=[bankB, hgB[ob]], wr=[afB[ob]])
        for i in range(NT):
            for ob2 in range(2):
                bank, bankB = st.bank()
                fns = []
                for ki2 in range(4):
                    def fn(e, i=i, ob2=ob2, ki2=ki2, bank=bank):
                        return e.matmul(bank[:, 0:512],
                                        af[:, ki2 * T + i * 128:ki2 * T + (i + 1) * 128],
                                        wed[:, ki2 * 1024 + ob2 * 512:ki2 * 1024 + (ob2 + 1) * 512],
                                        start=(ki2 == 0), stop=(ki2 == 3))
                    fns.append(fn)
                pr.group("t", fns, rd=[wedB] + [afB[k] for k in range(4)], wr=[bankB])
                a_ap = acc[:, i * D + ob2 * 512:i * D + (ob2 + 1) * 512]
                if ei < 0:
                    def fn(e, a_ap=a_ap, bank=bank):
                        return e.scalar_tensor_tensor(
                            a_ap, bank[:, 0:512], 1.0, a_ap, ALU.mult, ALU.add)
                else:
                    w_col = wts[:, i * 16 + ei:i * 16 + ei + 1]
                    def fn(e, a_ap=a_ap, bank=bank, w_col=w_col):
                        return e.scalar_tensor_tensor(
                            a_ap, bank[:, 0:512], w_col, a_ap, ALU.mult, ALU.add)
                pr.emit("v", fn, rd=[bankB, wtsB, accB[i]], wr=[accB[i]])
    return


L1F_WSPECS = {
    "L1_ffn_gate_w": (1024, 16), "L1_ffn_gate_b": (16,),
    "L1_ffn_shared_wg": (1024, 512), "L1_ffn_shared_wu": (1024, 512),
    "L1_ffn_shared_wd": (512, 1024),
    "L1_ffn_experts_wg": (16, 1024, 512), "L1_ffn_experts_wu": (16, 1024, 512),
    "L1_ffn_experts_wd": (16, 512, 1024),
}


def host_l1f_weights(I):
    return {k: I[k] for k in L1F_WSPECS}


def build_stage_l1f(NT=16):
    nc = bass.Bass("TRN2", target_bir_lowering=False)
    T = NT * 128
    xin = nc.dram_tensor("xin", [T, D], F32, kind="ExternalInput")
    ident = nc.dram_tensor("ident", [128, 128], F32, kind="ExternalInput")
    xout = nc.dram_tensor("xout", [T, D], F32, kind="ExternalOutput")
    aps = {"xin": xin.ap(), "ident": ident.ap(), "xout": xout.ap()}
    for name, shape in L1F_WSPECS.items():
        aps[name] = nc.dram_tensor(name, list(shape), F32, kind="ExternalInput").ap()
    pr = PR(nc)
    st = Stage(pr, NT, xin=aps["xin"], xout=aps["xout"], arenas=())
    acc = pr.sb("l1f_acc", [128, NT * D], F32)
    accB = [Buf(f"acc{i}") for i in range(NT)]
    st.load_consts(aps["ident"])
    st.ln_in(keep_tm=(acc, accB))
    emit_l1f(st, aps, acc, accB)
    st.ln_out_add_tm(acc, accB)
    toks = [(sm, v) for sm, v in pr.counts.items() if v > 0]
    pr.emit("g", lambda e: e.engine_nop(), extra=toks)
    pr.build_block()
    pr.close()
    return nc



def attn_core(st, qfm, qfmB, k_stat, v_stat, attn_d, attn_dB, qb, h, ktiles,
              scale, exp_bias=None, bias_fn=None, NH=16, pv=None, accum=False):
    pr = st.pr
    T = st.T
    if pv is None:
        pvb, pvbB = st.pb[4 + (qb * NH + h) % 2], st.pbB[4 + (qb * NH + h) % 2]
    else:
        pvb, pvbB = pv
    first = True
    for j, kt in enumerate(ktiles):
        bank, bankB = st.bank4()
        ks, ksB = k_stat(kt, h)
        qs = qfm[64 * (h % 2):64 * (h % 2) + 64,
                 (h // 2) * T + qb * 512:(h // 2) * T + (qb + 1) * 512]
        pr.group("t", [lambda e, ks=ks, qs=qs, bank=bank: e.matmul(
            bank[:, 0:512], ks, qs, start=True, stop=True)],
            rd=ksB + [qfmB[t] for t in range(qb * 4, qb * 4 + 4)], wr=[bankB])
        bias = bias_fn(kt) if bias_fn else None
        r = j % 2
        pro = st.aprobs[:, r * 512:(r + 1) * 512]
        proB = st.aprobsB[r]
        if bias is not None and bias[0] == "tile":
            sco = st.asco[:, 0:512]
            scoB = st.ascoB[0]
            pr.emit("v", lambda e, bank=bank, sco=sco, b=bias: e.scalar_tensor_tensor(
                sco, bank[:, 0:512], scale, b[1], ALU.mult, ALU.add),
                rd=[bankB] + bias[2], wr=[scoB])
            pr.emit("s", lambda e, sco=sco, pro=pro: e.activation(
                pro, sco, AF.Exp), rd=[scoB], wr=[proB])
        elif bias is not None and bias[0] == "const":
            pr.emit("s", lambda e, bank=bank, pro=pro, b=bias: e.activation(
                pro, bank[:, 0:512], AF.Exp, bias=b[1], scale=scale),
                rd=[bankB] + bias[2], wr=[proB])
        else:
            pr.emit("s", lambda e, bank=bank, pro=pro: e.activation(
                pro, bank[:, 0:512], AF.Exp, scale=scale), rd=[bankB], wr=[proB])
        vs, vsB = v_stat(kt, h)
        pr.group("t", [lambda e, vs=vs, pro=pro, pvb=pvb, first=first,
                       last=(j == len(ktiles) - 1): e.matmul(
            pvb[0:65, 0:512], vs, pro, start=first, stop=last)],
            rd=vsB + [proB], wr=[pvbB])
        first = False
    pr.emit("s", lambda e, pvb=pvb: e.copy(st.aznrm[64:65, 0:512], pvb[64:65, 0:512]),
            rd=[pvbB], wr=[st.azrowB])
    pr.emit("v", lambda e: e.reciprocal(st.aznrm[64:65, 512:1024], st.aznrm[64:65, 0:512]),
            rd=[st.azrowB], wr=[st.azrecB])
    bank, bankB = st.bank4()
    pr.group("t", [lambda e, bank=bank: e.matmul(
        bank[0:64, 0:512], st.aznrm[64:65, 1024:1088], st.aznrm[64:65, 512:1024],
        start=True, stop=True)], rd=[st.aones64B, st.azrecB], wr=[bankB])
    pr.emit("s", lambda e, bank=bank: e.copy(st.aznrm[0:64, 0:512], bank[0:64, 0:512]),
            rd=[bankB], wr=[st.azbB])
    r2 = (qb * NH + h) % 2
    stg = st.astg[:, r2 * 512:(r2 + 1) * 512]
    pr.emit("v", lambda e, pvb=pvb, stg=stg: e.tensor_tensor(
        stg[0:64, :], pvb[0:64, 0:512], st.aznrm[0:64, 0:512], ALU.mult),
        rd=[pvbB, st.azbB], wr=[st.astgB[r2]])
    ch = h // 2
    off = (ch * 128 + 64 * (h % 2)) * T + qb * 512
    tok = pr.emit("q", lambda e, stg=stg, off=off: e.dma_start(
        bass.AP(attn_d.tensor, off, [[T, 64], [1, 512]]), stg[0:64, :]),
        rd=[st.astgB[r2]], dma=True)
    attn_dB.r.append(tok)


def attn_arenas(st, need_sco=True):
    pr = st.pr
    if not hasattr(st, "aprobs"):
        st.aprobs = pr.sb("a_probs", [128, 4 * 512], BF16)
        st.aprobsB = [Buf(f"pr{i}") for i in range(4)]
        st.aznrm = pr.sb("a_nrm", [128, 1088], F32)
        st.azrowB = Buf(); st.azrecB = Buf(); st.azbB = Buf()
        st.aones64B = Buf()
        st.astg = pr.sb("a_stg", [128, 2 * 512], BF16)
        st.astgB = [Buf("astg0"), Buf("astg1")]
    if need_sco and not hasattr(st, "asco"):
        st.asco = pr.sb("a_sco", [128, 2 * 512], F32)
        st.ascoB = [Buf(f"sc{i}") for i in range(2)]
    pr.emit("v", lambda e: e.memset(st.aznrm[64:65, 1024:1088], 1.0),
            wr=[st.aones64B])



def emit_l1a(st, aps):
    kv_n = st.NT + 2
    pr = st.pr
    T = st.T
    NT = st.NT
    nc = pr.nc
    attn_arenas(st)
    wk = pr.sb("l1a_wk", [128, 8 * 128], BF16); wkB = Buf()
    wkS = pr.sb("l1a_wkS", [128, 8 * 128], BF16); wkSB = Buf()
    wv = pr.sb("l1a_wv", [128, 8 * 128], BF16); wvB = Buf()
    bq = pr.sb("l1a_bq", [128, 8], F32); bqB = Buf()
    bk = pr.sb("l1a_bk", [128, 1], F32); bkB = Buf()
    bkS = pr.sb("l1a_bkS", [128, 1], F32); bkSB = Buf()
    bv = pr.sb("l1a_bv", [128, 1], F32); bvB = Buf()
    bo = pr.sb("l1a_bo", [128, 8], F32); boB = Buf()
    for t, b, ap in ((wk, wkB, "L1_attn_Wk"), (wkS, wkSB, "L1_attn_WkS"),
                     (wv, wvB, "L1_attn_Wv")):
        weng, wdep = wsrc(st, aps[ap])
        pr.emit(weng, lambda e, t=t, ap=ap: e.dma_start(
            t[:, :], bass.AP(aps[ap].tensor, 0, [[128, 128], [16384, 8], [1, 128]])),
            wr=[b], rdall=wdep, dma=True)
    col_vec_load(pr, aps["L1_attn_bq"], bq, bqB, 8)
    col_vec_load(pr, aps["L1_attn_bk"], bk, bkB, 1)
    col_vec_load(pr, aps["L1_attn_bkS"], bkS, bkSB, 1)
    col_vec_load(pr, aps["L1_attn_bv"], bv, bvB, 1)
    col_vec_load(pr, aps["L1_attn_bo"], bo, boB, 8)
    masks = pr.sb("l1a_masks", [128, 6 * 512], BF16); masksB = Buf()
    pr.emit("q" if aps["l1_masks"].dtype == BF16 else "g",
            lambda e: e.dma_start(masks[:, :], aps["l1_masks"]), wr=[masksB], dma=True)

    qfm = pr.sb("l1a_qfm", [128, 8 * T], BF16)
    qfmB = [Buf(f"q{i}") for i in range(NT)]
    ffn_layer(st, aps["L1_attn_Wq"], 8, 8, ("sbuf", st.xln_fm, st.xlnfB),
              ("sbuf", qfm, qfmB), None, bias_sb=(bq, bqB), OBG=8)

    kfmA = pr.sb("l1a_kfmA", [128, kv_n * 128], BF16); kfmAB = Buf()
    kfmB_ = pr.sb("l1a_kfmB", [128, kv_n * 128], BF16); kfmBB = Buf()
    vtm = pr.sb("l1a_vtm", [128, kv_n * 130], BF16); vtmB = Buf()
    kvxf = pr.sb("l1a_kvxf", [128, 8 * 512], BF16)
    kvxfB = Buf("l1a_kvxf")
    xfull = aps.get("xfull")
    xfull_sw = aps.get("xfull_sw")
    kvz = pr.sb("l1a_kvz", [128, kv_n], F32); kvzB = Buf()
    pr.emit("q", lambda e: e.dma_start(kvz[:, :], aps["l1_kvzero"]), wr=[kvzB], dma=True)
    if xfull_sw is not None:
        pr.sw_begin("q")
    for g0 in range(0, kv_n, 4):
        gn = min(4, kv_n - g0)
        for gi in range(gn):
            kt = g0 + gi
            tok = kt * 128
            r = kt % 2
            xt = st.xt[:, r * D:(r + 1) * D]
            if xfull_sw is not None:
                def fnv(e, cc, tok=tok, xt=xt):
                    return e.dma_start(
                        xt, xfull_sw[(cc % 2) * HALF + tok:
                                     (cc % 2) * HALF + tok + 128, :])
                pr.emit_sw("q", fnv, wr=[st.xtB[r]], dma=True)
            else:
                pr.emit("q", lambda e, tok=tok, xt=xt: e.dma_start(
                    xt, xfull[tok:tok + 128, :]), wr=[st.xtB[r]], dma=True)
            st._ln_stats(xt, [st.xtB[r]])
            yt = st.ytm[:, r * D:(r + 1) * D]
            pr.emit("v", lambda e, xt=xt, yt=yt: e.tensor_scalar(
                yt, xt, st.stat[:, 2:3], st.stat[:, 6:7], ALU.subtract, ALU.mult),
                rd=[st.xtB[r], st.statB], wr=[st.ytmB[r]])
            st.tm_to_fm(st.ytm, st.ytmB, r, kvxf, [kvxfB] * 4, gi, fm_stride=512)
        for gi in range(gn):
            kt = g0 + gi
            for (wt, wtB, bb, bbB, dst, dstB) in (
                    (wk, wkB, bk, bkB, kfmA, kfmAB),
                    (wkS, wkSB, bkS, bkSB, kfmB_, kfmBB)):
                bank, bankB = st.bank4()
                fns = []
                for ki in range(8):
                    def fn(e, gi=gi, ki=ki, bank=bank, wt=wt):
                        return e.matmul(bank[:, 0:128], wt[:, ki * 128:(ki + 1) * 128],
                                        kvxf[:, ki * 512 + gi * 128:ki * 512 + (gi + 1) * 128],
                                        start=(ki == 0), stop=(ki == 7))
                    fns.append(fn)
                pr.group("t", fns, rd=[wtB, kvxfB], wr=[bankB])
                pr.emit("s", lambda e, kt=kt, bank=bank, bb=bb, dst=dst: e.activation(
                    dst[:, kt * 128:(kt + 1) * 128], bank[:, 0:128],
                    AF.Identity, bias=bb[:, 0:1], scale=1.0),
                    rd=[bankB, bbB], wr=[dstB])
            bank, bankB = st.bank4()
            fns = []
            for ki in range(8):
                def fn(e, gi=gi, ki=ki, bank=bank):
                    return e.matmul(bank[:, 0:128], wv[:, ki * 128:(ki + 1) * 128],
                                    kvxf[:, ki * 512 + gi * 128:ki * 512 + (gi + 1) * 128],
                                    start=(ki == 0), stop=(ki == 7))
                fns.append(fn)
            pr.group("t", fns, rd=[wvB, kvxfB], wr=[bankB])
            r = gi % 2
            yt2 = st.ytm[:, r * D:r * D + 128]
            pr.emit("s", lambda e, bank=bank, yt2=yt2: e.activation(
                yt2, bank[:, 0:128], AF.Identity, bias=bv[:, 0:1], scale=1.0),
                rd=[bankB, bvB], wr=[st.ytmB[r]])
            pr.group("t", [lambda e, yt2=yt2: e.transpose(
                st.ptb[:, 0:128], yt2, st.ident[:, :])],
                rd=[st.ytmB[r], st.identB], wr=[st.ptbB])
            for g in range(2):
                pr.emit("s", lambda e, kt=kt, g=g: e.copy(
                    vtm[:, kt * 130 + 65 * g:kt * 130 + 65 * g + 64],
                    st.ptb[:, 64 * g:64 * g + 64]), rd=[st.ptbB], wr=[vtmB])
                pr.emit("v", lambda e, kt=kt, g=g: e.memset(
                    vtm[:, kt * 130 + 65 * g + 64:kt * 130 + 65 * g + 65], 1.0),
                    wr=[vtmB])
            for tens, tB in ((kfmA, kfmAB), (kfmB_, kfmBB)):
                pr.emit("v", lambda e, kt=kt, tens=tens: e.tensor_scalar(
                    tens[:, kt * 128:(kt + 1) * 128], tens[:, kt * 128:(kt + 1) * 128],
                    kvz[:, kt:kt + 1], None, ALU.mult), rd=[kvzB, tB], wr=[tB])
            for g in range(2):
                pr.emit("v", lambda e, kt=kt, g=g: e.tensor_scalar(
                    vtm[:, kt * 130 + 65 * g:kt * 130 + 65 * g + 64],
                    vtm[:, kt * 130 + 65 * g:kt * 130 + 65 * g + 64],
                    kvz[:, kt:kt + 1], None, ALU.mult), rd=[kvzB, vtmB], wr=[vtmB])

    if xfull_sw is not None:
        pr.sw_end()

    attn_d = nc.dram_tensor("l1a_attn", [8 * 128, T], BF16, kind="Internal").ap()
    attn_dB = Buf("l1a_attn")

    def k_stat(kt, h):
        g = h // 8
        base = 64 * (h % 2)
        use_A = (g == 0) == (base == 0)
        t = kfmA if use_A else kfmB_
        tB = kfmAB if use_A else kfmBB
        return t[base:base + 64, kt * 128:(kt + 1) * 128], [tB]

    def v_stat(kt, h):
        g = h // 8
        return vtm[:, kt * 130 + 65 * g:kt * 130 + 65 * (g + 1)], [vtmB]

    for qb in range(T // 512):
        kt_first = qb * 4
        kts = [kt_first + j for j in range(6)]

        def bias_fn(kt, kt_first=kt_first):
            j = kt - kt_first
            return ("tile", masks[:, j * 512:(j + 1) * 512], [masksB])
        for h in range(16):
            attn_core(st, qfm, qfmB, k_stat, v_stat, attn_d, attn_dB, qb, h, kts,
                      1.0, bias_fn=bias_fn)

    ffn_layer(st, aps["L1_attn_Wo"], 8, 8, ("dram", attn_d, attn_dB),
              ("sbuf", st.yfm, st.yfB), None, bias_sb=(bo, boB), OBG=8)
    st.ln_out_add_fm(st.yfm, st.yfB)


L1A_WSPECS = {
    "L1_attn_Wq": (1024, 1024), "L1_attn_bq": (1024,),
    "L1_attn_Wk": (1024, 128), "L1_attn_bk": (128,),
    "L1_attn_WkS": (1024, 128), "L1_attn_bkS": (128,),
    "L1_attn_Wv": (1024, 128), "L1_attn_bv": (128,),
    "L1_attn_Wo": (1024, 1024), "L1_attn_bo": (1024,),
    "l1_masks": (128, 6 * 512),
}


def host_l1a_percore(x_row, q_tok0, NT=16):
    T = NT * 128
    win = np.zeros((T + 256, D), np.float32)
    lo = q_tok0 - 128
    s0, s1 = max(0, lo), min(x_row.shape[0], lo + T + 256)
    win[s0 - lo:s1 - lo] = x_row[s0:s1]
    kvzero = np.ones((128, NT + 2), np.float32)
    for kt in range(NT + 2):
        tok = lo + kt * 128
        if tok < 0 or tok >= x_row.shape[0]:
            kvzero[:, kt] = 0.0
    return {"xfull": win, "l1_kvzero": kvzero}


def host_l1a_weights(I):
    sw = np.concatenate([I["L1_attn_Wk"][:, 64:128], I["L1_attn_Wk"][:, 0:64]], axis=1)
    bs = np.concatenate([I["L1_attn_bk"][64:128], I["L1_attn_bk"][0:64]])
    masks = np.zeros((6, 128, 512), np.float32)
    for j in range(6):
        delta = (j - 1) * 128
        kl = np.arange(128)[:, None]
        ql = np.arange(512)[None, :]
        rel = delta + kl - ql
        masks[j] = np.where((rel >= -128) & (rel <= 127), 0.0, -1e9)
    masks_t = masks.transpose(1, 0, 2).reshape(128, 6 * 512)
    return {
        "L1_attn_Wq": I["L1_attn_Wq"], "L1_attn_bq": I["L1_attn_bq"],
        "L1_attn_Wk": I["L1_attn_Wk"], "L1_attn_bk": I["L1_attn_bk"],
        "L1_attn_WkS": np.ascontiguousarray(sw), "L1_attn_bkS": bs,
        "L1_attn_Wv": I["L1_attn_Wv"], "L1_attn_bv": I["L1_attn_bv"],
        "L1_attn_Wo": I["L1_attn_Wo"], "L1_attn_bo": I["L1_attn_bo"],
        "l1_masks": np.ascontiguousarray(masks_t).astype(ml_dtypes.bfloat16),
    }


def build_stage_l1a(NT=16):
    nc = bass.Bass("TRN2", target_bir_lowering=False)
    T = NT * 128
    xin = nc.dram_tensor("xin", [T, D], F32, kind="ExternalInput")
    xfull = nc.dram_tensor("xfull", [T + 256, D], F32, kind="ExternalInput")
    ident = nc.dram_tensor("ident", [128, 128], F32, kind="ExternalInput")
    xout = nc.dram_tensor("xout", [T, D], F32, kind="ExternalOutput")
    aps = {"xin": xin.ap(), "xfull": xfull.ap(), "ident": ident.ap(),
           "xout": xout.ap()}
    for name, shape in L1A_WSPECS.items():
        aps[name] = nc.dram_tensor(name, list(shape), F32, kind="ExternalInput").ap()
    aps["l1_kvzero"] = nc.dram_tensor("l1_kvzero", [128, NT + 2], F32,
                                      kind="ExternalInput").ap()
    pr = PR(nc)
    st = Stage(pr, NT, xin=aps["xin"], xout=aps["xout"], arenas=("ffn",),
               wg_cols=8192, slab_cols=4096)
    st.load_consts(aps["ident"])
    st.ln_in()
    emit_l1a(st, aps)
    toks = [(sm, v) for sm, v in pr.counts.items() if v > 0]
    pr.emit("g", lambda e: e.engine_nop(), extra=toks)
    pr.build_block()
    pr.close()
    return nc



def emit_full_attn(st, aps, layer):
    pr = st.pr
    nc = pr.nc
    T = st.T
    NT = st.NT
    KT = S // 128
    rope = layer == "l0"
    attn_arenas(st, need_sco=(layer == "l3"))
    pfx = layer + "a_"
    W = {}
    for nm in ("wk", "wv"):
        W[nm] = pr.sb(pfx + nm, [128, 8 * 1024], BF16)
        W[nm + "B"] = Buf()
        src_name = {"l0": {"wk": "L0_attn_w_kTP", "wv": "L0_attn_w_vT"},
                    "l3": {"wk": "L3_attn_WkT", "wv": "L3_attn_WvT"}}[layer][nm]
        weng, wdep = wsrc(st, aps[src_name])
        W[nm + "toks"] = []
        for c in range(4):
            def fn(e, t=W[nm], ap=aps[src_name], c=c):
                return e.dma_start(
                    t[:, c * 2048:(c + 1) * 2048],
                    bass.AP(ap.tensor, c * 2 * 131072,
                            [[1024, 128], [131072, 2], [1, 1024]]))
            tok = pr.emit(weng, fn, wr=[W[nm + "B"]] if c == 0 else (),
                          rdall=wdep, dma=True)
            if c > 0:
                W[nm + "B"].r.append(tok)
            W[nm + "toks"].append(tok)
    if rope:
        W["wq"] = st.wg
        W["wqB"] = st.wgB
        weng, wdep = wsrc(st, aps["L0_attn_w_qTP"])
        W["wqtoks"] = []
        for c in range(4):
            def fn(e, ap=aps["L0_attn_w_qTP"], wg=st.wg, c=c):
                return e.dma_start(
                    wg[:, c * 2048:(c + 1) * 2048],
                    bass.AP(ap.tensor, c * 2 * 131072,
                            [[1024, 128], [131072, 2], [1, 1024]]))
            tok = pr.emit(weng, fn, wr=[st.wgB] if c == 0 else (),
                          rdall=wdep, dma=True)
            if c > 0:
                st.wgB.r.append(tok)
            W["wqtoks"].append(tok)
    woT = aps["L0_attn_w_oT"] if layer == "l0" else aps["L3_attn_WoT"]

    Kd = nc.dram_tensor(pfx + "Kd", [KT * 128, 1024], BF16, kind="Internal").ap()
    KdB = [Buf() for _ in range(KT)]
    Vd = nc.dram_tensor(pfx + "Vd", [KT * 128, 1040], BF16, kind="Internal").ap()
    VdB = [Buf() for _ in range(KT)]
    kxl = st.xln_fm
    vxl = st.yfm
    kxlB = [Buf() for _ in range(KT)]
    vxlB = [Buf() for _ in range(KT)]

    def load_kv_slab(kt, hg):
        pr.emit("q", lambda e, kt=kt, hg=hg: e.dma_start(
            kxl[:, kt * 256:(kt + 1) * 256],
            bass.AP(Kd.tensor, kt * 128 * 1024 + hg * 256,
                    [[1024, 128], [1, 256]])),
            rdall=[KdB[kt]] + list(st.xlnfB), wr=[kxlB[kt]], dma=True)
        pr.emit("q", lambda e, kt=kt, hg=hg: e.dma_start(
            vxl[:, kt * 260:(kt + 1) * 260],
            bass.AP(Vd.tensor, kt * 128 * 1040 + hg * 260,
                    [[1040, 128], [1, 260]])),
            rdall=[VdB[kt]], wr=[vxlB[kt]], dma=True)
    attn_d = nc.dram_tensor(pfx + "attn", [8 * 128, T], BF16, kind="Internal").ap()
    attn_dB = Buf()

    if rope:
        rT1 = pr.sb(pfx + "rT1", [128, 2 * 1024], BF16)
        rT1B = [Buf(), Buf()]
        rT2 = pr.sb(pfx + "rT2", [128, 2 * 1024], BF16)
        rT2B = [Buf(), Buf()]
    if layer == "l3":
        diag = pr.sb(pfx + "diag", [16, 512], BF16)
        diagB = Buf()
        diag_d = nc.dram_tensor(pfx + "diagd", [16, 8191], BF16, kind="Internal").ap()
        diag_dB = Buf()
        oh = pr.sb(pfx + "oh", [32, 512], BF16)
        ohB = [Buf(), Buf()]
        rel = pr.sb(pfx + "rel", [32, 16], BF16)
        relB = Buf()
        st.l3_strip = pr.sb(pfx + "strip", [128, 4 * 2432], BF16)
        st.l3_stripB = Buf()
        st.l3_fcol = pr.sb(pfx + "fcol", [128, 12], F32)
        st.l3_fcolB = Buf()
        st.l3_f2 = pr.sb(pfx + "f2", [1, 12], BF16)
        st.l3_f2B = Buf()
        onesr128 = pr.sb(pfx + "o128", [1, 128], BF16)
        onesr128B = Buf()
        pr.emit("v", lambda e: e.memset(onesr128[:, :], 1.0), wr=[onesr128B])
        pr.emit("g", lambda e: e.dma_start(rel[:, :], aps["L3_attn_rel_emb"]), wr=[relB], dma=True)
        for ci, c0 in enumerate(range(0, 8191, 512)):
            cn = min(512, 8191 - c0)
            pr.emit("q" if aps["l3_ohT"].dtype == BF16 else "g",
                    lambda e, c0=c0, cn=cn: e.dma_start(
                oh[:, 0:cn],
                bass.AP(aps["l3_ohT"].tensor, c0, [[8191, 32], [1, cn]])),
                wr=[ohB[0]], dma=True)
            bank, bankB = st.bank4()
            pr.group("t", [lambda e, cn=cn, bank=bank: e.matmul(
                bank[0:16, 0:cn], rel[:, :], oh[:, 0:cn],
                start=True, stop=True)], rd=[relB, ohB[0]], wr=[bankB])
            pr.emit("s", lambda e, cn=cn, bank=bank: e.copy(
                diag[:, 0:cn], bank[0:16, 0:cn]), rd=[bankB], wr=[diagB])
            tok = pr.emit("q", lambda e, c0=c0, cn=cn: e.dma_start(
                bass.AP(diag_d.tensor, c0, [[8191, 16], [1, cn]]), diag[:, 0:cn]),
                rd=[diagB], dma=True)
            diag_dB.r.append(tok)

        def load_hg_bias(heads):
            for hi, h in enumerate(heads):
                for (woff, C, Wn) in ((0, 3456, 1152), (1152, 5504, 1280)):
                    def fn(e, hi=hi, h=h, woff=woff, C=C, Wn=Wn):
                        return e.dma_start(
                            st.l3_strip[:, hi * 2432 + woff:hi * 2432 + woff + Wn],
                            bass.AP(diag_d.tensor, h * 8191 + C, [[1, 128], [1, Wn]]))
                    tok = pr.emit("q", fn, rdall=[diag_dB], wr=[st.l3_stripB], dma=True)
            for hi, h in enumerate(heads):
                def fn(e, hi=hi, h=h):
                    return e.dma_start(
                        st.l3_f2[0:1, hi * 3:hi * 3 + 3],
                        bass.AP(diag_d.tensor, h * 8191, [[1, 1], [5375, 2], [1, 2]])
                    )
                pass
            for hi, h in enumerate(heads):
                for ci, idx in enumerate((0, 5375, 8190)):
                    def fn(e, hi=hi, h=h, ci=ci, idx=idx):
                        return e.dma_start(
                            st.l3_f2[0:1, hi * 3 + ci:hi * 3 + ci + 1],
                            bass.AP(diag_d.tensor, h * 8191 + idx, [[1, 1], [1, 1]]))
                    pr.emit("g", fn, rdall=[diag_dB], wr=[st.l3_f2B], dma=True)
            bank, bankB = st.bank4()
            pr.group("t", [lambda e, bank=bank: e.matmul(
                bank[:, 0:12], onesr128[0:1, :], st.l3_f2[0:1, :],
                start=True, stop=True)], rd=[onesr128B, st.l3_f2B], wr=[bankB])
            pr.emit("s", lambda e, bank=bank: e.copy(st.l3_fcol[:, :], bank[:, 0:12]),
                    rd=[bankB], wr=[st.l3_fcolB])
        st.l3_load_hg_bias = load_hg_bias
    scale = 0.125

    qfm = pr.sb(pfx + "qfm", [128, 8 * T], BF16)
    qfmB = [Buf() for _ in range(NT)]
    if rope:
        qw = W["wq"]
        for i in range(NT):
            r = i % 2
            for half in range(2):
                bank, bankB = st.bank4()
                fns = []
                for ki in range(8):
                    def fn(e, i=i, ki=ki, half=half, bank=bank, xf=st.xln_fm):
                        return e.matmul(
                            bank[:, 0:512],
                            xf[:, ki * T + i * 128:ki * T + (i + 1) * 128],
                            qw[:, ki * 1024 + half * 512:ki * 1024 + (half + 1) * 512],
                            start=(ki == 0), stop=(ki == 7))
                    fns.append(fn)
                pr.group("t", fns, rd=[W["wqB"], st.xlnfB[i]], wr=[bankB],
                         extra=W["wqtoks"])
                pr.emit("s", lambda e, r=r, half=half, bank=bank: e.copy(
                    st.ytm[:, r * D + half * 512:r * D + (half + 1) * 512],
                    bank[:, 0:512]), rd=[bankB], wr=[st.ytmB[r]])
            reng = "q" if aps["ropeT1"].dtype == BF16 else "g"
            pr.emit(reng, lambda e, i=i, r=r: e.dma_start(
                rT1[:, r * 1024:(r + 1) * 1024], aps["ropeT1"][i * 128:(i + 1) * 128, :]),
                wr=[rT1B[r]], dma=True)
            pr.emit(reng, lambda e, i=i, r=r: e.dma_start(
                rT2[:, r * 1024:(r + 1) * 1024], aps["ropeT2"][i * 128:(i + 1) * 128, :]),
                wr=[rT2B[r]], dma=True)
            self_rope(st, st.ytm, st.ytmB, r, rT1, rT1B, rT2, rT2B)
            st.tm_to_fm(st.ytm, st.ytmB, r, qfm, qfmB, i)
    else:
        ffn_layer(st, aps["L3_attn_WqT"], 8, 8, ("sbuf", st.xln_fm, st.xlnfB),
                  ("sbuf", qfm, qfmB), None, OBG=4)

    kvxf = pr.sb(pfx + "kvxf", [128, 2 * 2048], BF16)
    kvxfB4 = {}
    kstg = st.slab
    kstgB = [[Buf(), Buf()], [Buf(), Buf()]]
    vstg = st.slab
    VOFF = 2048
    vstgB = Buf()
    for h in range(16):
        pr.emit("v", lambda e, h=h: e.memset(
            vstg[:, VOFF + h * 65 + 64:VOFF + h * 65 + 65], 1.0), wr=[vstgB])
    xfull = aps["xfull"]
    for g0 in range(0, KT, 2):
        s2 = (g0 // 2) % 2
        for gi in range(2):
            kt = g0 + gi
            r = kt % 2
            xt = st.xt[:, r * D:(r + 1) * D]
            pr.emit("q", lambda e, kt=kt, xt=xt: e.dma_start(
                xt, xfull[kt * 128:(kt + 1) * 128, :]), wr=[st.xtB[r]], dma=True)
            st._ln_stats(xt, [st.xtB[r]])
            yt = st.ytm[:, r * D:(r + 1) * D]
            pr.emit("v", lambda e, xt=xt, yt=yt: e.tensor_scalar(
                yt, xt, st.stat[:, 2:3], st.stat[:, 6:7], ALU.subtract, ALU.mult),
                rd=[st.xtB[r], st.statB], wr=[st.ytmB[r]])
            fi = s2 * 16 + gi
            kvxfB4.setdefault(fi, Buf())
            st.tm_to_fm(st.ytm, st.ytmB, r, kvxf, kvxfB4, fi, fm_stride=256)
        for gi in range(2):
            kt = g0 + gi
            r = gi % 2
            sl = kt % 2
            for half in range(2):
                bank, bankB = st.bank4()
                fns = []
                for ki in range(8):
                    def fn(e, s2=s2, gi=gi, ki=ki, half=half, bank=bank):
                        return e.matmul(
                            bank[:, 0:512],
                            kvxf[:, s2 * 2048 + ki * 256 + gi * 128:
                                 s2 * 2048 + ki * 256 + (gi + 1) * 128],
                            W["wk"][:, ki * 1024 + half * 512:ki * 1024 + (half + 1) * 512],
                            start=(ki == 0), stop=(ki == 7))
                    fns.append(fn)
                pr.group("t", fns, rd=[W["wkB"], kvxfB4[s2 * 16 + gi]],
                         wr=[bankB], extra=W["wktoks"])
                pr.emit("s", lambda e, r=r, half=half, bank=bank: e.copy(
                    st.ytm[:, r * D + half * 512:r * D + (half + 1) * 512],
                    bank[:, 0:512]), rd=[bankB], wr=[st.ytmB[r]])
            if rope:
                reng = "q" if aps["ropeT1F"].dtype == BF16 else "g"
                pr.emit(reng, lambda e, kt=kt, r=r: e.dma_start(
                    rT1[:, r * 1024:(r + 1) * 1024],
                    aps["ropeT1F"][kt * 128:(kt + 1) * 128, :]), wr=[rT1B[r]], dma=True)
                pr.emit(reng, lambda e, kt=kt, r=r: e.dma_start(
                    rT2[:, r * 1024:(r + 1) * 1024],
                    aps["ropeT2F"][kt * 128:(kt + 1) * 128, :]), wr=[rT2B[r]], dma=True)
                self_rope(st, st.ytm, st.ytmB, r, rT1, rT1B, rT2, rT2B)
            for j0 in range(0, 8, 4):
                fns = []
                for j in range(j0, j0 + 4):
                    def fn(e, r=r, j=j):
                        return e.transpose(
                            st.ptb[:, (j % 4) * 256:(j % 4) * 256 + 128],
                            st.ytm[:, r * D + j * 128:r * D + (j + 1) * 128],
                            st.ident[:, :])
                    fns.append(fn)
                pr.group("t", fns, rd=[st.ytmB[r], st.identB], wr=[st.ptbB])
                for j in range(j0, j0 + 4):
                    def fn(e, sl=sl, j=j):
                        return e.tensor_copy(kstg[:, sl * 1024 + j * 128:
                                                  sl * 1024 + (j + 1) * 128],
                                      st.ptb[:, (j % 4) * 256:(j % 4) * 256 + 128])
                    pr.emit("v", fn, rd=[st.ptbB], wr=[kstgB[sl][j0 // 4]])
                for j in range(j0, j0 + 4):
                    def fn(e, kt=kt, sl=sl, j=j):
                        return e.dma_start(
                            bass.AP(Kd.tensor, kt * 128 * 1024 + j * 128,
                                    [[1024, 128], [1, 128]]),
                            kstg[:, sl * 1024 + j * 128:sl * 1024 + (j + 1) * 128])
                    tok = pr.emit("q", fn, rd=[kstgB[sl][j0 // 4]], dma=True)
                    KdB[kt].r.append(tok)
            for half in range(2):
                bank, bankB = st.bank4()
                fns = []
                for ki in range(8):
                    def fn(e, s2=s2, gi=gi, ki=ki, half=half, bank=bank):
                        return e.matmul(
                            bank[:, 0:512],
                            kvxf[:, s2 * 2048 + ki * 256 + gi * 128:
                                 s2 * 2048 + ki * 256 + (gi + 1) * 128],
                            W["wv"][:, ki * 1024 + half * 512:ki * 1024 + (half + 1) * 512],
                            start=(ki == 0), stop=(ki == 7))
                    fns.append(fn)
                pr.group("t", fns, rd=[W["wvB"], kvxfB4[s2 * 16 + gi]],
                         wr=[bankB], extra=W["wvtoks"])
                for hh in range(8):
                    h = half * 8 + hh
                    def fn(e, hh=hh, h=h, bank=bank):
                        return e.tensor_copy(
                            vstg[:, VOFF + h * 65:VOFF + h * 65 + 64],
                            bank[:, hh * 64:hh * 64 + 64])
                    pr.emit("v", fn, rd=[bankB], wr=[vstgB])
            tok = pr.emit("q", lambda e, kt=kt: e.dma_start(
                bass.AP(Vd.tensor, kt * 128 * 1040, [[1040, 128], [1, 1040]]),
                vstg[:, VOFF:VOFF + 1040]), rd=[vstgB], dma=True)
            VdB[kt].r.append(tok)
            load_kv_slab(kt, 0)

    for hg in range(4):
        heads = range(hg * 4, hg * 4 + 4)
        if hg > 0:
            for kt in range(KT):
                load_kv_slab(kt, hg)
        if layer == "l3":
            st.l3_load_hg_bias(heads)
        for qb in range(T // 512):
            qrd = [qfmB[t] for t in range(qb * 4, qb * 4 + 4)]
            for hp in range(2):
                pvs = [(st.pb[4 + i2], st.pbB[4 + i2]) for i2 in range(2)]
                pend = None
                for j, kt in enumerate(range(KT)):
                    ba = (j % 2) * 2
                    fns = []
                    for i2 in range(2):
                        hi = 2 * hp + i2
                        h = hg * 4 + hi
                        bank = st.pb[ba + i2]
                        ks = kxl[64 * (h % 2):64 * (h % 2) + 64,
                                 kt * 256 + (hi // 2) * 128:
                                 kt * 256 + (hi // 2) * 128 + 128]
                        qs = qfm[64 * (h % 2):64 * (h % 2) + 64,
                                 (h // 2) * T + qb * 512:
                                 (h // 2) * T + (qb + 1) * 512]
                        def fn(e, bank=bank, ks=ks, qs=qs):
                            return e.matmul(bank[:, 0:512], ks, qs,
                                            start=True, stop=True)
                        fns.append(fn)
                    pr.group("t", fns, rd=[kxlB[kt]] + qrd,
                             wr=[st.pbB[ba], st.pbB[ba + 1]])
                    for i2 in range(2):
                        hi = 2 * hp + i2
                        r = ba + i2
                        bank, bankB = st.pb[r], st.pbB[r]
                        pro = st.aprobs[:, r * 512:(r + 1) * 512]
                        proB = st.aprobsB[r]
                        if layer == "l3":
                            bias = l3_bias(st, kt, qb, hi)
                        else:
                            bias = None
                        if bias is not None and bias[0] == "tile":
                            sco = st.asco[:, i2 * 512:(i2 + 1) * 512]
                            scoB = st.ascoB[i2]
                            pr.emit("v", lambda e, bank=bank, sco=sco, bias=bias:
                                    e.scalar_tensor_tensor(
                                sco, bank[:, 0:512], scale, bias[1],
                                ALU.mult, ALU.add),
                                rd=[bankB] + bias[2], wr=[scoB])
                            pr.emit("s", lambda e, sco=sco, pro=pro: e.activation(
                                pro, sco, AF.Exp), rd=[scoB], wr=[proB])
                        elif bias is not None and bias[0] == "const":
                            pr.emit("s", lambda e, bank=bank, pro=pro, bias=bias:
                                    e.activation(
                                pro, bank[:, 0:512], AF.Exp, bias=bias[1],
                                scale=scale), rd=[bankB] + bias[2], wr=[proB])
                        else:
                            pr.emit("s", lambda e, bank=bank, pro=pro: e.activation(
                                pro, bank[:, 0:512], AF.Exp, scale=scale),
                                rd=[bankB], wr=[proB])
                    if pend is not None:
                        pr.group("t", pend[0], rd=pend[1], wr=pend[2])
                    fns = []
                    rds = [vxlB[kt]]
                    wrs = []
                    for i2 in range(2):
                        hi = 2 * hp + i2
                        r = ba + i2
                        pro = st.aprobs[:, r * 512:(r + 1) * 512]
                        vs = vxl[:, kt * 260 + hi * 65:kt * 260 + (hi + 1) * 65]
                        pvb, pvbB = pvs[i2]
                        def fn(e, vs=vs, pro=pro, pvb=pvb,
                               first=(j == 0), last=(j == KT - 1)):
                            return e.matmul(pvb[0:65, 0:512], vs, pro,
                                            start=first, stop=last)
                        fns.append(fn)
                        rds.append(st.aprobsB[r])
                        wrs.append(pvbB)
                    pend = (fns, rds, wrs)
                pr.group("t", pend[0], rd=pend[1], wr=pend[2])
                for i2 in range(2):
                    hi = 2 * hp + i2
                    h = hg * 4 + hi
                    pvb, pvbB = pvs[i2]
                    pr.emit("s", lambda e, pvb=pvb: e.copy(st.aznrm[64:65, 0:512],
                                                           pvb[64:65, 0:512]),
                            rd=[pvbB], wr=[st.azrowB])
                    pr.emit("v", lambda e: e.reciprocal(st.aznrm[64:65, 512:1024],
                                                        st.aznrm[64:65, 0:512]),
                            rd=[st.azrowB], wr=[st.azrecB])
                    bank, bankB = st.pb[0], st.pbB[0]
                    pr.group("t", [lambda e, bank=bank: e.matmul(
                        bank[0:64, 0:512], st.aznrm[64:65, 1024:1088], st.aznrm[64:65, 512:1024],
                        start=True, stop=True)], rd=[st.aones64B, st.azrecB],
                        wr=[bankB])
                    pr.emit("s", lambda e, bank=bank: e.copy(st.aznrm[0:64, 0:512],
                                                             bank[0:64, 0:512]),
                            rd=[bankB], wr=[st.azbB])
                    r2 = hi % 2
                    stg = st.astg[:, r2 * 512:(r2 + 1) * 512]
                    pr.emit("v", lambda e, pvb=pvb, stg=stg: e.tensor_tensor(
                        stg[0:64, :], pvb[0:64, 0:512], st.aznrm[0:64, 0:512], ALU.mult),
                        rd=[pvbB, st.azbB], wr=[st.astgB[r2]])
                    ch = h // 2
                    off = (ch * 128 + 64 * (h % 2)) * T + qb * 512
                    tok = pr.emit("q", lambda e, stg=stg, off=off: e.dma_start(
                        bass.AP(attn_d.tensor, off, [[T, 64], [1, 512]]),
                        stg[0:64, :]),
                        rd=[st.astgB[r2]], dma=True)
                    attn_dB.r.append(tok)

    st.dbg_qfm = (qfm, qfmB)
    st.dbg_Kd = (Kd, KdB)
    st.dbg_Vd = (Vd, VdB)
    st.dbg_attnd = (attn_d, attn_dB)
    ffn_layer(st, woT, 8, 8, ("dram", attn_d, attn_dB),
              ("sbuf", st.yfm, st.yfB), None, OBG=8 if rope else 4)
    st.ln_out_add_fm(st.yfm, st.yfB)


def self_rope(st, tm, tmB, r, rT1, rT1B, rT2, rT2B):
    pr = st.pr
    u = st.arp_u[:, 0:1024]
    v = st.arp_v[:, 0:1024]
    t = tm[:, r * D:(r + 1) * D]
    pr.emit("v", lambda e, t=t: e.tensor_tensor(u, t, rT1[:, r * D:(r + 1) * D],
                                                ALU.mult),
            rd=[tmB[r], rT1B[r]], wr=[st.arp_uB])
    pr.emit("v", lambda e, t=t: e.tensor_tensor(v, t, rT2[:, r * D:(r + 1) * D],
                                                ALU.mult),
            rd=[tmB[r], rT2B[r]], wr=[st.arp_vB])
    def sv(base, off):
        return bass.AP(base.tensor if hasattr(base, "tensor") else base,
                       off, [[1024, 128], [64, 16], [1, 32]])
    ur = bass.AP(st.arp_u, 0, [[1024, 128], [64, 16], [1, 32]])
    ui = bass.AP(st.arp_u, 32, [[1024, 128], [64, 16], [1, 32]])
    vr = bass.AP(st.arp_v, 0, [[1024, 128], [64, 16], [1, 32]])
    vi = bass.AP(st.arp_v, 32, [[1024, 128], [64, 16], [1, 32]])
    tr = bass.AP(tm, r * D + 0, [[tm.shape[1], 128], [64, 16], [1, 32]])
    ti = bass.AP(tm, r * D + 32, [[tm.shape[1], 128], [64, 16], [1, 32]])
    pr.emit("v", lambda e: e.tensor_tensor(tr, ur, vi, ALU.subtract),
            rd=[st.arp_uB, st.arp_vB], wr=[tmB[r]])
    pr.emit("v", lambda e: e.tensor_tensor(ti, ui, vr, ALU.add),
            rd=[st.arp_uB, st.arp_vB], wr=[tmB[r]])


def l3_bias(st, kt, qb, hi):
    delta = kt * 128 - qb * 512
    if delta <= -256:
        return ("const", st.l3_fcol[:, hi * 3 + 0:hi * 3 + 1], [st.l3_fcolB])
    if 640 <= delta <= 1919:
        return ("const", st.l3_fcol[:, hi * 3 + 1:hi * 3 + 2], [st.l3_fcolB])
    if delta >= 2688:
        return ("const", st.l3_fcol[:, hi * 3 + 2:hi * 3 + 3], [st.l3_fcolB])
    if delta <= 512:
        woff, C = 0, 3456
    else:
        woff, C = 1152, 5504
    w0 = 4095 + delta - C
    ap = bass.AP(st.l3_strip, hi * 2432 + woff + w0,
                 [[4 * 2432, 128], [-1, 512]])
    return ("tile", ap, [st.l3_stripB])


L0A_WSPECS = {
    "L0_attn_w_qTP": (1024, 1024), "L0_attn_w_kTP": (1024, 1024),
    "L0_attn_w_vT": (1024, 1024), "L0_attn_w_oT": (1024, 1024),
    "ropeT1": (HALF, 1024), "ropeT2": (HALF, 1024),
    "ropeT1F": (S, 1024), "ropeT2F": (S, 1024),
}

L3A_WSPECS = {
    "L3_attn_WqT": (1024, 1024), "L3_attn_WkT": (1024, 1024),
    "L3_attn_WvT": (1024, 1024), "L3_attn_WoT": (1024, 1024),
    "L3_attn_rel_emb": (32, 16), "l3_ohT": (32, 8191),
}


def _rope_tables_jax():
    import jax
    import jax.numpy as jnp
    cpu = jax.local_devices(backend="cpu")[0]
    with jax.default_device(cpu):
        i = jnp.arange(32, dtype=jnp.float32)
        angles = jnp.power(jnp.asarray(10000.0, jnp.float32), 2.0 * i / 64)
        theta = jnp.arange(S, dtype=jnp.float32)[:, None] * angles[None, :]
        c = np.asarray(jnp.cos(theta))
        s = np.asarray(jnp.sin(theta))
    return c, s


def _perm_cols():
    idx = np.zeros(1024, np.int64)
    for h in range(16):
        for j in range(64):
            old = h * 64 + (2 * j if j < 32 else 2 * (j - 32) + 1)
            idx[h * 64 + j] = old
    return idx


def host_l0a_weights(I):
    cos, sin = _rope_tables_jax()
    C = np.zeros((S, 1024), np.float32)
    Sg = np.zeros((S, 1024), np.float32)
    for h in range(16):
        C[:, h * 64:h * 64 + 32] = cos
        C[:, h * 64 + 32:h * 64 + 64] = cos
        Sg[:, h * 64:h * 64 + 32] = -sin
        Sg[:, h * 64 + 32:h * 64 + 64] = sin
    T1 = C
    T2 = np.zeros((S, 1024), np.float32)
    for h in range(16):
        T2[:, h * 64:h * 64 + 32] = sin
        T2[:, h * 64 + 32:h * 64 + 64] = sin
    perm = _perm_cols()
    return {
        "L0_attn_w_qTP": np.ascontiguousarray(I["L0_attn_w_q"].T[:, perm]),
        "L0_attn_w_kTP": np.ascontiguousarray(I["L0_attn_w_k"].T[:, perm]),
        "L0_attn_w_vT": np.ascontiguousarray(I["L0_attn_w_v"].T),
        "L0_attn_w_oT": np.ascontiguousarray(I["L0_attn_w_o"].T),
        "ropeT1F": T1.astype(ml_dtypes.bfloat16),
        "ropeT2F": T2.astype(ml_dtypes.bfloat16),
    }


def host_l0a_percore(q_tok0):
    cos, sin = _rope_tables_jax()
    T1 = np.zeros((HALF, 1024), np.float32)
    T2 = np.zeros((HALF, 1024), np.float32)
    c = cos[q_tok0:q_tok0 + HALF]
    s = sin[q_tok0:q_tok0 + HALF]
    for h in range(16):
        T1[:, h * 64:h * 64 + 64] = np.concatenate([c, c], axis=1)
        T2[:, h * 64:h * 64 + 64] = np.concatenate([s, s], axis=1)
    return {"ropeT1": T1.astype(ml_dtypes.bfloat16),
            "ropeT2": T2.astype(ml_dtypes.bfloat16)}


def _bucket_np(rp):
    import math
    nb = 16
    rb = (rp > 0).astype(np.int64) * nb
    rp = np.abs(rp)
    me = nb // 2
    small = rp < me
    rpf = np.maximum(rp, 1).astype(np.float32)
    lg = me + (np.log(rpf / me) / math.log(128 / me) * (nb - me)).astype(np.int64)
    lg = np.minimum(lg, nb - 1)
    return rb + np.where(small, rp, lg)


def host_l3a_weights(I):
    return {
        "L3_attn_WqT": np.ascontiguousarray(I["L3_attn_Wq"].T),
        "L3_attn_WkT": np.ascontiguousarray(I["L3_attn_Wk"].T),
        "L3_attn_WvT": np.ascontiguousarray(I["L3_attn_Wv"].T),
        "L3_attn_WoT": np.ascontiguousarray(I["L3_attn_Wo"].T),
        "L3_attn_rel_emb": I["L3_attn_rel_emb"],
    }


def host_l3a_percore(q_tok0):
    i = np.arange(8191)
    bk = _bucket_np(i - 4095 - q_tok0)
    oh = np.zeros((32, 8191), ml_dtypes.bfloat16)
    oh[bk, i] = 1.0
    return {"l3_ohT": oh}


def build_stage_fullattn(layer, NT=16):
    nc = bass.Bass("TRN2", target_bir_lowering=False)
    T = NT * 128
    xin = nc.dram_tensor("xin", [T, D], F32, kind="ExternalInput")
    xfull = nc.dram_tensor("xfull", [S, D], F32, kind="ExternalInput")
    ident = nc.dram_tensor("ident", [128, 128], F32, kind="ExternalInput")
    xout = nc.dram_tensor("xout", [T, D], F32, kind="ExternalOutput")
    aps = {"xin": xin.ap(), "xfull": xfull.ap(), "ident": ident.ap(),
           "xout": xout.ap()}
    wspecs = L0A_WSPECS if layer == "l0" else L3A_WSPECS
    for name, shape in wspecs.items():
        aps[name] = nc.dram_tensor(name, list(shape), F32, kind="ExternalInput").ap()
    pr = PR(nc)
    st = Stage(pr, NT, xin=aps["xin"], xout=aps["xout"], arenas=("ffn",),
               wg_cols=8192 if layer == "l0" else 4096, slab_cols=4096)
    if layer == "l0":
        st.arp_u = pr.sb("arp_u", [128, 1024], BF16)
        st.arp_uB = Buf()
        st.arp_v = pr.sb("arp_v", [128, 1024], BF16)
        st.arp_vB = Buf()
    st.load_consts(aps["ident"])
    st.ln_in()
    emit_full_attn(st, aps, layer)
    toks = [(sm, v) for sm, v in pr.counts.items() if v > 0]
    pr.emit("g", lambda e: e.engine_nop(), extra=toks)
    pr.build_block()
    pr.close()
    return nc



L2FRONT_WSPECS = {"L2_attn_in_projT": (1024, 4096)}


def emit_l2front(st, aps, xz_d, xz_dB):
    ffn_layer(st, aps["L2_attn_in_projT"], 8, 32, ("sbuf", st.xln_fm, st.xlnfB),
              ("dram", xz_d, xz_dB), None, OBG=8)


def build_stage_l2front(NT=16):
    nc = bass.Bass("TRN2", target_bir_lowering=False)
    T = NT * 128
    xin = nc.dram_tensor("xin", [T, D], F32, kind="ExternalInput")
    ident = nc.dram_tensor("ident", [128, 128], F32, kind="ExternalInput")
    xz_out = nc.dram_tensor("xz_out", [4096, T], BF16, kind="ExternalOutput")
    aps = {"xin": xin.ap(), "ident": ident.ap()}
    for name, shape in L2FRONT_WSPECS.items():
        aps[name] = nc.dram_tensor(name, list(shape), F32, kind="ExternalInput").ap()
    pr = PR(nc)
    st = Stage(pr, NT, xin=aps["xin"], xout=None, arenas=("ffn",))
    st.load_consts(aps["ident"])
    st.ln_in()
    xzB = Buf()
    emit_l2front(st, aps, xz_out.ap(), xzB)
    toks = [(sm, v) for sm, v in pr.counts.items() if v > 0]
    pr.emit("g", lambda e: e.engine_nop(), extra=toks)
    pr.build_block()
    pr.close()
    return nc


L2SCAN_WSPECS = {
    "xcT": (2048, S), "zT": (1024, S),
    "conv_w": (2048, 4), "conv_b": (2048,),
    "x_projT": (2048, 96),
    "dt_projT_sl": (64, 1024), "dt_proj_b_sl": (1024,),
    "A_log_sl": (1024, 16), "D_sl": (1024,),
}


def build_stage_l2scan():
    L = S
    nc = bass.Bass("TRN2", target_bir_lowering=False)
    aps = {}
    for name, shape in L2SCAN_WSPECS.items():
        dt_ = BF16 if name in ("xcT",) else F32
        aps[name] = nc.dram_tensor(name, list(shape), dt_, kind="ExternalInput").ap()
    yg_out = nc.dram_tensor("yg_out", [1024, L], BF16, kind="ExternalOutput")
    pr = PR(nc)
    xs_d = nc.dram_tensor("xs_d", [1024, L], BF16, kind="Internal").ap()
    xs_dB = Buf()
    bc_d = nc.dram_tensor("bc_d", [32, L], F32, kind="Internal").ap()
    bc_dB = Buf()

    xc = pr.sb("xc", [128, L + 8], BF16); xcB = Buf()
    acc = pr.sb("acc", [128, L], F32); accB = Buf()
    xs = pr.sb("xs", [128, L], BF16); xsB = Buf()
    cw = pr.sb("cw", [128, 16 * 4], F32); cwB = Buf()
    cb = pr.sb("cb", [128, 16], F32); cbB = Buf()
    xdt = pr.sb("xdt", [64, L], F32); xdtB = Buf()
    xbc = pr.sb("xbc", [32, L], F32); xbcB = Buf()
    dtc = pr.sb("dtc", [128, L], F32); dtcB = Buf()
    uc = pr.sb("uc", [128, L], BF16); ucB = Buf()
    dA = pr.sb("dA", [128, L], F32); dAB = Buf()
    dbx = pr.sb("dbx", [128, L], BF16); dbxB = Buf()
    hh = pr.sb("hh", [128, L], BF16); hhB = Buf()
    yac = pr.sb("yac", [128, L], F32); yacB = Buf()
    zc = pr.sb("zc", [128, L], F32); zcB = Buf()
    brow = pr.sb("brow", [1, 2 * L], BF16); browB = Buf()
    xdtb = pr.sb("xdtb", [64, L], BF16); xdtbB = Buf()
    dtw = pr.sb("dtw", [64, 8 * 128], BF16); dtwB = Buf()
    dtb = pr.sb("dtb", [128, 8], F32); dtbB = Buf()
    Asb = pr.sb("Asb", [128, 8 * 16], F32); AsbB = Buf()
    Dsb = pr.sb("Dsb", [128, 8], F32); DsbB = Buf()
    xpw = pr.sb("xpw", [128, 16 * 96], BF16); xpwB = Buf()
    onesc = pr.sb("onesc", [128, 1], F32); onescB = Buf()
    onesr = pr.sb("onesr", [1, 128], BF16); onesrB = Buf()
    pb = [pr.ps(f"pb{i}", [128, 512], F32) for i in range(6)]
    pbB = [Buf(f"pb{i}") for i in range(6)]
    _rot = [0]

    def bank():
        i = _rot[0] % 6
        _rot[0] += 1
        return pb[i], pbB[i]

    pr.emit("v", lambda e: e.memset(onesc[:, :], 1.0), wr=[onescB])
    pr.emit("v", lambda e: e.memset(onesr[:, :], 1.0), wr=[onesrB])
    pr.emit("q", lambda e: e.dma_start(
        cw[:, :], bass.AP(aps["conv_w"].tensor, 0, [[4, 128], [512, 16], [1, 4]])),
        wr=[cwB], dma=True)
    col_vec_load(pr, aps["conv_b"], cb, cbB, 16)
    weng, wdep = wsrc(st, aps["x_projT"])
    pr.emit(weng, lambda e: e.dma_start(
        xpw[:, :], bass.AP(aps["x_projT"].tensor, 0, [[96, 128], [12288, 16], [1, 96]])),
        wr=[xpwB], rdall=wdep, dma=True)
    weng, wdep = wsrc(st, aps["dt_projT_sl"])
    pr.emit(weng, lambda e: e.dma_start(
        dtw[:, :], bass.AP(aps["dt_projT_sl"].tensor, 0, [[1024, 64], [1, 1024]])),
        wr=[dtwB], rdall=wdep, dma=True)
    col_vec_load(pr, aps["dt_proj_b_sl"], dtb, dtbB, 8)
    pr.emit("q", lambda e: e.dma_start(
        Asb[:, :], bass.AP(aps["A_log_sl"].tensor, 0, [[16, 128], [2048, 8], [1, 16]])),
        wr=[AsbB], dma=True)
    pr.emit("s", lambda e: e.activation(Asb[:, :], Asb[:, :], AF.Exp, scale=1.0),
            rd=[AsbB], wr=[AsbB])
    pr.emit("v", lambda e: e.tensor_scalar(Asb[:, :], Asb[:, :], -1.0, None, ALU.mult),
            rd=[AsbB], wr=[AsbB])
    col_vec_load(pr, aps["D_sl"], Dsb, DsbB, 8)

    first = [True]
    for ch in range(16):
        pr.emit("v", lambda e: e.memset(xc[:, 0:8], 0.0), wr=[xcB])
        pr.emit("q", lambda e, ch=ch: e.dma_start(
            xc[:, 8:8 + L], aps["xcT"][ch * 128:(ch + 1) * 128, :]),
            wr=[xcB], dma=True)
        pr.emit("v", lambda e, ch=ch: e.tensor_scalar(
            acc[:, :], xc[:, 5:5 + L], cw[:, ch * 4 + 0:ch * 4 + 1], None, ALU.mult),
            rd=[xcB, cwB], wr=[accB])
        for t in range(1, 4):
            def fn(e, ch=ch, t=t):
                return e.scalar_tensor_tensor(
                    acc[:, :], xc[:, 5 + t:5 + t + L], cw[:, ch * 4 + t:ch * 4 + t + 1],
                    acc[:, :], ALU.mult, ALU.add)
            pr.emit("v", fn, rd=[xcB, cwB, accB], wr=[accB])
        pr.emit("s", lambda e, ch=ch: e.activation(
            xs[:, :], acc[:, :], AF.Silu, bias=cb[:, ch:ch + 1], scale=1.0),
            rd=[accB, cbB], wr=[xsB])
        if ch < 8:
            tok = pr.emit_sw("q", lambda e, cc, ch=ch: e.dma_start(
                bass.AP(xs_d.tensor, ch * 128 * L, [[L, 128], [1, L]]), xs[:, :]),
                rd=[xsB], dma=True)
            xs_dB.r.append(tok)
        for lb in range(8):
            bank_, bankB_ = bank()
            pr.group("t", [lambda e, ch=ch, lb=lb, bank_=bank_: e.matmul(
                bank_[0:64, 0:512], xpw[:, ch * 96:ch * 96 + 64],
                xs[:, lb * 512:(lb + 1) * 512], start=True, stop=True)],
                rd=[xpwB, xsB], wr=[bankB_])
            if ch == 0:
                pr.emit("s", lambda e, lb=lb, bank_=bank_: e.copy(
                    xdt[:, lb * 512:(lb + 1) * 512], bank_[0:64, 0:512]),
                    rd=[bankB_], wr=[xdtB])
            else:
                pr.emit("v", lambda e, lb=lb, bank_=bank_: e.tensor_tensor(
                    xdt[:, lb * 512:(lb + 1) * 512], xdt[:, lb * 512:(lb + 1) * 512],
                    bank_[0:64, 0:512], ALU.add), rd=[bankB_, xdtB], wr=[xdtB])
            bank_, bankB_ = bank()
            pr.group("t", [lambda e, ch=ch, lb=lb, bank_=bank_: e.matmul(
                bank_[0:32, 0:512], xpw[:, ch * 96 + 64:ch * 96 + 96],
                xs[:, lb * 512:(lb + 1) * 512], start=True, stop=True)],
                rd=[xpwB, xsB], wr=[bankB_])
            if ch == 0:
                pr.emit("s", lambda e, lb=lb, bank_=bank_: e.copy(
                    xbc[:, lb * 512:(lb + 1) * 512], bank_[0:32, 0:512]),
                    rd=[bankB_], wr=[xbcB])
            else:
                pr.emit("v", lambda e, lb=lb, bank_=bank_: e.tensor_tensor(
                    xbc[:, lb * 512:(lb + 1) * 512], xbc[:, lb * 512:(lb + 1) * 512],
                    bank_[0:32, 0:512], ALU.add), rd=[bankB_, xbcB], wr=[xbcB])
    tok = pr.emit("q", lambda e: e.dma_start(bc_d, xbc[0:32, :]), rd=[xbcB], dma=True)
    bc_dB.r.append(tok)
    pr.emit("v", lambda e: e.tensor_copy(xdtb[:, :], xdt[:, :]), rd=[xdtB], wr=[xdtbB])

    for dch in range(8):
        for lb in range(8):
            bank_, bankB_ = bank()
            pr.group("t", [lambda e, dch=dch, lb=lb, bank_=bank_: e.matmul(
                bank_[:, 0:512], dtw[:, dch * 128:(dch + 1) * 128],
                xdtb[:, lb * 512:(lb + 1) * 512], start=True, stop=True)],
                rd=[dtwB, xdtbB], wr=[bankB_])
            pr.emit("s", lambda e, dch=dch, lb=lb, bank_=bank_: e.activation(
                dtc[:, lb * 512:(lb + 1) * 512], bank_[:, 0:512], AF.Exp,
                bias=dtb[:, dch:dch + 1], scale=1.0), rd=[bankB_, dtbB], wr=[dtcB])
        pr.emit("s", lambda e: e.activation(dtc[:, :], dtc[:, :], AF.Ln,
                                            bias=onesc[:, 0:1], scale=1.0),
                rd=[dtcB, onescB], wr=[dtcB])
        pr.emit_sw("q", lambda e, cc, dch=dch: e.dma_start(
            xs[:, :], bass.AP(xs_d.tensor, dch * 128 * L, [[L, 128], [1, L]])),
            rdall=[xs_dB], wr=[xsB], dma=True)
        pr.emit("v", lambda e: e.tensor_tensor(uc[:, :], dtc[:, :], xs[:, :], ALU.mult),
                rd=[dtcB, xsB], wr=[ucB])
        pr.emit("v", lambda e: e.memset(yac[:, :], 0.0), wr=[yacB])
        for n in range(16):
            pr.emit("g", lambda e, n=n: e.dma_start(
                brow[0:1, 0:L], bass.AP(bc_d.tensor, n * L, [[1, 1], [1, L]])),
                rdall=[bc_dB], wr=[browB], dma=True)
            pr.emit("g", lambda e, n=n: e.dma_start(
                brow[0:1, L:2 * L], bass.AP(bc_d.tensor, (16 + n) * L, [[1, 1], [1, L]])),
                rdall=[bc_dB], wr=[browB], dma=True)
            pr.emit("s", lambda e, dch=dch, n=n: e.activation(
                dA[:, :], dtc[:, :], AF.Exp,
                scale=Asb[:, dch * 16 + n:dch * 16 + n + 1]),
                rd=[dtcB, AsbB], wr=[dAB])
            for lb in range(8):
                bank_, bankB_ = bank()
                pr.group("t", [lambda e, lb=lb, bank_=bank_: e.matmul(
                    bank_[:, 0:512], onesr[0:1, :], brow[0:1, lb * 512:(lb + 1) * 512],
                    start=True, stop=True)], rd=[onesrB, browB], wr=[bankB_])
                pr.emit("v", lambda e, lb=lb, bank_=bank_: e.tensor_tensor(
                    dbx[:, lb * 512:(lb + 1) * 512], uc[:, lb * 512:(lb + 1) * 512],
                    bank_[:, 0:512], ALU.mult), rd=[ucB, bankB_], wr=[dbxB])
            pr.emit("v", lambda e: e.tensor_tensor_scan(
                hh[:, 0:2048], dA[:, 0:2048], dbx[:, 0:2048], 0.0,
                ALU.mult, ALU.add), rd=[dAB, dbxB], wr=[hhB])
            pr.emit("v", lambda e: e.tensor_tensor_scan(
                hh[:, 2048:4096], dA[:, 2048:4096], dbx[:, 2048:4096],
                hh[:, 2047:2048], ALU.mult, ALU.add), rd=[dAB, dbxB, hhB], wr=[hhB])
            for lb in range(8):
                bank_, bankB_ = bank()
                pr.group("t", [lambda e, lb=lb, bank_=bank_: e.matmul(
                    bank_[:, 0:512], onesr[0:1, :],
                    brow[0:1, L + lb * 512:L + (lb + 1) * 512],
                    start=True, stop=True)], rd=[onesrB, browB], wr=[bankB_])
                pr.emit("v", lambda e, lb=lb, bank_=bank_: e.tensor_tensor(
                    acc[:, lb * 512:(lb + 1) * 512], hh[:, lb * 512:(lb + 1) * 512],
                    bank_[:, 0:512], ALU.mult), rd=[hhB, bankB_], wr=[accB])
            pr.emit("v", lambda e: e.tensor_tensor(yac[:, :], yac[:, :], acc[:, :],
                                                   ALU.add), rd=[accB, yacB], wr=[yacB])
        pr.emit("v", lambda e, dch=dch: e.scalar_tensor_tensor(
            yac[:, :], xs[:, :], Dsb[:, dch:dch + 1], yac[:, :], ALU.mult, ALU.add),
            rd=[xsB, DsbB, yacB], wr=[yacB])
        pr.emit("q", lambda e, dch=dch: e.dma_start(
            zc[:, :], aps["zT"][dch * 128:(dch + 1) * 128, :]), wr=[zcB], dma=True)
        pr.emit("s", lambda e: e.activation(zc[:, :], zc[:, :], AF.Silu, scale=1.0),
                rd=[zcB], wr=[zcB])
        pr.emit("v", lambda e: e.tensor_tensor(yac[:, :], yac[:, :], zc[:, :],
                                               ALU.mult), rd=[zcB, yacB], wr=[yacB])
        tok = pr.emit("g", lambda e, dch=dch: e.dma_start(
            yg_out.ap()[dch * 128:(dch + 1) * 128, :], yac[:, :]),
            rd=[yacB], dma=True)
    toks = [(sm, v) for sm, v in pr.counts.items() if v > 0]
    pr.emit("g", lambda e: e.engine_nop(), extra=toks)
    pr.build_block()
    pr.close()
    return nc


def host_l2scan_percore(xz_row, dh, I):
    return {
        "xcT": np.ascontiguousarray(xz_row[0:2048][np.r_[dh * 1024:(dh + 1) * 1024, (1 - dh) * 1024:(2 - dh) * 1024]]),
        "zT": np.ascontiguousarray(xz_row[2048 + dh * 1024: 2048 + (dh + 1) * 1024]).astype(np.float32),
        "conv_w": np.ascontiguousarray(I["L2_attn_conv_w"][:, 0, :][np.r_[dh * 1024:(dh + 1) * 1024, (1 - dh) * 1024:(2 - dh) * 1024]]),
        "conv_b": np.ascontiguousarray(I["L2_attn_conv_b"][np.r_[dh * 1024:(dh + 1) * 1024, (1 - dh) * 1024:(2 - dh) * 1024]]),
        "x_projT": np.ascontiguousarray(I["L2_attn_x_proj_w"].T[np.r_[dh * 1024:(dh + 1) * 1024, (1 - dh) * 1024:(2 - dh) * 1024]]),
        "dt_projT_sl": np.ascontiguousarray(I["L2_attn_dt_proj_w"].T[:, dh * 1024:(dh + 1) * 1024]),
        "dt_proj_b_sl": I["L2_attn_dt_proj_b"][dh * 1024:(dh + 1) * 1024],
        "A_log_sl": I["L2_attn_A_log"][dh * 1024:(dh + 1) * 1024],
        "D_sl": I["L2_attn_D"][dh * 1024:(dh + 1) * 1024],
    }


L2OUT_WSPECS = {"L2_attn_out_projT": (2048, 1024), "ygT": (2048, 2048)}


def build_stage_l2out(NT=16):
    nc = bass.Bass("TRN2", target_bir_lowering=False)
    T = NT * 128
    xin = nc.dram_tensor("xin", [T, D], F32, kind="ExternalInput")
    ident = nc.dram_tensor("ident", [128, 128], F32, kind="ExternalInput")
    xout = nc.dram_tensor("xout", [T, D], F32, kind="ExternalOutput")
    aps = {"xin": xin.ap(), "ident": ident.ap(), "xout": xout.ap()}
    aps["L2_attn_out_projT"] = nc.dram_tensor("L2_attn_out_projT", [2048, 1024],
                                              F32, kind="ExternalInput").ap()
    ygT = nc.dram_tensor("ygT", [2048, T], BF16, kind="ExternalInput")
    pr = PR(nc)
    st = Stage(pr, NT, xin=aps["xin"], xout=aps["xout"], arenas=("ffn",),
               wg_cols=8192, slab_cols=8192)
    st.load_consts(aps["ident"])
    ygB = Buf()
    ffn_layer(st, aps["L2_attn_out_projT"], 16, 8, ("dram", ygT.ap(), ygB),
              ("sbuf", st.yfm, st.yfB), None, OBG=4)
    st.ln_out_add_fm(st.yfm, st.yfB)
    toks = [(sm, v) for sm, v in pr.counts.items() if v > 0]
    pr.emit("g", lambda e: e.engine_nop(), extra=toks)
    pr.build_block()
    pr.close()
    return nc



_PROGS = {}


def _prog(name, builder):
    if name not in _PROGS:
        _PROGS[name] = builder()
    return _PROGS[name]


def _run(nc, in_maps):
    return run_bass_kernel_spmd(nc, in_maps, core_ids=list(range(8)))


def _gather_x(res):
    out = np.empty((B, S, D), np.float32)
    for c in range(8):
        b, h = c // 2, c % 2
        out[b, h * HALF:(h + 1) * HALF] = res.results[c]["xout"]
    return out


def kernel_unfused(**inputs):
    import ml_dtypes
    I = {k: np.asarray(v) for k, v in inputs.items()}
    x = np.asarray(I["L0_attn_x"], np.float32)
    ident = np.eye(128, dtype=np.float32)

    def maps_tl(x, w, percore=None):
        ms = []
        for c in range(8):
            b, h = c // 2, c % 2
            m = {"xin": np.ascontiguousarray(x[b, h * HALF:(h + 1) * HALF]),
                 "ident": ident}
            if percore is not None:
                m.update(percore(b, h))
            m.update(w)
            ms.append(m)
        return ms

    w = host_l0a_weights(I)
    xc_ = x
    res = _run(_prog("l0a", lambda: build_stage_fullattn("l0", 16)),
               maps_tl(x, w, lambda b, h: dict(
                   xfull=np.ascontiguousarray(xc_[b]), **host_l0a_percore(h * HALF))))
    x = _gather_x(res)
    res = _run(_prog("l0f", lambda: build_stage_tl(emit_l0f, L0F_WSPECS, 16)),
               maps_tl(x, host_l0f_weights(I)))
    x = _gather_x(res)
    w = host_l1a_weights(I)
    xc_ = x
    res = _run(_prog("l1a", lambda: build_stage_l1a(16)),
               maps_tl(x, w, lambda b, h: host_l1a_percore(xc_[b], h * HALF)))
    x = _gather_x(res)
    res = _run(_prog("l1f", lambda: build_stage_l1f(16)),
               maps_tl(x, host_l1f_weights(I)))
    x = _gather_x(res)
    w = {"L2_attn_in_projT": np.ascontiguousarray(I["L2_attn_in_proj_w"].T)}
    res = _run(_prog("l2front", lambda: build_stage_l2front(16)), maps_tl(x, w))
    xz = np.empty((B, 4096, S), ml_dtypes.bfloat16)
    for c in range(8):
        b, h = c // 2, c % 2
        xz[b, :, h * HALF:(h + 1) * HALF] = res.results[c]["xz_out"]
    ms = []
    for c in range(8):
        b, dh = c // 2, c % 2
        ms.append(host_l2scan_percore(xz[b], dh, I))
    res = _run(_prog("l2scan", build_stage_l2scan), ms)
    yg = np.empty((B, 2048, S), ml_dtypes.bfloat16)
    for c in range(8):
        b, dh = c // 2, c % 2
        yg[b, dh * 1024:(dh + 1) * 1024] = res.results[c]["yg_out"]
    w = {"L2_attn_out_projT": np.ascontiguousarray(I["L2_attn_out_proj_w"].T)}
    res = _run(_prog("l2out", lambda: build_stage_l2out(16)),
               maps_tl(x, w, lambda b, h: {
                   "ygT": np.ascontiguousarray(yg[b, :, h * HALF:(h + 1) * HALF])}))
    x = _gather_x(res)
    res = _run(_prog("l2f", lambda: build_stage_tl(emit_l2f, L2F_WSPECS, 16)),
               maps_tl(x, host_l2f_weights(I)))
    x = _gather_x(res)
    w = host_l3a_weights(I)
    xc_ = x
    res = _run(_prog("l3a", lambda: build_stage_fullattn("l3", 16)),
               maps_tl(x, w, lambda b, h: dict(
                   xfull=np.ascontiguousarray(xc_[b]), **host_l3a_percore(h * HALF))))
    x = _gather_x(res)
    res = _run(_prog("l3f", lambda: build_stage_tl(emit_l3f, L3F_WSPECS, 16)),
               maps_tl(x, host_l3f_weights(I)))
    x = _gather_x(res)
    return x



def emit_l2scan_fused(st, aps, XZ, XZB, yg_local, yg_localB):
    pr = st.pr
    nc = pr.nc
    L = S
    xs_d = nc.dram_tensor("f_xs_d", [1024, L], BF16, kind="Internal").ap()
    xs_dB = Buf()
    xc = pr.sb("s_xc", [128, L + 8], BF16); xcB = Buf()
    acc = pr.sb("s_acc", [128, L], F32); accB = Buf()
    xs = pr.sb("s_xs", [128, L], BF16); xsB = Buf()
    cw = pr.sb("s_cw", [128, 16 * 4], F32); cwB = Buf()
    cb = pr.sb("s_cb", [128, 16], F32); cbB = Buf()
    xdt = pr.sb("s_xdt", [64, L], F32); xdtB = Buf()
    xbc = pr.sb("s_xbc", [32, L], F32); xbcB = Buf()
    xdtb = pr.sb("s_xdtb", [64, L], BF16); xdtbB = Buf()
    dtc = pr.sb("s_dtc", [128, L], F32); dtcB = Buf()
    uc = pr.sb("s_uc", [128, L], BF16); ucB = Buf()
    dA = pr.sb("s_dA", [128, L], F32); dAB = Buf()
    dbx = pr.sb("s_dbx", [128, L], BF16); dbxB = Buf()
    hh = pr.sb("s_hh", [128, L], BF16); hhB = Buf()
    yac = pr.sb("s_yac", [128, L], F32); yacB = Buf()
    zc = pr.sb("s_zc", [128, L], BF16); zcB = Buf()
    xbcb = pr.sb("s_xbcb", [32, L], BF16); xbcbB = Buf()
    bcsel = pr.sb("s_bcsel", [32, 4096], BF16); bcselB = Buf()
    dtw = pr.sb("s_dtw", [64, 8 * 128], BF16); dtwB = Buf()
    dtb = pr.sb("s_dtb", [128, 8], F32); dtbB = Buf()
    Asb = pr.sb("s_Asb", [128, 8 * 16], F32); AsbB = Buf()
    Dsb = pr.sb("s_Dsb", [128, 8], F32); DsbB = Buf()
    xpw = pr.sb("s_xpw", [128, 16 * 96], BF16); xpwB = Buf()
    onesc = pr.sb("s_onesc", [128, 1], F32); onescB = Buf()
    onesr = pr.sb("s_onesr", [1, 128], BF16); onesrB = Buf()

    pr.emit("v", lambda e: e.memset(onesc[:, :], 1.0), wr=[onescB])
    pr.emit("v", lambda e: e.memset(onesr[:, :], 1.0), wr=[onesrB])
    pr.emit("q" if aps["l2_bcsel"].dtype == BF16 else "g",
            lambda e: e.dma_start(bcsel[:, :], aps["l2_bcsel"]),
            wr=[bcselB], dma=True)
    pr.emit("q", lambda e: e.dma_start(
        cw[:, :], bass.AP(aps["conv_w"].tensor, 0, [[4, 128], [512, 16], [1, 4]])),
        wr=[cwB], dma=True)
    col_vec_load(pr, aps["conv_b"], cb, cbB, 16)
    weng, wdep = wsrc(st, aps["x_projT"])
    pr.emit(weng, lambda e: e.dma_start(
        xpw[:, :], bass.AP(aps["x_projT"].tensor, 0, [[96, 128], [12288, 16], [1, 96]])),
        wr=[xpwB], rdall=wdep, dma=True)
    weng, wdep = wsrc(st, aps["dt_projT_sl"])
    pr.emit(weng, lambda e: e.dma_start(
        dtw[:, :], bass.AP(aps["dt_projT_sl"].tensor, 0, [[1024, 64], [1, 1024]])),
        wr=[dtwB], rdall=wdep, dma=True)
    col_vec_load(pr, aps["dt_proj_b_sl"], dtb, dtbB, 8)
    pr.emit("q", lambda e: e.dma_start(
        Asb[:, :], bass.AP(aps["A_log_sl"].tensor, 0, [[16, 128], [2048, 8], [1, 16]])),
        wr=[AsbB], dma=True)
    pr.emit("s", lambda e: e.activation(Asb[:, :], Asb[:, :], AF.Exp, scale=1.0),
            rd=[AsbB], wr=[AsbB])
    pr.emit("v", lambda e: e.tensor_scalar(Asb[:, :], Asb[:, :], -1.0, None, ALU.mult),
            rd=[AsbB], wr=[AsbB])
    col_vec_load(pr, aps["D_sl"], Dsb, DsbB, 8)

    pr.sw_begin("q")
    for ch in range(16):
        pr.emit("v", lambda e: e.memset(xc[:, 0:8], 0.0), wr=[xcB])

        def fnv(e, cc, ch=ch):
            dh = cc % 2
            row = (dh * 1024 + ch * 128) if ch < 8 else ((1 - dh) * 1024 + (ch - 8) * 128)
            return e.dma_start(xc[:, 8:8 + L],
                               bass.AP(XZ.tensor, row * L, [[L, 128], [1, L]]))
        pr.emit_sw("q", fnv, rdall=[XZB], wr=[xcB], dma=True)
        pr.emit("v", lambda e, ch=ch: e.tensor_scalar(
            acc[:, :], xc[:, 5:5 + L], cw[:, ch * 4 + 0:ch * 4 + 1], None, ALU.mult),
            rd=[xcB, cwB], wr=[accB])
        for t in range(1, 4):
            def fn(e, ch=ch, t=t):
                return e.scalar_tensor_tensor(
                    acc[:, :], xc[:, 5 + t:5 + t + L], cw[:, ch * 4 + t:ch * 4 + t + 1],
                    acc[:, :], ALU.mult, ALU.add)
            pr.emit("v", fn, rd=[xcB, cwB, accB], wr=[accB])
        pr.emit("s", lambda e, ch=ch: e.activation(
            xs[:, :], acc[:, :], AF.Silu, bias=cb[:, ch:ch + 1], scale=1.0),
            rd=[accB, cbB], wr=[xsB])
        if ch < 8:
            tok = pr.emit_sw("q", lambda e, cc, ch=ch: e.dma_start(
                bass.AP(xs_d.tensor, ch * 128 * L, [[L, 128], [1, L]]), xs[:, :]),
                rd=[xsB], dma=True)
            xs_dB.r.append(tok)
        for lb in range(8):
            bank_, bankB_ = st.bank4()
            pr.group("t", [lambda e, ch=ch, lb=lb, bank_=bank_: e.matmul(
                bank_[0:64, 0:512], xpw[:, ch * 96:ch * 96 + 64],
                xs[:, lb * 512:(lb + 1) * 512], start=True, stop=True)],
                rd=[xpwB, xsB], wr=[bankB_])
            if ch == 0:
                pr.emit("s", lambda e, lb=lb, bank_=bank_: e.copy(
                    xdt[:, lb * 512:(lb + 1) * 512], bank_[0:64, 0:512]),
                    rd=[bankB_], wr=[xdtB])
            else:
                pr.emit("v", lambda e, lb=lb, bank_=bank_: e.tensor_tensor(
                    xdt[:, lb * 512:(lb + 1) * 512], xdt[:, lb * 512:(lb + 1) * 512],
                    bank_[0:64, 0:512], ALU.add), rd=[bankB_, xdtB], wr=[xdtB])
            bank_, bankB_ = st.bank4()
            pr.group("t", [lambda e, ch=ch, lb=lb, bank_=bank_: e.matmul(
                bank_[0:32, 0:512], xpw[:, ch * 96 + 64:ch * 96 + 96],
                xs[:, lb * 512:(lb + 1) * 512], start=True, stop=True)],
                rd=[xpwB, xsB], wr=[bankB_])
            if ch == 0:
                pr.emit("s", lambda e, lb=lb, bank_=bank_: e.copy(
                    xbc[:, lb * 512:(lb + 1) * 512], bank_[0:32, 0:512]),
                    rd=[bankB_], wr=[xbcB])
            else:
                pr.emit("v", lambda e, lb=lb, bank_=bank_: e.tensor_tensor(
                    xbc[:, lb * 512:(lb + 1) * 512], xbc[:, lb * 512:(lb + 1) * 512],
                    bank_[0:32, 0:512], ALU.add), rd=[bankB_, xbcB], wr=[xbcB])
    pr.sw_end()
    pr.emit("v", lambda e: e.tensor_copy(xdtb[:, :], xdt[:, :]), rd=[xdtB], wr=[xdtbB])
    pr.emit("v", lambda e: e.tensor_copy(xbcb[:, :], xbc[0:32, :]), rd=[xbcB], wr=[xbcbB])

    pr.sw_begin("q")
    for dch in range(8):
        for lb in range(8):
            bank_, bankB_ = st.bank4()
            pr.group("t", [lambda e, dch=dch, lb=lb, bank_=bank_: e.matmul(
                bank_[:, 0:512], dtw[:, dch * 128:(dch + 1) * 128],
                xdtb[:, lb * 512:(lb + 1) * 512], start=True, stop=True)],
                rd=[dtwB, xdtbB], wr=[bankB_])
            pr.emit("s", lambda e, dch=dch, lb=lb, bank_=bank_: e.activation(
                dtc[:, lb * 512:(lb + 1) * 512], bank_[:, 0:512], AF.Exp,
                bias=dtb[:, dch:dch + 1], scale=1.0), rd=[bankB_, dtbB], wr=[dtcB])
        pr.emit("s", lambda e: e.activation(dtc[:, :], dtc[:, :], AF.Ln,
                                            bias=onesc[:, 0:1], scale=1.0),
                rd=[dtcB, onescB], wr=[dtcB])
        pr.emit_sw("q", lambda e, cc, dch=dch: e.dma_start(
            xs[:, :], bass.AP(xs_d.tensor, dch * 128 * L, [[L, 128], [1, L]])),
            rdall=[xs_dB], wr=[xsB], dma=True)
        pr.emit("v", lambda e: e.tensor_tensor(uc[:, :], dtc[:, :], xs[:, :], ALU.mult),
                rd=[dtcB, xsB], wr=[ucB])
        pr.emit("v", lambda e: e.memset(yac[:, :], 0.0), wr=[yacB])
        for n in range(16):
            pr.emit("s", lambda e, dch=dch, n=n: e.activation(
                dA[:, :], dtc[:, :], AF.Exp,
                scale=Asb[:, dch * 16 + n:dch * 16 + n + 1]),
                rd=[dtcB, AsbB], wr=[dAB])
            for lb in range(8):
                bank_, bankB_ = st.bank4()
                pr.group("t", [lambda e, n=n, lb=lb, bank_=bank_: e.matmul(
                    bank_[:, 0:512], bcsel[:, n * 128:(n + 1) * 128],
                    xbcb[:, lb * 512:(lb + 1) * 512],
                    start=True, stop=True)], rd=[bcselB, xbcbB], wr=[bankB_])
                pr.emit("v", lambda e, lb=lb, bank_=bank_: e.tensor_tensor(
                    dbx[:, lb * 512:(lb + 1) * 512], uc[:, lb * 512:(lb + 1) * 512],
                    bank_[:, 0:512], ALU.mult), rd=[ucB, bankB_], wr=[dbxB])
            pr.emit("v", lambda e: e.tensor_tensor_scan(
                hh[:, 0:2048], dA[:, 0:2048], dbx[:, 0:2048], 0.0,
                ALU.mult, ALU.add), rd=[dAB, dbxB], wr=[hhB])
            pr.emit("v", lambda e: e.tensor_tensor_scan(
                hh[:, 2048:4096], dA[:, 2048:4096], dbx[:, 2048:4096],
                hh[:, 2047:2048], ALU.mult, ALU.add), rd=[dAB, dbxB, hhB], wr=[hhB])
            for lb in range(8):
                bank_, bankB_ = st.bank4()
                pr.group("t", [lambda e, n=n, lb=lb, bank_=bank_: e.matmul(
                    bank_[:, 0:512], bcsel[:, (16 + n) * 128:(17 + n) * 128],
                    xbcb[:, lb * 512:(lb + 1) * 512],
                    start=True, stop=True)], rd=[bcselB, xbcbB], wr=[bankB_])
                pr.emit("v", lambda e, lb=lb, bank_=bank_: e.tensor_tensor(
                    acc[:, lb * 512:(lb + 1) * 512], hh[:, lb * 512:(lb + 1) * 512],
                    bank_[:, 0:512], ALU.mult), rd=[hhB, bankB_], wr=[accB])
            pr.emit("v", lambda e: e.tensor_tensor(yac[:, :], yac[:, :], acc[:, :],
                                                   ALU.add), rd=[accB, yacB], wr=[yacB])
        pr.emit("v", lambda e, dch=dch: e.scalar_tensor_tensor(
            yac[:, :], xs[:, :], Dsb[:, dch:dch + 1], yac[:, :], ALU.mult, ALU.add),
            rd=[xsB, DsbB, yacB], wr=[yacB])

        def fnv(e, cc, dch=dch):
            dh = cc % 2
            row = 2048 + dh * 1024 + dch * 128
            return e.dma_start(zc[:, :], bass.AP(XZ.tensor, row * L, [[L, 128], [1, L]]))
        pr.emit_sw("q", fnv, rdall=[XZB], wr=[zcB], dma=True)
        pr.emit("s", lambda e: e.activation(zc[:, :], zc[:, :], AF.Silu, scale=1.0),
                rd=[zcB], wr=[zcB])
        pr.emit("v", lambda e: e.tensor_tensor(yac[:, :], yac[:, :], zc[:, :],
                                               ALU.mult), rd=[zcB, yacB], wr=[yacB])
        tok = pr.emit("g", lambda e, dch=dch: e.dma_start(
            bass.AP(yg_local.tensor, dch * 128 * L, [[L, 128], [1, L]]), yac[:, :]),
            rd=[yacB], dma=True)
        yg_localB.r.append(tok)
    pr.sw_end()


MEGA_NT = 16


def build_megakernel():
    nc = bass.Bass("TRN2", target_bir_lowering=False)
    T = MEGA_NT * 128
    xin = nc.dram_tensor("xin", [T, D], F32, kind="ExternalInput")
    xfull0 = nc.dram_tensor("xfull0", [S, D], F32, kind="ExternalInput")
    ident = nc.dram_tensor("ident", [128, 128], F32, kind="ExternalInput")
    xout = nc.dram_tensor("xout", [T, D], F32, kind="ExternalOutput")
    aps = {"ident": ident.ap()}
    allw = {}
    for spec in (L0A_WSPECS, L0F_WSPECS, L1A_WSPECS, L1F_WSPECS, L2FRONT_WSPECS,
                 L2F_WSPECS, L3A_WSPECS, L3F_WSPECS):
        allw.update(spec)
    allw.pop("ropeT1")
    allw.pop("ropeT2")
    BF16_IN = {"ropeT1F", "ropeT2F", "ropeT1", "ropeT2", "l1_masks", "l3_ohT",
               "l2_bcsel"}
    for name, shape in allw.items():
        dt_ = BF16 if name in BF16_IN else F32
        aps[name] = nc.dram_tensor(name, list(shape), dt_, kind="ExternalInput").ap()
    for name, shape in (("ropeT1", (HALF, 1024)), ("ropeT2", (HALF, 1024)),
                        ("l1_kvzero", (128, MEGA_NT + 2)),
                        ("conv_w", (2048, 4)), ("conv_b", (2048,)),
                        ("x_projT", (2048, 96)), ("dt_projT_sl", (64, 1024)),
                        ("dt_proj_b_sl", (1024,)), ("A_log_sl", (1024, 16)),
                        ("D_sl", (1024,)), ("l2_bcsel", (32, 4096)),
                        ("L2_attn_out_projT", (2048, 1024))):
        dt_ = BF16 if name in BF16_IN else F32
        aps[name] = nc.dram_tensor(name, list(shape), dt_, kind="ExternalInput").ap()

    xA = nc.dram_tensor("m_xA", [T, D], F32, kind="Internal").ap()
    xB = nc.dram_tensor("m_xB", [T, D], F32, kind="Internal").ap()
    xz_local = nc.dram_tensor("m_xzl", [4096, T], BF16, kind="Internal").ap()
    yg_local = nc.dram_tensor("m_ygl", [1024, S], BF16, kind="Internal").ap()
    R1 = nc.dram_tensor("m_R1", [S + 256, D], F32, kind="Internal", addr_space="Shared").ap()
    R2 = nc.dram_tensor("m_R2", [S + 256, D], F32, kind="Internal", addr_space="Shared").ap()
    XZ = nc.dram_tensor("m_XZ", [4096, S], BF16, kind="Internal", addr_space="Shared").ap()
    YG = nc.dram_tensor("m_YG", [2048, S], BF16, kind="Internal", addr_space="Shared").ap()
    FLG = nc.dram_tensor("m_FLG", [8, 256], U32, kind="Internal", addr_space="Shared").ap()

    pr = PR(nc)
    pr.need_polls = True
    st = Stage(pr, MEGA_NT, xin=xin.ap(), xout=xA, arenas=(), fm_scoped=True)
    evrow = pr.sb("m_evrow", [1, 8], U32)
    evrowB = Buf()
    fb = pr.sb("m_fb", [1, 256], U32)
    fbB = Buf()
    zrow, zrowB = st.scr, st.scrB
    for ev in range(8):
        pr.emit("v", lambda e, ev=ev: e.memset(evrow[0:1, ev:ev + 1],
                                               MAGIC + ev + 1), wr=[evrowB])
    pr.emit("v", lambda e: e.memset(zrow[:, :], 0.0), wr=[zrowB])
    st.load_consts(aps["ident"])
    attn_arenas(st, need_sco=True)
    st.wdeps = {}
    CACHE_W = ["L0_attn_w_qTP", "L0_attn_w_kTP", "L0_attn_w_vT", "L0_attn_w_oT",
               "L0_ffn_WT",
               "L1_attn_Wq", "L1_attn_Wk", "L1_attn_WkS", "L1_attn_Wv",
               "L1_attn_Wo",
               "L1_ffn_experts_wg", "L1_ffn_experts_wu", "L1_ffn_experts_wd",
               "L1_ffn_shared_wg", "L1_ffn_shared_wu", "L1_ffn_shared_wd",
               "L2_attn_in_projT", "x_projT", "dt_projT_sl",
               "L2_attn_out_projT",
               "L2_ffn_w1_wT", "L2_ffn_w2_wT", "L2_ffn_h1_wT", "L2_ffn_h2_wT",
               "L2_ffn_w3_wT",
               "L3_attn_WqT", "L3_attn_WkT", "L3_attn_WvT", "L3_attn_WoT"]
    _cl = [2]
    for name in CACHE_W:
        apf = aps[name]
        ct = nc.dram_tensor("wc_" + name, list(apf.tensor.shape), BF16,
                            kind="Internal").ap()
        cb = Buf()
        ln = _cl[0]
        _cl[0] = 5 - ln
        pr.emit("g", lambda e, ct=ct, apf=apf: e.dma_start(ct, apf),
                wr=[cb], dma=True, lane=ln)
        st.wdeps[ct.tensor.name] = cb
        aps[name] = ct
    for Rx in (R1, R2):
        for off in (0, (S + 128) * D):
            pr.emit("q", lambda e, Rx=Rx, off=off: e.dma_start(
                bass.AP(Rx.tensor, off, [[D, 128], [1, D]]), zrow[:, :]),
                rd=[zrowB], dma=True)

    R1B, R2B, XZB, YGB = Buf(), Buf(), Buf(), Buf()

    def d2d_xhalf(src_ap, dst, dstB, ev):
        C4 = T // 4
        toks = []
        pr.sw_begin("q")
        for c4 in range(4):
            def fnv(e, cc, c4=c4):
                off = (128 + (cc % 2) * HALF + c4 * C4) * D
                return e.dma_start(
                    bass.AP(dst.tensor, off, [[D, C4], [1, D]]),
                    bass.AP(src_ap.tensor, src_ap.offset + c4 * C4 * D,
                            [[D, C4], [1, D]]))
            toks.append(pr.emit_sw("q", fnv, dma=True))
        pr.sw_end()
        for tok in toks:
            dstB.r.append(tok)
        pr.barrier_event(ev, FLG, evrow, evrowB, toks)

    pr.mark("S1_l0a")
    pr.push_scope()
    st.alloc_fm()
    st.alloc_ffn(8192, 4096)
    st.arp_u = pr.sb("arp_u", [128, 1024], BF16); st.arp_uB = Buf()
    st.arp_v = pr.sb("arp_v", [128, 1024], BF16); st.arp_vB = Buf()
    a0 = dict(aps)
    a0["xfull"] = xfull0.ap()
    st.xin, st.xout = xin.ap(), xA
    st.ln_in()
    emit_full_attn(st, a0, "l0")
    pr.pop_scope()
    pr.mark("S2_l0f")
    pr.push_scope()
    st.alloc_fm()
    st.alloc_ffn(8192, 4096)
    st.xin, st.xout = xA, xB
    st.ln_in()
    emit_l0f(st, aps)
    pr.pop_scope()
    d2d_xhalf(xB, R1, R1B, 0)
    pr.mark("S3_l1a")
    pr.push_scope()
    st.alloc_fm()
    st.alloc_ffn(8192, 4096)
    st.xin, st.xout = xB, xA
    pr.poll_event(0, FLG, fb, fbB)
    pr.set_fence()
    st.ln_in()
    a1 = dict(aps)
    a1["xfull_sw"] = R1
    emit_l1a(st, a1)
    pr.pop_scope()
    pr.mark("S4_l1f")
    pr.push_scope()
    st.alloc_fm(y=False)
    acc = pr.sb("m_acc", [128, MEGA_NT * D], F32)
    accB = [Buf(f"macc{i}") for i in range(MEGA_NT)]
    st.xin, st.xout = xA, xB
    st.ln_in(keep_tm=(acc, accB))
    emit_l1f(st, aps, acc, accB)
    st.ln_out_add_tm(acc, accB)
    pr.pop_scope()
    pr.mark("S5_l2front")
    pr.push_scope()
    st.alloc_fm(y=False)
    st.alloc_ffn(8192, 4096)
    st.xin = xB
    st.ln_in()
    xzlB = Buf()
    ffn_layer(st, aps["L2_attn_in_projT"], 8, 32, ("sbuf", st.xln_fm, st.xlnfB),
              ("dram", xz_local, xzlB), None, OBG=8)
    toks = []
    pr.sw_begin("q")
    for c4 in range(4):
        def fnv(e, cc, c4=c4):
            return e.dma_start(
                bass.AP(XZ.tensor, (cc % 2) * HALF + c4 * 1024 * S,
                        [[S, 1024], [1, T]]),
                bass.AP(xz_local.tensor, c4 * 1024 * T, [[T, 1024], [1, T]]))
        toks.append(pr.emit_sw("q", fnv, rdall=[xzlB], dma=True))
    pr.sw_end()
    for tok in toks:
        XZB.r.append(tok)
    pr.barrier_event(1, FLG, evrow, evrowB, toks)
    pr.pop_scope()
    pr.mark("S6_l2scan")
    pr.push_scope()
    pr.poll_event(1, FLG, fb, fbB)
    pr.set_fence()
    yglB = Buf()
    emit_l2scan_fused(st, aps, XZ, XZB, yg_local, yglB)
    toks = []
    pr.sw_begin("q")
    for c4 in range(4):
        def fnv(e, cc, c4=c4):
            off = (cc % 2) * 1024 * S + c4 * 256 * S
            return e.dma_start(bass.AP(YG.tensor, off, [[S, 256], [1, S]]),
                               bass.AP(yg_local.tensor, c4 * 256 * S,
                                       [[S, 256], [1, S]]))
        toks.append(pr.emit_sw("q", fnv, rdall=[yglB], dma=True))
    pr.sw_end()
    for tok in toks:
        YGB.r.append(tok)
    pr.barrier_event(2, FLG, evrow, evrowB, toks)
    pr.pop_scope()
    pr.mark("S7_l2out")
    pr.push_scope()
    st.alloc_fm(xln=False)
    st.alloc_ffn(8192, 8192)
    st.xin, st.xout = xB, xA
    pr.poll_event(2, FLG, fb, fbB)
    pr.set_fence()
    ffn_layer(st, aps["L2_attn_out_projT"], 16, 8, ("dram_sw", YG, YGB, S),
              ("sbuf", st.yfm, st.yfB), None, OBG=4)
    st.ln_out_add_fm(st.yfm, st.yfB)
    pr.pop_scope()
    pr.mark("S8_l2f")
    pr.push_scope()
    st.alloc_fm()
    st.alloc_ffn(16384, 16384)
    st.alloc_src2()
    st.xin, st.xout = xA, xB
    st.ln_in()
    emit_l2f(st, aps)
    pr.pop_scope()
    d2d_xhalf(xB, R2, R2B, 3)
    pr.mark("S9_l3a")
    pr.push_scope()
    st.alloc_fm()
    st.alloc_ffn(4096, 4096)
    st.xin, st.xout = xB, xA
    pr.poll_event(3, FLG, fb, fbB)
    pr.set_fence()
    st.ln_in()
    a3 = dict(aps)
    a3["xfull"] = R1
    a3["xfull"] = bass.AP(R2.tensor, 128 * D, [[D, S], [1, D]])
    emit_full_attn(st, a3, "l3")
    pr.pop_scope()
    pr.mark("S10_l3f")
    pr.push_scope()
    st.alloc_fm()
    st.alloc_ffn(16384, 16384)
    st.alloc_tern()
    st.xin, st.xout = xA, xout.ap()
    st.ln_in()
    emit_l3f(st, aps)
    pr.pop_scope()
    toks = [(sm, v) for sm, v in pr.counts.items() if v > 0]
    pr.emit("g", lambda e: e.engine_nop(), extra=toks)
    pr.build_block()
    pr.close()
    return nc


def kernel_fused(**inputs):
    import ml_dtypes
    I = {k: np.asarray(v) for k, v in inputs.items()}
    x = np.asarray(I["L0_attn_x"], np.float32)
    ident = np.eye(128, dtype=np.float32)
    nc = _prog("mega", build_megakernel)
    w = {}
    w.update(host_l0a_weights(I))
    w.update(host_l0f_weights(I))
    w.update(host_l1a_weights(I))
    w.update(host_l1f_weights(I))
    w["L2_attn_in_projT"] = np.ascontiguousarray(I["L2_attn_in_proj_w"].T)
    w["L2_attn_out_projT"] = np.ascontiguousarray(I["L2_attn_out_proj_w"].T)
    w.update(host_l2f_weights(I))
    w.update(host_l3a_weights(I))
    w.update(host_l3f_weights(I))
    in_maps = []
    for c in range(8):
        b, h = c // 2, c % 2
        m = {"xin": np.ascontiguousarray(x[b, h * HALF:(h + 1) * HALF]),
             "xfull0": np.ascontiguousarray(x[b]), "ident": ident}
        m.update(host_l0a_percore(h * HALF))
        m.update({"l1_kvzero": host_l1a_percore(x[b], h * HALF)["l1_kvzero"]})
        m.update({k: v for k, v in host_l2scan_percore(
            np.zeros((4096, 1), ml_dtypes.bfloat16), h, I).items()
            if k not in ("xcT", "zT")})
        bcsel = np.zeros((32, 4096), ml_dtypes.bfloat16)
        for n_ in range(32):
            bcsel[n_, n_ * 128:(n_ + 1) * 128] = 1.0
        m["l2_bcsel"] = bcsel
        m.update(host_l3a_percore(h * HALF))
        m.update(w)
        in_maps.append(m)
    res = _run(nc, in_maps)
    out = np.empty((B, S, D), np.float32)
    for c in range(8):
        b, h = c // 2, c % 2
        out[b, h * HALF:(h + 1) * HALF] = res.results[c]["xout"]
    return out


kernel = kernel_fused


# version 12 (final)
# speedup vs baseline: 1.6091x; 1.6091x over previous
import numpy as np
import ml_dtypes
from contextlib import ExitStack

import concourse.bass as bass
import concourse.mybir as mybir
from concourse.bass_utils import run_bass_kernel_spmd

F32 = mybir.dt.float32
BF16 = mybir.dt.bfloat16
U32 = mybir.dt.uint32
AF = mybir.ActivationFunctionType
ALU = mybir.AluOpType
AX = mybir.AxisListType

D = 1024
NCH = 8
B, S = 4, 4096
HALF = 2048
MAGIC = 0x5AFE0000

ENG2ATTR = {"t": "tensor", "v": "vector", "s": "scalar", "g": "gpsimd", "q": "sync"}
DMA_LANES = {"q": 6, "g": 4, "s": 2, "v": 2}
RR_LANES = {"g": 2}


class Buf:
    __slots__ = ("name", "w", "r")

    def __init__(self, name=""):
        self.name = name
        self.w = None
        self.r = []


class Entry:
    __slots__ = ("eng", "waits", "fn", "token", "inc")

    def __init__(self, eng, waits, fn, token, inc):
        self.eng = eng
        self.waits = waits
        self.fn = fn
        self.token = token
        self.inc = inc


class PR:

    def __init__(self, nc):
        self.nc = nc
        self.es = ExitStack()
        self.sem = {}
        self.counts = {}
        names = ["t", "v", "s", "g", "q", "pl"]
        for e, n in DMA_LANES.items():
            names += [f"d{e}{i}" for i in range(n)]
        for nm in names:
            self.sem[nm] = self.es.enter_context(nc.semaphore(nm))
            self.counts[nm] = 0
        self._lane_rr = {e: 0 for e in DMA_LANES}
        self.prog = []
        self.wm = {}
        self.pid = {}
        self._regs = {}
        self.fence = []

    def set_fence(self):
        self.fence = [(sm, v) for sm, v in self.counts.items() if v > 0]

    def sb(self, name, shape, dtype):
        es = self.scopes[-1] if getattr(self, "scopes", None) else self.es
        if not hasattr(self, "_names"):
            self._names = {}
        k = self._names.get(name, 0)
        self._names[name] = k + 1
        nm = name if k == 0 else f"{name}_{k}"
        return es.enter_context(self.nc.sbuf_tensor(nm, list(shape), dtype))

    def ps(self, name, shape, dtype=F32):
        return self.es.enter_context(self.nc.psum_tensor(name, list(shape), dtype))

    def push_scope(self):
        if not hasattr(self, "scopes"):
            self.scopes = []
        self.scopes.append(ExitStack())

    def pop_scope(self):
        self.scopes.pop().close()
        self.set_fence()

    def emit_sw(self, eng, fnv, rd=(), wr=(), extra=(), dma=True, rdall=(),
                lane=None):
        extra = list(extra)
        if dma:
            if lane is None:
                lane = self._lane_rr[eng]
                self._lane_rr[eng] = (lane + 1) % RR_LANES.get(eng, DMA_LANES[eng])
            sm = f"d{eng}{lane}"
            if self.counts[sm] > 0:
                extra.append((sm, self.counts[sm]))
        else:
            sm = eng
        waits = self._collect(eng, rd, wr, extra, rdall)
        amt = 16 if dma else 1
        self.counts[sm] += amt
        token = (sm, self.counts[sm])

        swb = getattr(self, "_swb", None)
        if swb is not None and swb[0] == eng:
            swb[2].append((waits, fnv, sm, amt))
        else:
            def fn(e, fnv=fnv, sm=sm, amt=amt):
                last = None
                for cc in e.Switch(self.pid[e], 8):
                    inst = fnv(e, cc)
                    inst.then_inc(self.sem[sm], amt)
                    last = inst
                return last
            self.prog.append(Entry(eng, waits, fn, None, 0))
        for b in rd:
            b.r.append(token)
        for b in wr:
            b.w = token
            b.r = []
        return token

    def sw_begin(self, eng):
        assert getattr(self, "_swb", None) is None
        self._swb = (eng, len(self.prog), [])

    def sw_end(self):
        eng, pos, items = self._swb
        self._swb = None
        for en in self.prog[pos:]:
            assert en.eng != eng, "same-engine entry emitted during sw batch"

        def fn(e, items=items):
            last = None
            for cc in e.Switch(self.pid[e], 8):
                for waits, fnv, sm, amt in items:
                    for s_, v_ in waits:
                        e.wait_ge(self.sem[s_], v_)
                    inst = fnv(e, cc)
                    inst.then_inc(self.sem[sm], amt)
                    last = inst
            return last
        self.prog.append(Entry(eng, [], fn, None, 0))

    def barrier_event(self, ev, flags_ap, evrow, evrowB, write_toks):
        def fnv(e, cc, ev=ev):
            return e.dma_start(
                bass.AP(flags_ap.tensor, ev * 256 + (cc % 2) * 128, [[1, 1], [1, 1]]),
                evrow[0:1, ev:ev + 1])
        tok = self.emit_sw("g", fnv, rd=[evrowB], extra=list(write_toks), dma=True)
        if not hasattr(self, "ev_toks"):
            self.ev_toks = {}
        self.ev_toks[ev] = tok

    def poll_event(self, ev, flags_ap, fb, fbB):
        target = MAGIC + ev + 1

        def fn(e, ev=ev, target=target):
            regs = self._poll_regs
            creg, mn, fi, itr, cnt, tmp = regs
            e.reg_mov(creg, 1)
            e.reg_mov(itr, 0)
            with e.While(creg):
                lane = f"dg{0}"
                e.dma_start(fb[0:1, 0:256],
                            bass.AP(flags_ap.tensor, ev * 256, [[1, 1], [1, 256]])
                            ).then_inc(self.sem["pl"], 16)
                e.reg_add(cnt, cnt, 16)
                e.wait_ge(self.sem["pl"], cnt)
                e.reg_load(mn, fb[0:1, 0:1])
                e.reg_load(fi, fb[0:1, 128:129])
                e.reg_alu(mn, mn, target, ALU.is_equal)
                e.reg_alu(fi, fi, target, ALU.is_equal)
                e.reg_alu(mn, mn, fi, ALU.mult)
                e.reg_add(itr, itr, 1)
                e.reg_alu(creg, mn, 1, ALU.is_lt)
                e.reg_alu(tmp, itr, 2000000, ALU.is_lt)
                e.reg_alu(creg, creg, tmp, ALU.mult)
            return e.engine_nop()
        extra = []
        if hasattr(self, "ev_toks") and ev in self.ev_toks:
            extra.append(self.ev_toks[ev])
        tok = self.emit("g", fn, rd=[fbB], wr=[fbB], extra=extra)
        return tok

    def _collect(self, eng, rd, wr, extra, rdall=()):
        deps = {}
        def add(tok):
            if tok is None:
                return
            sm, val = tok
            if deps.get(sm, -1) < val:
                deps[sm] = val
        for b in rd:
            add(b.w)
        for b in rdall:
            add(b.w)
            for t in b.r:
                add(t)
        for b in wr:
            add(b.w)
            for t in b.r:
                add(t)
        for t in extra:
            add(t)
        for t in self.fence:
            add(t)
        waits = []
        for sm, val in deps.items():
            key = (eng, sm)
            if self.wm.get(key, 0) >= val:
                continue
            self.wm[key] = val
            waits.append((sm, val))
        return waits

    def emit(self, eng, fn, rd=(), wr=(), extra=(), dma=False, inc=True, rdall=(),
             lane=None):
        extra = list(extra)
        if dma:
            if lane is None:
                lane = self._lane_rr[eng]
                self._lane_rr[eng] = (lane + 1) % RR_LANES.get(eng, DMA_LANES[eng])
            sm = f"d{eng}{lane}"
            if self.counts[sm] > 0:
                extra.append((sm, self.counts[sm]))
        waits = self._collect(eng, rd, wr, extra, rdall)
        if inc:
            if not dma:
                sm = eng
            amt = 16 if dma else 1
            self.counts[sm] += amt
            token = (sm, self.counts[sm])
        else:
            token, amt = None, 0
        swb = getattr(self, "_swb", None)
        if dma and inc and swb is not None and swb[0] == eng:
            swb[2].append((waits, (lambda e, cc, fn=fn: fn(e)), sm, amt))
        else:
            self.prog.append(Entry(eng, waits, fn, token, amt))
        if token is not None:
            for b in rd:
                b.r.append(token)
            for b in wr:
                b.w = token
                b.r = []
        return token

    def group(self, eng, fns, rd=(), wr=(), extra=(), dma=False):
        assert not dma
        waits = self._collect(eng, rd, wr, extra)
        for fn in fns[:-1]:
            self.prog.append(Entry(eng, waits, fn, None, 0))
            waits = []
        sm = eng
        amt = 1
        self.counts[sm] += amt
        token = (sm, self.counts[sm])
        self.prog.append(Entry(eng, waits, fns[-1], token, amt))
        for b in rd:
            b.r.append(token)
        for b in wr:
            b.w = token
            b.r = []
        return token

    def mark(self, name):
        if not hasattr(self, "marks"):
            self.marks = []
        self.marks.append((len(self.prog), name))

    def build_block(self):
        nc = self.nc
        used = {e.eng for e in self.prog}
        marks = getattr(self, "marks", [])
        with nc.Block() as block:
            for ename, attr in ENG2ATTR.items():
                if ename not in used:
                    continue

                def mk(ename):
                    def body(eng):
                        self.cur = eng
                        self.pid[eng] = eng.partition_id()
                        if ename == "g" and getattr(self, "need_polls", False):
                            self._poll_reg_ctx = [eng.register(f"plr{i}")
                                                  for i in range(6)]
                            self._poll_regs = [c.__enter__()
                                               for c in self._poll_reg_ctx]
                            cnt0 = self._poll_regs[4]
                            eng.reg_mov(cnt0, 0)
                        mi = 0
                        open_sc = None
                        for idx, e in enumerate(self.prog):
                            while mi < len(marks) and marks[mi][0] <= idx:
                                if open_sc is not None:
                                    nc.leave_named_scope(open_sc[0], open_sc[1],
                                                         False)
                                sid, _ = nc.enter_named_scope(marks[mi][1], False)
                                open_sc = (marks[mi][1], sid)
                                mi += 1
                            if e.eng != ename:
                                continue
                            for sm, val in e.waits:
                                eng.wait_ge(self.sem[sm], val)
                            inst = e.fn(eng)
                            if e.token is not None:
                                inst.then_inc(self.sem[e.token[0]], e.inc)
                        if open_sc is not None:
                            nc.leave_named_scope(open_sc[0], open_sc[1], False)
                    return body

                getattr(block, attr)(mk(ename))

    def close(self):
        self.es.close()



class Stage:
    def __init__(self, pr, NT, xin=None, xout=None, arenas=(), wg_cols=16384,
                 slab_cols=16384, fm_scoped=False):
        self.pr = pr
        self.NT = NT
        self.T = NT * 128
        self.xin = xin
        self.xout = xout
        T = self.T
        self.scr = pr.sb("ln_scr", [128, D], F32)
        self.scrB = Buf("scr")
        self.stat = pr.sb("ln_stat", [128, 8], F32)
        self.statB = Buf("stat")
        self.xt = pr.sb("x_tile", [128, 2 * D], F32)
        self.xtB = [Buf("xt0"), Buf("xt1")]
        self.ytm = pr.sb("y_tm", [128, 2 * D], BF16)
        self.ytmB = [Buf("ytm0"), Buf("ytm1")]
        if not fm_scoped:
            self.alloc_fm()
        self.pb = [pr.ps(f"pb{i}", [128, 512], F32) for i in range(6)]
        self.pbB = [Buf(f"pb{i}") for i in range(6)]
        self.ptb = pr.ps("ptb", [128, 1024], BF16)
        self.ptbB = Buf("ptb")
        self._rot = 0
        self.identf = pr.sb("identf", [128, 128], F32)
        self.identfB = Buf("identf")
        self.ident = pr.sb("ident_bf", [128, 128], BF16)
        self.identB = Buf("identB")
        self.epsc = pr.sb("epsc", [128, 2], F32)
        self.epscB = Buf("epsc")
        if "ffn" in arenas:
            self.alloc_ffn(wg_cols, slab_cols)
        if "src2" in arenas:
            self.alloc_src2()
        if "tern" in arenas:
            self.alloc_tern()

    def alloc_fm(self, xln=True, y=True):
        pr, NT, T = self.pr, self.NT, self.T
        if xln:
            self.xln_fm = pr.sb("xln_fm", [128, NCH * T], BF16)
            self.xlnfB = [Buf(f"xlnf{i}") for i in range(NT)]
        if y:
            self.yfm = pr.sb("y_fm", [128, NCH * T], BF16)
            self.yfB = [Buf(f"yf{i}") for i in range(NT)]

    def alloc_ffn(self, wg_cols=16384, slab_cols=16384):
        pr = self.pr
        self.wg = pr.sb("wg_arena", [128, wg_cols], BF16)
        self.wgB = Buf("wg")
        self.slab = pr.sb("slab_arena", [128, slab_cols], BF16)
        self.slabB = Buf("slab")
        self.dstg = pr.sb("dstg", [128, 2 * 512], BF16)
        self.dstgB = [Buf(f"dstg{i}") for i in range(2)]
        self._dstg_rot = 0

    def alloc_src2(self):
        pr = self.pr
        self.s2t = pr.sb("s2t", [128, 2 * 512], BF16)
        self.s2tB = [Buf("s2t0"), Buf("s2t1")]

    def alloc_tern(self):
        pr = self.pr
        self.wgf = pr.sb("wgf_arena", [128, 4096], F32)
        self.wgfB = Buf("wgf")
        self.tscr = pr.sb("tscr", [128, 4096], BF16)
        self.tscrB = Buf("tscr")
        self.tscr2 = pr.sb("tscr2", [128, 4096], BF16)
        self.tscr2B = Buf("tscr2")

    def bank(self):
        i = self._rot % 6
        self._rot += 1
        return self.pb[i], self.pbB[i]

    def bank4(self):
        i = self._rot % 4
        self._rot += 1
        return self.pb[i], self.pbB[i]

    def load_consts(self, ident_ap):
        pr = self.pr
        pr.emit("q", lambda e: e.dma_start(self.identf[:, :], ident_ap),
                wr=[self.identfB], dma=True)
        pr.emit("v", lambda e: e.tensor_copy(self.ident[:, :], self.identf[:, :]),
                rd=[self.identfB], wr=[self.identB])
        pr.emit("v", lambda e: e.memset(self.epsc[:, :], 1e-5), wr=[self.epscB])

    def _ln_stats(self, tile_ap, rdB):
        pr = self.pr
        st = self.stat
        pr.emit("v", lambda e: e.tensor_scalar(
            self.scr[:, :], tile_ap, 1.0, 0.0, ALU.mult, ALU.add,
            accum_out=st[:, 0:1]), rd=rdB, wr=[self.scrB, self.statB])
        pr.emit("v", lambda e: e.scalar_tensor_tensor(
            self.scr[:, :], tile_ap, 1.0, tile_ap, ALU.mult, ALU.mult,
            accum_out=st[:, 1:2]), rd=rdB, wr=[self.scrB, self.statB])
        pr.emit("v", lambda e: e.tensor_scalar(st[:, 2:3], st[:, 0:1], 1.0 / D,
                                               None, ALU.mult),
                rd=[self.statB], wr=[self.statB])
        pr.emit("v", lambda e: e.tensor_tensor(st[:, 3:4], st[:, 2:3], st[:, 2:3],
                                               ALU.mult), rd=[self.statB], wr=[self.statB])
        pr.emit("v", lambda e: e.tensor_scalar(st[:, 4:5], st[:, 1:2], 1.0 / D,
                                               st[:, 3:4], ALU.mult, ALU.subtract),
                rd=[self.statB], wr=[self.statB])
        pr.emit("s", lambda e: e.activation(st[:, 5:6], st[:, 4:5], AF.Sqrt,
                                            bias=self.epsc[:, 0:1], scale=1.0),
                rd=[self.statB, self.epscB], wr=[self.statB])
        pr.emit("v", lambda e: e.reciprocal(st[:, 6:7], st[:, 5:6]),
                rd=[self.statB], wr=[self.statB])

    def ln_in(self, tiles=None, zero_tiles=(), keep_tm=None):
        pr = self.pr
        tiles = range(self.NT) if tiles is None else tiles
        for i in tiles:
            r = i % 2
            xt = self.xt[:, r * D:(r + 1) * D]
            if i in zero_tiles:
                pr.emit("v", lambda e, xt=xt: e.memset(xt, 0.0), wr=[self.xtB[r]])
            else:
                pr.emit("q", lambda e, i=i, xt=xt, xin=self.xin: e.dma_start(
                    xt, xin[i * 128:(i + 1) * 128, :]),
                    wr=[self.xtB[r]], dma=True)
            self._ln_stats(xt, [self.xtB[r]])
            if keep_tm is None:
                dst_t, dst_B, slot = self.ytm, self.ytmB, r
            else:
                dst_t, dst_B, slot = keep_tm[0], keep_tm[1], i
            yt = dst_t[:, slot * D:(slot + 1) * D]
            pr.emit("v", lambda e, xt=xt, yt=yt: e.tensor_scalar(
                yt, xt, self.stat[:, 2:3], self.stat[:, 6:7],
                ALU.subtract, ALU.mult),
                rd=[self.xtB[r], self.statB], wr=[dst_B[slot]])
            self.tm_to_fm(dst_t, dst_B, slot, self.xln_fm, self.xlnfB, i,
                          f32=keep_tm is not None)

    def tm_to_fm(self, tm, tmB, r, fm, fmB, i, f32=False, fm_stride=None):
        pr = self.pr
        T = self.T if fm_stride is None else fm_stride
        for j0 in range(0, NCH, 4):
            fns = []
            if f32:
                bank, bankB = self.bank()
            for j in range(j0, j0 + 4):
                if f32:
                    def fn(e, r=r, j=j, bank=bank):
                        return e.transpose(
                            bank[:, (j % 4) * 128:(j % 4) * 128 + 128],
                            tm[:, r * D + j * 128:r * D + (j + 1) * 128],
                            self.identf[:, :])
                else:
                    def fn(e, r=r, j=j):
                        return e.transpose(
                            self.ptb[:, (j % 4) * 256:(j % 4) * 256 + 128],
                            tm[:, r * D + j * 128:r * D + (j + 1) * 128],
                            self.ident[:, :])
                fns.append(fn)
            if f32:
                pr.group("t", fns, rd=[tmB[r], self.identfB], wr=[bankB])
                for j in range(j0, j0 + 4):
                    def fn(e, i=i, j=j, bank=bank):
                        return e.copy(fm[:, j * T + i * 128:j * T + (i + 1) * 128],
                                      bank[:, (j % 4) * 128:(j % 4) * 128 + 128])
                    pr.emit("s", fn, rd=[bankB], wr=[fmB[i]])
            else:
                pr.group("t", fns, rd=[tmB[r], self.identB], wr=[self.ptbB])
                for j in range(j0, j0 + 4):
                    def fn(e, i=i, j=j):
                        return e.copy(fm[:, j * T + i * 128:j * T + (i + 1) * 128],
                                      self.ptb[:, (j % 4) * 256:(j % 4) * 256 + 128])
                    pr.emit("s", fn, rd=[self.ptbB], wr=[fmB[i]])

    def ln_out_add_fm(self, y_fm, yfB, tiles=None, xout=None):
        pr = self.pr
        T = self.T
        xout = self.xout if xout is None else xout
        tiles = range(self.NT) if tiles is None else tiles
        toks = []
        for i in tiles:
            r = i % 2
            for j0 in range(0, NCH, 4):
                fns = []
                for j in range(j0, j0 + 4):
                    def fn(e, i=i, j=j):
                        return e.transpose(
                            self.ptb[:, (j % 4) * 256:(j % 4) * 256 + 128],
                            y_fm[:, j * T + i * 128:j * T + (i + 1) * 128],
                            self.ident[:, :])
                    fns.append(fn)
                pr.group("t", fns,
                         rd=[yfB[i] if isinstance(yfB, list) else yfB, self.identB],
                         wr=[self.ptbB])
                for j in range(j0, j0 + 4):
                    def fn(e, r=r, j=j):
                        return e.copy(
                            self.ytm[:, r * D + j * 128:r * D + (j + 1) * 128],
                            self.ptb[:, (j % 4) * 256:(j % 4) * 256 + 128])
                    pr.emit("s", fn, rd=[self.ptbB], wr=[self.ytmB[r]])
            yt = self.ytm[:, r * D:(r + 1) * D]
            self._ln_stats(yt, [self.ytmB[r]])
            pr.emit("v", lambda e, yt=yt: e.tensor_scalar(
                self.scr[:, :], yt, self.stat[:, 2:3], self.stat[:, 6:7],
                ALU.subtract, ALU.mult),
                rd=[self.ytmB[r], self.statB], wr=[self.scrB])
            xt = self.xt[:, r * D:(r + 1) * D]
            pr.emit("q", lambda e, i=i, xt=xt, xin=self.xin: e.dma_start(
                xt, xin[i * 128:(i + 1) * 128, :]),
                wr=[self.xtB[r]], dma=True)
            pr.emit("v", lambda e, xt=xt: e.tensor_tensor(
                xt, xt, self.scr[:, :], ALU.add),
                rd=[self.scrB, self.xtB[r]], wr=[self.xtB[r]])
            toks.append(pr.emit("q", lambda e, i=i, xt=xt: e.dma_start(
                xout[i * 128:(i + 1) * 128, :], xt),
                rd=[self.xtB[r]], dma=True))
        return toks

    def ln_out_add_tm(self, src_t, srcB, tiles=None, xout=None):
        pr = self.pr
        xout = self.xout if xout is None else xout
        tiles = range(self.NT) if tiles is None else tiles
        toks = []
        for i in tiles:
            r = i % 2
            yt = src_t[:, i * D:(i + 1) * D]
            self._ln_stats(yt, [srcB[i]])
            pr.emit("v", lambda e, yt=yt: e.tensor_scalar(
                self.scr[:, :], yt, self.stat[:, 2:3], self.stat[:, 6:7],
                ALU.subtract, ALU.mult),
                rd=[srcB[i], self.statB], wr=[self.scrB])
            xt = self.xt[:, r * D:(r + 1) * D]
            pr.emit("q", lambda e, i=i, xt=xt, xin=self.xin: e.dma_start(
                xt, xin[i * 128:(i + 1) * 128, :]),
                wr=[self.xtB[r]], dma=True)
            pr.emit("v", lambda e, xt=xt: e.tensor_tensor(
                xt, xt, self.scr[:, :], ALU.add),
                rd=[self.scrB, self.xtB[r]], wr=[self.xtB[r]])
            toks.append(pr.emit("q", lambda e, i=i, xt=xt: e.dma_start(
                xout[i * 128:(i + 1) * 128, :], xt),
                rd=[self.xtB[r]], dma=True))
        return toks

    def ffn_evac(self, dst, act, scale_sb, bias_sb, ob, tb, bank, bankB, src2_mul):
        pr = self.pr
        T = self.T
        kw = {}
        sc = 1.0 if scale_sb is None else scale_sb[0][:, ob:ob + 1]
        rdeps = [bankB] + ([scale_sb[1]] if scale_sb else []) + \
                ([bias_sb[1]] if bias_sb else [])
        func = act if act is not None else AF.Identity
        if dst[0] == "sbuf":
            out_ap = dst[1][:, ob * T + tb * 512:ob * T + (tb + 1) * 512]
            outB = [dst[2][t] for t in range(tb * 4, tb * 4 + 4)]
        else:
            s = self._dstg_rot % 2
            self._dstg_rot += 1
            out_ap = self.dstg[:, s * 512:(s + 1) * 512]
            outB = [self.dstgB[s]]
        if bias_sb is not None:
            def fn(e, out_ap=out_ap, bank=bank, sc=sc):
                return e.activation(out_ap, bank[:, 0:512], func,
                                    bias=bias_sb[0][:, ob:ob + 1], scale=sc)
        else:
            def fn(e, out_ap=out_ap, bank=bank, sc=sc):
                return e.activation(out_ap, bank[:, 0:512], func, scale=sc)
        pr.emit("s", fn, rd=rdeps, wr=outB)
        if src2_mul is not None:
            sh, shB = src2_mul[1], src2_mul[2]
            r2 = (ob * 37 + tb) % 2
            s2 = self.s2t[:, r2 * 512:(r2 + 1) * 512]
            pr.emit("q", lambda e, ob=ob, tb=tb, s2=s2: e.dma_start(
                s2, bass.AP(sh.tensor, sh.offset + ob * 128 * T + tb * 512,
                            [[T, 128], [1, 512]])),
                rdall=[shB], wr=[self.s2tB[r2]], dma=True)
            pr.emit("v", lambda e, out_ap=out_ap, s2=s2: e.tensor_tensor(
                out_ap, out_ap, s2, ALU.mult),
                rd=[self.s2tB[r2]] + outB, wr=outB)
        if dst[0] == "dram":
            dh, dhB = dst[1], dst[2]
            tok = pr.emit("q", lambda e, ob=ob, tb=tb, out_ap=out_ap: e.dma_start(
                bass.AP(dh.tensor, dh.offset + ob * 128 * T + tb * 512,
                        [[T, 128], [1, 512]]), out_ap),
                rd=outB, dma=True)
            dhB.r.append(tok)



def mm_dense(st, wsb, wB, xfm, xfB, T, ob_n, ki_n, evac, w_ki_stride=None, psN=512):
    pr = st.pr
    if w_ki_stride is None:
        w_ki_stride = ob_n * 128
    tb_n = T // psN
    for ob in range(ob_n):
        for tb in range(tb_n):
            bank, bankB = st.bank()
            fns = []
            for ki in range(ki_n):
                def fn(e, ob=ob, tb=tb, ki=ki, bank=bank):
                    return e.matmul(bank[:, 0:psN],
                                    wsb[:, ki * w_ki_stride + ob * 128:
                                        ki * w_ki_stride + (ob + 1) * 128],
                                    xfm[:, ki * T + tb * psN:ki * T + (tb + 1) * psN],
                                    start=(ki == 0), stop=(ki == ki_n - 1))
                fns.append(fn)
            rdb = [wB] + [xfB[t] for t in range(tb * psN // 128, (tb + 1) * psN // 128)]
            pr.group("t", fns, rd=rdb, wr=[bankB])
            evac(ob, tb, bank, bankB)


def emit_l0f(st, aps):
    pr = st.pr
    T = st.T
    wbf = pr.sb("l0f_wbf", [128, 8 * 1024], BF16)
    wbfB = Buf("l0f_wbf")
    lup = pr.sb("l0f_lup", [50, 4], BF16); lupB = Buf("lup")
    upxT = pr.sb("l0f_upxT", [50, 1024], BF16); upxTB = Buf("upxT")
    ldn = pr.sb("l0f_ldn", [4, 100], BF16); ldnB = Buf("ldn")
    dax = pr.sb("l0f_dax", [100, 1024], BF16); daxB = Buf("dax")
    u1 = pr.sb("l0f_u1", [4, 1024], BF16); u1B = Buf("u1")
    u2 = pr.sb("l0f_u2", [100, 1024], BF16); u2B = Buf("u2")
    bsb = pr.sb("l0f_b", [128, 8], F32); bsbB = Buf("l0f_b")
    yfm, yfB = st.yfm, st.yfB

    WT = aps["L0_ffn_WT"]
    weng, wdep = wsrc(st, WT)
    pr.emit(weng, lambda e: e.dma_start(
        wbf[:, :], bass.AP(WT.tensor, 0, [[1024, 128], [131072, 8], [1, 1024]])),
        wr=[wbfB], rdall=wdep, dma=True)
    pr.emit("g", lambda e: e.dma_start(lup[:, :], aps["L0_ffn_lora_up"]), wr=[lupB], dma=True)
    pr.emit("g", lambda e: e.dma_start(upxT[:, :], aps["L0_ffn_up_auxT"]), wr=[upxTB], dma=True)
    pr.emit("g", lambda e: e.dma_start(ldn[:, :], aps["L0_ffn_lora_down"]), wr=[ldnB], dma=True)
    pr.emit("g", lambda e: e.dma_start(dax[:, :], aps["L0_ffn_down_aux"]), wr=[daxB], dma=True)
    col_vec_load(pr, aps["L0_ffn_b"], bsb, bsbB, 8)

    bank, bankB = st.bank()
    pr.group("t", [lambda e, bank=bank: e.matmul(
        bank[0:4, 0:512], lup[:, :], upxT[:, 0:512], start=True, stop=True)],
        rd=[lupB, upxTB], wr=[bankB])
    bank2, bank2B = st.bank()
    pr.group("t", [lambda e, bank2=bank2: e.matmul(
        bank2[0:4, 0:512], lup[:, :], upxT[:, 512:1024], start=True, stop=True)],
        rd=[lupB, upxTB], wr=[bank2B])
    pr.emit("s", lambda e, bank=bank: e.copy(u1[0:4, 0:512], bank[0:4, 0:512]),
            rd=[bankB], wr=[u1B])
    pr.emit("s", lambda e, bank2=bank2: e.copy(u1[0:4, 512:1024], bank2[0:4, 0:512]),
            rd=[bank2B], wr=[u1B])
    for nb in range(2):
        bank, bankB = st.bank()
        pr.group("t", [lambda e, nb=nb, bank=bank: e.matmul(
            bank[0:100, 0:512], ldn[:, :], u1[0:4, nb * 512:(nb + 1) * 512],
            start=True, stop=True)], rd=[ldnB, u1B], wr=[bankB])
        pr.emit("s", lambda e, nb=nb, bank=bank: e.copy(
            u2[0:100, nb * 512:(nb + 1) * 512], bank[0:100, 0:512]),
            rd=[bankB], wr=[u2B])
    for mo in range(8):
        for nb in range(2):
            bank, bankB = st.bank()
            pr.group("t", [lambda e, mo=mo, nb=nb, bank=bank: e.matmul(
                bank[:, 0:512], dax[:, mo * 128:(mo + 1) * 128],
                u2[0:100, nb * 512:(nb + 1) * 512], start=True, stop=True)],
                rd=[daxB, u2B], wr=[bankB])
            def fn(e, mo=mo, nb=nb, bank=bank):
                return e.tensor_tensor(
                    wbf[:, mo * 1024 + nb * 512:mo * 1024 + (nb + 1) * 512],
                    wbf[:, mo * 1024 + nb * 512:mo * 1024 + (nb + 1) * 512],
                    bank[:, 0:512], ALU.add)
            pr.emit("v", fn, rd=[bankB, wbfB], wr=[wbfB])

    def evac(ob, tb, bank, bankB):
        def fn(e, ob=ob, tb=tb, bank=bank):
            return e.activation(
                yfm[:, ob * T + tb * 512:ob * T + (tb + 1) * 512],
                bank[:, 0:512], AF.Identity, bias=bsb[:, ob:ob + 1], scale=1.0)
        pr.emit("s", fn, rd=[bankB, bsbB],
                wr=[yfB[t] for t in range(tb * 4, tb * 4 + 4)])

    mm_dense(st, wbf, wbfB, st.xln_fm, st.xlnfB, T, 8, 8, evac, w_ki_stride=1024)
    st.ln_out_add_fm(yfm, yfB)



def mean_abs_scale(st, W_dram, total, name):
    pr = st.pr
    nchunk = total // (128 * 4096)
    acc = pr.sb(f"{name}_acc", [128, nchunk], F32); accB = Buf()
    thr = pr.sb(f"{name}_thr", [128, 2], F32); thrB = Buf()
    onesf = pr.sb(f"{name}_ones", [128, 1], F32); onesB = Buf()
    onesr = pr.sb(f"{name}_onesr", [1, 128], F32); onesrB = Buf()
    tot = pr.sb(f"{name}_tot", [1, nchunk], F32); totB = Buf()
    pr.emit("v", lambda e: e.memset(onesf[:, :], 1.0), wr=[onesB])
    pr.emit("v", lambda e: e.memset(onesr[:, :], 1.0), wr=[onesrB])
    for c in range(nchunk):
        pr.emit("q", lambda e, c=c: e.dma_start(
            st.wgf[:, :], bass.AP(W_dram.tensor, W_dram.offset + c * 128 * 4096,
                                  [[4096, 128], [1, 4096]])),
            wr=[st.wgfB], dma=True)
        pr.emit("v", lambda e, c=c: e.reduce_sum(
            acc[:, c:c + 1], st.wgf[:, :], axis=AX.X, apply_absolute_value=True),
            rd=[st.wgfB], wr=[accB])
    bank, bankB = st.bank()
    pr.group("t", [lambda e, bank=bank: e.matmul(
        bank[0:1, 0:nchunk], onesf[:, 0:1], acc[:, :], start=True, stop=True)],
        rd=[onesB, accB], wr=[bankB])
    pr.emit("s", lambda e, bank=bank: e.copy(tot[0:1, :], bank[0:1, 0:nchunk]),
            rd=[bankB], wr=[totB])
    pr.emit("v", lambda e: e.reduce_sum(tot[0:1, 0:1], tot[0:1, :], axis=AX.X),
            rd=[totB], wr=[totB])
    bank, bankB = st.bank()
    pr.group("t", [lambda e, bank=bank: e.matmul(
        bank[:, 0:1], onesr[0:1, :], tot[0:1, 0:1], start=True, stop=True)],
        rd=[onesrB, totB], wr=[bankB])
    pr.emit("s", lambda e, bank=bank: e.activation(
        thr[:, 0:1], bank[:, 0:1], AF.Identity, scale=0.5 / total), rd=[bankB], wr=[thrB])
    pr.emit("s", lambda e, bank=bank: e.activation(
        thr[:, 1:2], bank[:, 0:1], AF.Identity, scale=-0.5 / total), rd=[bankB], wr=[thrB])
    return thr, thrB


def emit_l3f(st, aps):
    pr = st.pr
    nc = pr.nc
    T = st.T
    b1 = pr.sb("l3f_b1", [128, 32], F32); b1B = Buf()
    b2 = pr.sb("l3f_b2", [128, 8], F32); b2B = Buf()
    col_vec_load(pr, aps["L3_ffn_b1"], b1, b1B, 32)
    col_vec_load(pr, aps["L3_ffn_b2"], b2, b2B, 8)
    thr1 = mean_abs_scale(st, aps["L3_ffn_W1T"], 1024 * 4096, "w1")
    thr2 = mean_abs_scale(st, aps["L3_ffn_W2T"], 4096 * 1024, "w2")
    h = nc.dram_tensor("l3f_h", [4096, T], BF16, kind="Internal")
    hB = Buf("l3f_h")
    ffn_layer(st, aps["L3_ffn_W1T"], 8, 32, ("sbuf", st.xln_fm, st.xlnfB),
              ("dram", h.ap(), hB), AF.Gelu, bias_sb=(b1, b1B), OBG=4,
              tern_thr=thr1)
    ffn_layer(st, aps["L3_ffn_W2T"], 32, 8, ("dram", h.ap(), hB),
              ("sbuf", st.yfm, st.yfB), None, bias_sb=(b2, b2B), OBG=4,
              tern_thr=thr2)
    st.ln_out_add_fm(st.yfm, st.yfB)


def emit_l2f(st, aps):
    pr = st.pr
    nc = pr.nc
    T = st.T
    s1 = pr.sb("l2f_s1", [128, 32], F32); s1B = Buf()
    s2 = pr.sb("l2f_s2", [128, 32], F32); s2B = Buf()
    h1s = pr.sb("l2f_h1s", [128, 32], F32); h1sB = Buf()
    h2s = pr.sb("l2f_h2s", [128, 32], F32); h2sB = Buf()
    s3 = pr.sb("l2f_s3", [128, 8], F32); s3B = Buf()
    col_vec_load(pr, aps["L2_ffn_w1_s"], s1, s1B, 32)
    col_vec_load(pr, aps["L2_ffn_w2_s"], s2, s2B, 32)
    col_vec_load(pr, aps["L2_ffn_h1_s"], h1s, h1sB, 32)
    col_vec_load(pr, aps["L2_ffn_h2_s"], h2s, h2sB, 32)
    col_vec_load(pr, aps["L2_ffn_w3_s"], s3, s3B, 8)
    hA = nc.dram_tensor("l2f_hA", [4096, T], BF16, kind="Internal"); hAB = Buf()
    hBt = nc.dram_tensor("l2f_hB", [4096, T], BF16, kind="Internal"); hBB = Buf()
    hC = nc.dram_tensor("l2f_hC", [4096, T], BF16, kind="Internal"); hCB = Buf()
    ffn_layer(st, aps["L2_ffn_w1_wT"], 8, 32, ("sbuf", st.xln_fm, st.xlnfB),
              ("dram", hA.ap(), hAB), AF.Silu, scale_sb=(s1, s1B), OBG=8)
    ffn_layer(st, aps["L2_ffn_w2_wT"], 8, 32, ("sbuf", st.xln_fm, st.xlnfB),
              ("dram", hBt.ap(), hBB), None, scale_sb=(s2, s2B), OBG=8,
              src2_mul=("dram", hA.ap(), hAB))
    ffn_layer(st, aps["L2_ffn_h1_wT"], 32, 32, ("dram", hBt.ap(), hBB),
              ("dram", hC.ap(), hCB), AF.Gelu, scale_sb=(h1s, h1sB), OBG=4)
    hA2B = Buf()
    ffn_layer(st, aps["L2_ffn_h2_wT"], 32, 32, ("dram", hC.ap(), hCB),
              ("dram", hA.ap(), hA2B), AF.Gelu, scale_sb=(h2s, h2sB), OBG=4)
    ffn_layer(st, aps["L2_ffn_w3_wT"], 32, 8, ("dram", hA.ap(), hA2B),
              ("sbuf", st.yfm, st.yfB), None, scale_sb=(s3, s3B), OBG=4)
    st.ln_out_add_fm(st.yfm, st.yfB)



def build_stage_tl(emitter, wspecs, NT=16, arenas=("ffn", "tern", "src2")):
    nc = bass.Bass("TRN2", target_bir_lowering=False)
    T = NT * 128
    xin = nc.dram_tensor("xin", [T, D], F32, kind="ExternalInput")
    ident = nc.dram_tensor("ident", [128, 128], F32, kind="ExternalInput")
    xout = nc.dram_tensor("xout", [T, D], F32, kind="ExternalOutput")
    aps = {"xin": xin.ap(), "ident": ident.ap(), "xout": xout.ap()}
    for name, shape in wspecs.items():
        aps[name] = nc.dram_tensor(name, list(shape), F32, kind="ExternalInput").ap()
    pr = PR(nc)
    st = Stage(pr, NT, xin=aps["xin"], xout=aps["xout"], arenas=arenas)
    st.load_consts(aps["ident"])
    st.ln_in()
    emitter(st, aps)
    toks = [(sm, v) for sm, v in pr.counts.items() if v > 0]
    pr.emit("g", lambda e: e.engine_nop(), extra=toks)
    pr.build_block()
    pr.close()
    return nc


L0F_WSPECS = {
    "L0_ffn_WT": (1024, 1024), "L0_ffn_b": (1024,),
    "L0_ffn_lora_up": (50, 4), "L0_ffn_lora_down": (4, 100),
    "L0_ffn_down_aux": (100, 1024), "L0_ffn_up_auxT": (50, 1024),
}

L3F_WSPECS = {
    "L3_ffn_W1T": (1024, 4096), "L3_ffn_b1": (4096,),
    "L3_ffn_W2T": (4096, 1024), "L3_ffn_b2": (1024,),
}

L2F_WSPECS = {
    "L2_ffn_w1_wT": (1024, 4096), "L2_ffn_w1_s": (4096,),
    "L2_ffn_w2_wT": (1024, 4096), "L2_ffn_w2_s": (4096,),
    "L2_ffn_h1_wT": (4096, 4096), "L2_ffn_h1_s": (4096,),
    "L2_ffn_h2_wT": (4096, 4096), "L2_ffn_h2_s": (4096,),
    "L2_ffn_w3_wT": (4096, 1024), "L2_ffn_w3_s": (1024,),
}


def host_l0f_weights(I):
    return {
        "L0_ffn_WT": np.ascontiguousarray(I["L0_ffn_W"].T),
        "L0_ffn_b": I["L0_ffn_b"],
        "L0_ffn_lora_up": I["L0_ffn_lora_up"],
        "L0_ffn_lora_down": I["L0_ffn_lora_down"],
        "L0_ffn_down_aux": I["L0_ffn_down_aux"],
        "L0_ffn_up_auxT": np.ascontiguousarray(I["L0_ffn_up_aux"].T),
    }


def host_l3f_weights(I):
    return {
        "L3_ffn_W1T": np.ascontiguousarray(I["L3_ffn_W1"].T),
        "L3_ffn_b1": I["L3_ffn_b1"],
        "L3_ffn_W2T": np.ascontiguousarray(I["L3_ffn_W2"].T),
        "L3_ffn_b2": I["L3_ffn_b2"],
    }


def host_l2f_weights(I):
    return {
        "L2_ffn_w1_wT": np.ascontiguousarray(I["L2_ffn_w1_w"].T),
        "L2_ffn_w1_s": I["L2_ffn_w1_s"],
        "L2_ffn_w2_wT": np.ascontiguousarray(I["L2_ffn_w2_w"].T),
        "L2_ffn_w2_s": I["L2_ffn_w2_s"],
        "L2_ffn_h1_wT": np.ascontiguousarray(I["L2_ffn_h1_w"].T),
        "L2_ffn_h1_s": I["L2_ffn_h1_s"],
        "L2_ffn_h2_wT": np.ascontiguousarray(I["L2_ffn_h2_w"].T),
        "L2_ffn_h2_s": I["L2_ffn_h2_s"],
        "L2_ffn_w3_wT": np.ascontiguousarray(I["L2_ffn_w3_w"].T),
        "L2_ffn_w3_s": I["L2_ffn_w3_s"],
    }


def col_vec_load(pr, dram_ap, sb_tile, buf, n, eng="q"):
    for j in range(n):
        pr.emit(eng, lambda e, j=j: e.dma_start(
            sb_tile[:, j:j + 1], bass.AP(dram_ap.tensor, dram_ap.offset + j * 128,
                                         [[1, 128], [1, 1]])),
            wr=[buf], dma=True)


def wsrc(st, ap):
    if ap.dtype == BF16:
        dep = getattr(st, "wdeps", {}).get(ap.tensor.name)
        return "q", ([dep] if dep is not None else [])
    return "g", []


def ffn_layer(st, W_dram, ki_n, ob_n, src, dst, act, scale_sb=None, bias_sb=None,
              OBG=8, tern_thr=None, src2_mul=None):
    pr = st.pr
    T = st.T
    tbn = T // 512
    wg, wgB = st.wg, st.wgB
    wgf, wgfB = (st.wgf, st.wgfB) if tern_thr is not None else (None, None)
    if src[0] == "sbuf":
        slab, slabB = None, None
    else:
        slab, slabB = st.slab, st.slabB
    src_T = T if len(src) < 4 else src[3]
    if src[0] == "dram_sw":
        pr.sw_begin("q")
    for og0 in range(0, ob_n, OBG):
        obg = min(OBG, ob_n - og0)
        if tern_thr is None:
            weng, wdep = wsrc(st, W_dram)
            nk = 4 if ki_n % 4 == 0 else (2 if ki_n % 2 == 0 else 1)
            kk = ki_n // nk
            wtoks = []
            for c in range(nk):
                def fn(e, og0=og0, obg=obg, c=c, kk=kk):
                    return e.dma_start(
                        wg[:, c * kk * obg * 128:(c + 1) * kk * obg * 128],
                        bass.AP(W_dram.tensor,
                                W_dram.offset + og0 * 128
                                + c * kk * ob_n * 128 * 128,
                                [[ob_n * 128, 128], [ob_n * 128 * 128, kk],
                                 [1, obg * 128]]))
                tok = pr.emit(weng, fn, wr=[wgB] if c == 0 else (),
                              rdall=wdep, dma=True)
                if c > 0:
                    wgB.r.append(tok)
                wtoks.append(tok)
        else:
            wtoks = []
            thr, thrB = tern_thr
            CH = 4096
            kstep = CH // (obg * 128)
            for k0 in range(0, ki_n, kstep):
                kk = min(kstep, ki_n - k0)
                c0 = k0 * obg * 128
                c1 = c0 + kk * obg * 128
                def fn(e, og0=og0, obg=obg, k0=k0, kk=kk):
                    return e.dma_start(
                        wgf[:, 0:kk * obg * 128],
                        bass.AP(W_dram.tensor,
                                W_dram.offset + og0 * 128 + k0 * ob_n * 128 * 128,
                                [[ob_n * 128, 128], [ob_n * 128 * 128, kk],
                                 [1, obg * 128]]))
                pr.emit("q", fn, wr=[wgfB], dma=True)
                pr.emit("v", lambda e, c0=c0, c1=c1: e.tensor_scalar(
                    st.tscr[:, 0:c1 - c0], wgf[:, 0:c1 - c0], thr[:, 0:1], None,
                    ALU.is_ge), rd=[wgfB, thrB], wr=[st.tscrB])
                pr.emit("v", lambda e, c0=c0, c1=c1: e.tensor_scalar(
                    st.tscr2[:, 0:c1 - c0], wgf[:, 0:c1 - c0], thr[:, 1:2], None,
                    ALU.is_le), rd=[wgfB, thrB], wr=[st.tscr2B])
                pr.emit("v", lambda e, c0=c0, c1=c1: e.tensor_tensor(
                    wg[:, c0:c1], st.tscr[:, 0:c1 - c0], st.tscr2[:, 0:c1 - c0],
                    ALU.subtract), rd=[st.tscrB, st.tscr2B], wr=[wgB])
        for tb in range(tbn):
            if src[0] == "dram_sw":
                sh, shB = src[1], src[2]
                def fnv(e, cc, tb=tb):
                    return e.dma_start(
                        slab[:, 0:ki_n * 512],
                        bass.AP(sh.tensor, sh.offset + (cc % 2) * HALF + tb * 512,
                                [[src_T, 128], [src_T * 128, ki_n], [1, 512]]))
                pr.emit_sw("q", fnv, rdall=[shB], wr=[slabB], dma=True)
                mov, movB = slab, [slabB]
            elif src[0] == "dram":
                sh, shB = src[1], src[2]
                def fn(e, tb=tb):
                    return e.dma_start(
                        slab[:, 0:ki_n * 512],
                        bass.AP(sh.tensor, sh.offset + tb * 512,
                                [[src_T, 128], [src_T * 128, ki_n], [1, 512]]))
                pr.emit("q", fn, rdall=[shB], wr=[slabB], dma=True)
                mov, movB = slab, [slabB]
            else:
                mov = src[1]
                movB = [src[2][t] for t in range(tb * 4, tb * 4 + 4)]
            for obl in range(obg):
                ob = og0 + obl
                bank, bankB = st.bank()
                fns = []
                for ki in range(ki_n):
                    if src[0] in ("dram", "dram_sw"):
                        rhs_ap = lambda e, ki=ki: mov[:, ki * 512:(ki + 1) * 512]
                    else:
                        rhs_ap = lambda e, ki=ki, tb=tb: mov[:, ki * T + tb * 512:
                                                            ki * T + tb * 512 + 512]
                    def fn(e, ki=ki, obl=obl, obg=obg, bank=bank, rhs_ap=rhs_ap):
                        return e.matmul(
                            bank[:, 0:512],
                            wg[:, ki * obg * 128 + obl * 128:
                               ki * obg * 128 + (obl + 1) * 128],
                            rhs_ap(e), start=(ki == 0), stop=(ki == ki_n - 1))
                    fns.append(fn)
                pr.group("t", fns, rd=[wgB] + movB, wr=[bankB], extra=wtoks)
                st.ffn_evac(dst, act, scale_sb, bias_sb, ob, tb, bank, bankB,
                            src2_mul)
    if src[0] == "dram_sw":
        pr.sw_end()



def emit_l1f(st, aps, acc, accB):
    pr = st.pr
    T = st.T
    NT = st.NT
    gsb = pr.sb("l1f_gw", [128, 8 * 16], BF16); gsbB = Buf()
    gbr = pr.sb("l1f_gb", [1, 16], BF16); gbrB = Buf()
    onesr = pr.sb("l1f_ones", [1, 128], BF16); onesrB = Buf()
    rtm = pr.sb("l1f_r", [128, NT * 16], F32); rtmB = Buf()
    wts = pr.sb("l1f_w", [128, NT * 16], F32); wtsB = Buf()
    tv = pr.sb("l1f_tv", [128, NT * 16], F32); tvB = Buf()
    v12 = pr.sb("l1f_v12", [128, 4], F32); v12B = Buf()
    weg = pr.sb("l1f_weg", [128, 8 * 512], BF16); wegB = Buf()
    weu = pr.sb("l1f_weu", [128, 8 * 512], BF16); weuB = Buf()
    wed = pr.sb("l1f_wed", [128, 4 * 1024], BF16); wedB = Buf()
    hg = pr.sb("l1f_hg", [128, 4 * T], BF16)
    hgB = [Buf(f"hg{i}") for i in range(4)]
    af = pr.sb("l1f_act", [128, 4 * T], BF16)
    afB = [Buf(f"af{i}") for i in range(4)]

    pr.emit("g", lambda e: e.dma_start(
        gsb[:, :], bass.AP(aps["L1_ffn_gate_w"].tensor, 0,
                           [[16, 128], [2048, 8], [1, 16]])), wr=[gsbB], dma=True)
    pr.emit("g", lambda e: e.dma_start(gbr[:, :], aps["L1_ffn_gate_b"][None, :]),
            wr=[gbrB], dma=True)
    pr.emit("v", lambda e: e.memset(onesr[:, :], 1.0), wr=[onesrB])

    for i in range(NT):
        bank, bankB = st.bank()
        fns = []
        for ki in range(8):
            def fn(e, i=i, ki=ki, bank=bank, xf=st.xln_fm):
                return e.matmul(bank[:, 0:16],
                                xf[:, ki * T + i * 128:ki * T + (i + 1) * 128],
                                gsb[:, ki * 16:(ki + 1) * 16],
                                start=(ki == 0), stop=False)
            fns.append(fn)
        def fnb(e, bank=bank):
            return e.matmul(bank[:, 0:16], onesr[0:1, :], gbr[0:1, :],
                            start=False, stop=True)
        fns.append(fnb)
        pr.group("t", fns, rd=[st.xlnfB[i], gsbB, onesrB, gbrB], wr=[bankB])
        pr.emit("s", lambda e, i=i, bank=bank: e.activation(
            rtm[:, i * 16:(i + 1) * 16], bank[:, 0:16], AF.Sigmoid),
            rd=[bankB], wr=[rtmB])
        r_ap = rtm[:, i * 16:(i + 1) * 16]
        t_ap = tv[:, i * 16:(i + 1) * 16]
        w_ap = wts[:, i * 16:(i + 1) * 16]
        pr.emit("v", lambda e, r_ap=r_ap: e.reduce_max(v12[:, 0:1], r_ap, axis=AX.X),
                rd=[rtmB], wr=[v12B])
        pr.emit("v", lambda e, r_ap=r_ap, t_ap=t_ap: e.tensor_scalar(
            t_ap, r_ap, v12[:, 0:1], -1e30, ALU.is_ge, ALU.mult),
            rd=[rtmB, v12B], wr=[tvB])
        pr.emit("v", lambda e, r_ap=r_ap, t_ap=t_ap: e.tensor_tensor(
            t_ap, t_ap, r_ap, ALU.add), rd=[rtmB, tvB], wr=[tvB])
        pr.emit("v", lambda e, t_ap=t_ap: e.reduce_max(v12[:, 1:2], t_ap, axis=AX.X),
                rd=[tvB], wr=[v12B])
        pr.emit("v", lambda e: e.tensor_tensor(v12[:, 2:3], v12[:, 0:1], v12[:, 1:2],
                                               ALU.add), rd=[v12B], wr=[v12B])
        pr.emit("v", lambda e: e.reciprocal(v12[:, 3:4], v12[:, 2:3]),
                rd=[v12B], wr=[v12B])
        pr.emit("v", lambda e, r_ap=r_ap, w_ap=w_ap: e.tensor_scalar(
            w_ap, r_ap, v12[:, 1:2], None, ALU.is_ge), rd=[rtmB, v12B], wr=[wtsB])
        pr.emit("v", lambda e, r_ap=r_ap, w_ap=w_ap: e.tensor_tensor(
            w_ap, w_ap, r_ap, ALU.mult), rd=[rtmB, wtsB], wr=[wtsB])
        pr.emit("v", lambda e, w_ap=w_ap: e.tensor_scalar(
            w_ap, w_ap, v12[:, 3:4], None, ALU.mult), rd=[v12B, wtsB], wr=[wtsB])

    wgT = aps["L1_ffn_experts_wg"]
    wuT = aps["L1_ffn_experts_wu"]
    wdT = aps["L1_ffn_experts_wd"]
    for ei in range(-1, 16):
        if ei < 0:
            g_ap = bass.AP(aps["L1_ffn_shared_wg"].tensor, 0, [[512, 128], [65536, 8], [1, 512]])
            u_ap = bass.AP(aps["L1_ffn_shared_wu"].tensor, 0, [[512, 128], [65536, 8], [1, 512]])
            d_ap = bass.AP(aps["L1_ffn_shared_wd"].tensor, 0, [[1024, 128], [131072, 4], [1, 1024]])
        else:
            g_ap = bass.AP(wgT.tensor, ei * 1024 * 512, [[512, 128], [65536, 8], [1, 512]])
            u_ap = bass.AP(wuT.tensor, ei * 1024 * 512, [[512, 128], [65536, 8], [1, 512]])
            d_ap = bass.AP(wdT.tensor, ei * 512 * 1024, [[1024, 128], [131072, 4], [1, 1024]])
        for ap_, dst_, dstB_ in ((g_ap, weg, wegB), (u_ap, weu, weuB),
                                 (d_ap, wed, wedB)):
            weng, wdep = wsrc(st, ap_)
            pr.emit(weng, lambda e, ap_=ap_, dst_=dst_: e.dma_start(dst_[:, :], ap_),
                    wr=[dstB_], rdall=wdep, dma=True)
        for ob in range(4):
            for tb in range(T // 512):
                bank, bankB = st.bank()
                fns = []
                for ki in range(8):
                    def fn(e, ob=ob, tb=tb, ki=ki, bank=bank, xf=st.xln_fm):
                        return e.matmul(bank[:, 0:512],
                                        weg[:, ki * 512 + ob * 128:ki * 512 + (ob + 1) * 128],
                                        xf[:, ki * T + tb * 512:ki * T + (tb + 1) * 512],
                                        start=(ki == 0), stop=(ki == 7))
                    fns.append(fn)
                pr.group("t", fns, rd=[wegB] + [st.xlnfB[t] for t in range(tb * 4, tb * 4 + 4)],
                         wr=[bankB])
                pr.emit("s", lambda e, ob=ob, tb=tb, bank=bank: e.activation(
                    hg[:, ob * T + tb * 512:ob * T + (tb + 1) * 512],
                    bank[:, 0:512], AF.Silu), rd=[bankB], wr=[hgB[ob]])
        for ob in range(4):
            for tb in range(T // 512):
                bank, bankB = st.bank()
                fns = []
                for ki in range(8):
                    def fn(e, ob=ob, tb=tb, ki=ki, bank=bank, xf=st.xln_fm):
                        return e.matmul(bank[:, 0:512],
                                        weu[:, ki * 512 + ob * 128:ki * 512 + (ob + 1) * 128],
                                        xf[:, ki * T + tb * 512:ki * T + (tb + 1) * 512],
                                        start=(ki == 0), stop=(ki == 7))
                    fns.append(fn)
                pr.group("t", fns, rd=[weuB] + [st.xlnfB[t] for t in range(tb * 4, tb * 4 + 4)],
                         wr=[bankB])
                def fn(e, ob=ob, tb=tb, bank=bank):
                    return e.tensor_tensor(
                        af[:, ob * T + tb * 512:ob * T + (tb + 1) * 512],
                        bank[:, 0:512],
                        hg[:, ob * T + tb * 512:ob * T + (tb + 1) * 512], ALU.mult)
                pr.emit("v", fn, rd=[bankB, hgB[ob]], wr=[afB[ob]])
        for i in range(NT):
            for ob2 in range(2):
                bank, bankB = st.bank()
                fns = []
                for ki2 in range(4):
                    def fn(e, i=i, ob2=ob2, ki2=ki2, bank=bank):
                        return e.matmul(bank[:, 0:512],
                                        af[:, ki2 * T + i * 128:ki2 * T + (i + 1) * 128],
                                        wed[:, ki2 * 1024 + ob2 * 512:ki2 * 1024 + (ob2 + 1) * 512],
                                        start=(ki2 == 0), stop=(ki2 == 3))
                    fns.append(fn)
                pr.group("t", fns, rd=[wedB] + [afB[k] for k in range(4)], wr=[bankB])
                a_ap = acc[:, i * D + ob2 * 512:i * D + (ob2 + 1) * 512]
                if ei < 0:
                    def fn(e, a_ap=a_ap, bank=bank):
                        return e.scalar_tensor_tensor(
                            a_ap, bank[:, 0:512], 1.0, a_ap, ALU.mult, ALU.add)
                else:
                    w_col = wts[:, i * 16 + ei:i * 16 + ei + 1]
                    def fn(e, a_ap=a_ap, bank=bank, w_col=w_col):
                        return e.scalar_tensor_tensor(
                            a_ap, bank[:, 0:512], w_col, a_ap, ALU.mult, ALU.add)
                pr.emit("v", fn, rd=[bankB, wtsB, accB[i]], wr=[accB[i]])
    return


L1F_WSPECS = {
    "L1_ffn_gate_w": (1024, 16), "L1_ffn_gate_b": (16,),
    "L1_ffn_shared_wg": (1024, 512), "L1_ffn_shared_wu": (1024, 512),
    "L1_ffn_shared_wd": (512, 1024),
    "L1_ffn_experts_wg": (16, 1024, 512), "L1_ffn_experts_wu": (16, 1024, 512),
    "L1_ffn_experts_wd": (16, 512, 1024),
}


def host_l1f_weights(I):
    return {k: I[k] for k in L1F_WSPECS}


def build_stage_l1f(NT=16):
    nc = bass.Bass("TRN2", target_bir_lowering=False)
    T = NT * 128
    xin = nc.dram_tensor("xin", [T, D], F32, kind="ExternalInput")
    ident = nc.dram_tensor("ident", [128, 128], F32, kind="ExternalInput")
    xout = nc.dram_tensor("xout", [T, D], F32, kind="ExternalOutput")
    aps = {"xin": xin.ap(), "ident": ident.ap(), "xout": xout.ap()}
    for name, shape in L1F_WSPECS.items():
        aps[name] = nc.dram_tensor(name, list(shape), F32, kind="ExternalInput").ap()
    pr = PR(nc)
    st = Stage(pr, NT, xin=aps["xin"], xout=aps["xout"], arenas=())
    acc = pr.sb("l1f_acc", [128, NT * D], F32)
    accB = [Buf(f"acc{i}") for i in range(NT)]
    st.load_consts(aps["ident"])
    st.ln_in(keep_tm=(acc, accB))
    emit_l1f(st, aps, acc, accB)
    st.ln_out_add_tm(acc, accB)
    toks = [(sm, v) for sm, v in pr.counts.items() if v > 0]
    pr.emit("g", lambda e: e.engine_nop(), extra=toks)
    pr.build_block()
    pr.close()
    return nc



def attn_core(st, qfm, qfmB, k_stat, v_stat, attn_d, attn_dB, qb, h, ktiles,
              scale, exp_bias=None, bias_fn=None, NH=16, pv=None, accum=False):
    pr = st.pr
    T = st.T
    if pv is None:
        pvb, pvbB = st.pb[4 + (qb * NH + h) % 2], st.pbB[4 + (qb * NH + h) % 2]
    else:
        pvb, pvbB = pv
    first = True
    for j, kt in enumerate(ktiles):
        bank, bankB = st.bank4()
        ks, ksB = k_stat(kt, h)
        qs = qfm[64 * (h % 2):64 * (h % 2) + 64,
                 (h // 2) * T + qb * 512:(h // 2) * T + (qb + 1) * 512]
        pr.group("t", [lambda e, ks=ks, qs=qs, bank=bank: e.matmul(
            bank[:, 0:512], ks, qs, start=True, stop=True)],
            rd=ksB + [qfmB[t] for t in range(qb * 4, qb * 4 + 4)], wr=[bankB])
        bias = bias_fn(kt) if bias_fn else None
        r = j % 2
        pro = st.aprobs[:, r * 512:(r + 1) * 512]
        proB = st.aprobsB[r]
        if bias is not None and bias[0] == "tile":
            sco = st.asco[:, 0:512]
            scoB = st.ascoB[0]
            pr.emit("v", lambda e, bank=bank, sco=sco, b=bias: e.scalar_tensor_tensor(
                sco, bank[:, 0:512], scale, b[1], ALU.mult, ALU.add),
                rd=[bankB] + bias[2], wr=[scoB])
            pr.emit("s", lambda e, sco=sco, pro=pro: e.activation(
                pro, sco, AF.Exp), rd=[scoB], wr=[proB])
        elif bias is not None and bias[0] == "const":
            pr.emit("s", lambda e, bank=bank, pro=pro, b=bias: e.activation(
                pro, bank[:, 0:512], AF.Exp, bias=b[1], scale=scale),
                rd=[bankB] + bias[2], wr=[proB])
        else:
            pr.emit("s", lambda e, bank=bank, pro=pro: e.activation(
                pro, bank[:, 0:512], AF.Exp, scale=scale), rd=[bankB], wr=[proB])
        vs, vsB = v_stat(kt, h)
        pr.group("t", [lambda e, vs=vs, pro=pro, pvb=pvb, first=first,
                       last=(j == len(ktiles) - 1): e.matmul(
            pvb[0:65, 0:512], vs, pro, start=first, stop=last)],
            rd=vsB + [proB], wr=[pvbB])
        first = False
    pr.emit("s", lambda e, pvb=pvb: e.copy(st.aznrm[64:65, 0:512], pvb[64:65, 0:512]),
            rd=[pvbB], wr=[st.azrowB])
    pr.emit("v", lambda e: e.reciprocal(st.aznrm[64:65, 512:1024], st.aznrm[64:65, 0:512]),
            rd=[st.azrowB], wr=[st.azrecB])
    bank, bankB = st.bank4()
    pr.group("t", [lambda e, bank=bank: e.matmul(
        bank[0:64, 0:512], st.aznrm[64:65, 1024:1088], st.aznrm[64:65, 512:1024],
        start=True, stop=True)], rd=[st.aones64B, st.azrecB], wr=[bankB])
    pr.emit("s", lambda e, bank=bank: e.copy(st.aznrm[0:64, 0:512], bank[0:64, 0:512]),
            rd=[bankB], wr=[st.azbB])
    r2 = (qb * NH + h) % 2
    stg = st.astg[:, r2 * 512:(r2 + 1) * 512]
    pr.emit("v", lambda e, pvb=pvb, stg=stg: e.tensor_tensor(
        stg[0:64, :], pvb[0:64, 0:512], st.aznrm[0:64, 0:512], ALU.mult),
        rd=[pvbB, st.azbB], wr=[st.astgB[r2]])
    ch = h // 2
    off = (ch * 128 + 64 * (h % 2)) * T + qb * 512
    tok = pr.emit("q", lambda e, stg=stg, off=off: e.dma_start(
        bass.AP(attn_d.tensor, off, [[T, 64], [1, 512]]), stg[0:64, :]),
        rd=[st.astgB[r2]], dma=True)
    attn_dB.r.append(tok)


def attn_arenas(st, need_sco=True):
    pr = st.pr
    if not hasattr(st, "aprobs"):
        st.aprobs = pr.sb("a_probs", [128, 4 * 512], BF16)
        st.aprobsB = [Buf(f"pr{i}") for i in range(4)]
        st.aznrm = pr.sb("a_nrm", [128, 1088], F32)
        st.azrowB = Buf(); st.azrecB = Buf(); st.azbB = Buf()
        st.aones64B = Buf()
        st.astg = pr.sb("a_stg", [128, 2 * 512], BF16)
        st.astgB = [Buf("astg0"), Buf("astg1")]
    if need_sco and not hasattr(st, "asco"):
        st.asco = pr.sb("a_sco", [128, 2 * 512], F32)
        st.ascoB = [Buf(f"sc{i}") for i in range(2)]
    pr.emit("v", lambda e: e.memset(st.aznrm[64:65, 1024:1088], 1.0),
            wr=[st.aones64B])



def emit_l1a(st, aps):
    kv_n = st.NT + 2
    pr = st.pr
    T = st.T
    NT = st.NT
    nc = pr.nc
    attn_arenas(st)
    wk = pr.sb("l1a_wk", [128, 8 * 128], BF16); wkB = Buf()
    wkS = pr.sb("l1a_wkS", [128, 8 * 128], BF16); wkSB = Buf()
    wv = pr.sb("l1a_wv", [128, 8 * 128], BF16); wvB = Buf()
    bq = pr.sb("l1a_bq", [128, 8], F32); bqB = Buf()
    bk = pr.sb("l1a_bk", [128, 1], F32); bkB = Buf()
    bkS = pr.sb("l1a_bkS", [128, 1], F32); bkSB = Buf()
    bv = pr.sb("l1a_bv", [128, 1], F32); bvB = Buf()
    bo = pr.sb("l1a_bo", [128, 8], F32); boB = Buf()
    for t, b, ap in ((wk, wkB, "L1_attn_Wk"), (wkS, wkSB, "L1_attn_WkS"),
                     (wv, wvB, "L1_attn_Wv")):
        weng, wdep = wsrc(st, aps[ap])
        pr.emit(weng, lambda e, t=t, ap=ap: e.dma_start(
            t[:, :], bass.AP(aps[ap].tensor, 0, [[128, 128], [16384, 8], [1, 128]])),
            wr=[b], rdall=wdep, dma=True)
    col_vec_load(pr, aps["L1_attn_bq"], bq, bqB, 8)
    col_vec_load(pr, aps["L1_attn_bk"], bk, bkB, 1)
    col_vec_load(pr, aps["L1_attn_bkS"], bkS, bkSB, 1)
    col_vec_load(pr, aps["L1_attn_bv"], bv, bvB, 1)
    col_vec_load(pr, aps["L1_attn_bo"], bo, boB, 8)
    masks = pr.sb("l1a_masks", [128, 6 * 512], BF16); masksB = Buf()
    pr.emit("q" if aps["l1_masks"].dtype == BF16 else "g",
            lambda e: e.dma_start(masks[:, :], aps["l1_masks"]), wr=[masksB], dma=True)

    qfm = pr.sb("l1a_qfm", [128, 8 * T], BF16)
    qfmB = [Buf(f"q{i}") for i in range(NT)]
    ffn_layer(st, aps["L1_attn_Wq"], 8, 8, ("sbuf", st.xln_fm, st.xlnfB),
              ("sbuf", qfm, qfmB), None, bias_sb=(bq, bqB), OBG=8)

    kfmA = pr.sb("l1a_kfmA", [128, kv_n * 128], BF16); kfmAB = Buf()
    kfmB_ = pr.sb("l1a_kfmB", [128, kv_n * 128], BF16); kfmBB = Buf()
    vtm = pr.sb("l1a_vtm", [128, kv_n * 130], BF16); vtmB = Buf()
    kvxf = pr.sb("l1a_kvxf", [128, 8 * 512], BF16)
    kvxfB = Buf("l1a_kvxf")
    xfull = aps.get("xfull")
    xfull_sw = aps.get("xfull_sw")
    kvz = pr.sb("l1a_kvz", [128, kv_n], F32); kvzB = Buf()
    pr.emit("q", lambda e: e.dma_start(kvz[:, :], aps["l1_kvzero"]), wr=[kvzB], dma=True)
    if xfull_sw is not None:
        pr.sw_begin("q")
    for g0 in range(0, kv_n, 4):
        gn = min(4, kv_n - g0)
        for gi in range(gn):
            kt = g0 + gi
            tok = kt * 128
            r = kt % 2
            xt = st.xt[:, r * D:(r + 1) * D]
            if xfull_sw is not None:
                def fnv(e, cc, tok=tok, xt=xt):
                    return e.dma_start(
                        xt, xfull_sw[(cc % 2) * HALF + tok:
                                     (cc % 2) * HALF + tok + 128, :])
                pr.emit_sw("q", fnv, wr=[st.xtB[r]], dma=True)
            else:
                pr.emit("q", lambda e, tok=tok, xt=xt: e.dma_start(
                    xt, xfull[tok:tok + 128, :]), wr=[st.xtB[r]], dma=True)
            st._ln_stats(xt, [st.xtB[r]])
            yt = st.ytm[:, r * D:(r + 1) * D]
            pr.emit("v", lambda e, xt=xt, yt=yt: e.tensor_scalar(
                yt, xt, st.stat[:, 2:3], st.stat[:, 6:7], ALU.subtract, ALU.mult),
                rd=[st.xtB[r], st.statB], wr=[st.ytmB[r]])
            st.tm_to_fm(st.ytm, st.ytmB, r, kvxf, [kvxfB] * 4, gi, fm_stride=512)
        for gi in range(gn):
            kt = g0 + gi
            for (wt, wtB, bb, bbB, dst, dstB) in (
                    (wk, wkB, bk, bkB, kfmA, kfmAB),
                    (wkS, wkSB, bkS, bkSB, kfmB_, kfmBB)):
                bank, bankB = st.bank4()
                fns = []
                for ki in range(8):
                    def fn(e, gi=gi, ki=ki, bank=bank, wt=wt):
                        return e.matmul(bank[:, 0:128], wt[:, ki * 128:(ki + 1) * 128],
                                        kvxf[:, ki * 512 + gi * 128:ki * 512 + (gi + 1) * 128],
                                        start=(ki == 0), stop=(ki == 7))
                    fns.append(fn)
                pr.group("t", fns, rd=[wtB, kvxfB], wr=[bankB])
                pr.emit("s", lambda e, kt=kt, bank=bank, bb=bb, dst=dst: e.activation(
                    dst[:, kt * 128:(kt + 1) * 128], bank[:, 0:128],
                    AF.Identity, bias=bb[:, 0:1], scale=1.0),
                    rd=[bankB, bbB], wr=[dstB])
            bank, bankB = st.bank4()
            fns = []
            for ki in range(8):
                def fn(e, gi=gi, ki=ki, bank=bank):
                    return e.matmul(bank[:, 0:128], wv[:, ki * 128:(ki + 1) * 128],
                                    kvxf[:, ki * 512 + gi * 128:ki * 512 + (gi + 1) * 128],
                                    start=(ki == 0), stop=(ki == 7))
                fns.append(fn)
            pr.group("t", fns, rd=[wvB, kvxfB], wr=[bankB])
            r = gi % 2
            yt2 = st.ytm[:, r * D:r * D + 128]
            pr.emit("s", lambda e, bank=bank, yt2=yt2: e.activation(
                yt2, bank[:, 0:128], AF.Identity, bias=bv[:, 0:1], scale=1.0),
                rd=[bankB, bvB], wr=[st.ytmB[r]])
            pr.group("t", [lambda e, yt2=yt2: e.transpose(
                st.ptb[:, 0:128], yt2, st.ident[:, :])],
                rd=[st.ytmB[r], st.identB], wr=[st.ptbB])
            for g in range(2):
                pr.emit("s", lambda e, kt=kt, g=g: e.copy(
                    vtm[:, kt * 130 + 65 * g:kt * 130 + 65 * g + 64],
                    st.ptb[:, 64 * g:64 * g + 64]), rd=[st.ptbB], wr=[vtmB])
                pr.emit("v", lambda e, kt=kt, g=g: e.memset(
                    vtm[:, kt * 130 + 65 * g + 64:kt * 130 + 65 * g + 65], 1.0),
                    wr=[vtmB])
            for tens, tB in ((kfmA, kfmAB), (kfmB_, kfmBB)):
                pr.emit("v", lambda e, kt=kt, tens=tens: e.tensor_scalar(
                    tens[:, kt * 128:(kt + 1) * 128], tens[:, kt * 128:(kt + 1) * 128],
                    kvz[:, kt:kt + 1], None, ALU.mult), rd=[kvzB, tB], wr=[tB])
            for g in range(2):
                pr.emit("v", lambda e, kt=kt, g=g: e.tensor_scalar(
                    vtm[:, kt * 130 + 65 * g:kt * 130 + 65 * g + 64],
                    vtm[:, kt * 130 + 65 * g:kt * 130 + 65 * g + 64],
                    kvz[:, kt:kt + 1], None, ALU.mult), rd=[kvzB, vtmB], wr=[vtmB])

    if xfull_sw is not None:
        pr.sw_end()

    attn_d = nc.dram_tensor("l1a_attn", [8 * 128, T], BF16, kind="Internal").ap()
    attn_dB = Buf("l1a_attn")

    def k_stat(kt, h):
        g = h // 8
        base = 64 * (h % 2)
        use_A = (g == 0) == (base == 0)
        t = kfmA if use_A else kfmB_
        tB = kfmAB if use_A else kfmBB
        return t[base:base + 64, kt * 128:(kt + 1) * 128], [tB]

    def v_stat(kt, h):
        g = h // 8
        return vtm[:, kt * 130 + 65 * g:kt * 130 + 65 * (g + 1)], [vtmB]

    for qb in range(T // 512):
        kt_first = qb * 4
        kts = [kt_first + j for j in range(6)]

        def bias_fn(kt, kt_first=kt_first):
            j = kt - kt_first
            return ("tile", masks[:, j * 512:(j + 1) * 512], [masksB])
        for h in range(16):
            attn_core(st, qfm, qfmB, k_stat, v_stat, attn_d, attn_dB, qb, h, kts,
                      1.0, bias_fn=bias_fn)

    ffn_layer(st, aps["L1_attn_Wo"], 8, 8, ("dram", attn_d, attn_dB),
              ("sbuf", st.yfm, st.yfB), None, bias_sb=(bo, boB), OBG=8)
    st.ln_out_add_fm(st.yfm, st.yfB)


L1A_WSPECS = {
    "L1_attn_Wq": (1024, 1024), "L1_attn_bq": (1024,),
    "L1_attn_Wk": (1024, 128), "L1_attn_bk": (128,),
    "L1_attn_WkS": (1024, 128), "L1_attn_bkS": (128,),
    "L1_attn_Wv": (1024, 128), "L1_attn_bv": (128,),
    "L1_attn_Wo": (1024, 1024), "L1_attn_bo": (1024,),
    "l1_masks": (128, 6 * 512),
}


def host_l1a_percore(x_row, q_tok0, NT=16):
    T = NT * 128
    win = np.zeros((T + 256, D), np.float32)
    lo = q_tok0 - 128
    s0, s1 = max(0, lo), min(x_row.shape[0], lo + T + 256)
    win[s0 - lo:s1 - lo] = x_row[s0:s1]
    kvzero = np.ones((128, NT + 2), np.float32)
    for kt in range(NT + 2):
        tok = lo + kt * 128
        if tok < 0 or tok >= x_row.shape[0]:
            kvzero[:, kt] = 0.0
    return {"xfull": win, "l1_kvzero": kvzero}


def host_l1a_weights(I):
    sw = np.concatenate([I["L1_attn_Wk"][:, 64:128], I["L1_attn_Wk"][:, 0:64]], axis=1)
    bs = np.concatenate([I["L1_attn_bk"][64:128], I["L1_attn_bk"][0:64]])
    masks = np.zeros((6, 128, 512), np.float32)
    for j in range(6):
        delta = (j - 1) * 128
        kl = np.arange(128)[:, None]
        ql = np.arange(512)[None, :]
        rel = delta + kl - ql
        masks[j] = np.where((rel >= -128) & (rel <= 127), 0.0, -1e9)
    masks_t = masks.transpose(1, 0, 2).reshape(128, 6 * 512)
    return {
        "L1_attn_Wq": I["L1_attn_Wq"], "L1_attn_bq": I["L1_attn_bq"],
        "L1_attn_Wk": I["L1_attn_Wk"], "L1_attn_bk": I["L1_attn_bk"],
        "L1_attn_WkS": np.ascontiguousarray(sw), "L1_attn_bkS": bs,
        "L1_attn_Wv": I["L1_attn_Wv"], "L1_attn_bv": I["L1_attn_bv"],
        "L1_attn_Wo": I["L1_attn_Wo"], "L1_attn_bo": I["L1_attn_bo"],
        "l1_masks": np.ascontiguousarray(masks_t).astype(ml_dtypes.bfloat16),
    }


def build_stage_l1a(NT=16):
    nc = bass.Bass("TRN2", target_bir_lowering=False)
    T = NT * 128
    xin = nc.dram_tensor("xin", [T, D], F32, kind="ExternalInput")
    xfull = nc.dram_tensor("xfull", [T + 256, D], F32, kind="ExternalInput")
    ident = nc.dram_tensor("ident", [128, 128], F32, kind="ExternalInput")
    xout = nc.dram_tensor("xout", [T, D], F32, kind="ExternalOutput")
    aps = {"xin": xin.ap(), "xfull": xfull.ap(), "ident": ident.ap(),
           "xout": xout.ap()}
    for name, shape in L1A_WSPECS.items():
        aps[name] = nc.dram_tensor(name, list(shape), F32, kind="ExternalInput").ap()
    aps["l1_kvzero"] = nc.dram_tensor("l1_kvzero", [128, NT + 2], F32,
                                      kind="ExternalInput").ap()
    pr = PR(nc)
    st = Stage(pr, NT, xin=aps["xin"], xout=aps["xout"], arenas=("ffn",),
               wg_cols=8192, slab_cols=4096)
    st.load_consts(aps["ident"])
    st.ln_in()
    emit_l1a(st, aps)
    toks = [(sm, v) for sm, v in pr.counts.items() if v > 0]
    pr.emit("g", lambda e: e.engine_nop(), extra=toks)
    pr.build_block()
    pr.close()
    return nc



def emit_full_attn(st, aps, layer):
    pr = st.pr
    nc = pr.nc
    T = st.T
    NT = st.NT
    KT = S // 128
    rope = layer == "l0"
    attn_arenas(st, need_sco=(layer == "l3"))
    pfx = layer + "a_"
    W = {}
    for nm in ("wk", "wv"):
        W[nm] = pr.sb(pfx + nm, [128, 8 * 1024], BF16)
        W[nm + "B"] = Buf()
        src_name = {"l0": {"wk": "L0_attn_w_kTP", "wv": "L0_attn_w_vT"},
                    "l3": {"wk": "L3_attn_WkT", "wv": "L3_attn_WvT"}}[layer][nm]
        weng, wdep = wsrc(st, aps[src_name])
        W[nm + "toks"] = []
        for c in range(4):
            def fn(e, t=W[nm], ap=aps[src_name], c=c):
                return e.dma_start(
                    t[:, c * 2048:(c + 1) * 2048],
                    bass.AP(ap.tensor, c * 2 * 131072,
                            [[1024, 128], [131072, 2], [1, 1024]]))
            tok = pr.emit(weng, fn, wr=[W[nm + "B"]] if c == 0 else (),
                          rdall=wdep, dma=True)
            if c > 0:
                W[nm + "B"].r.append(tok)
            W[nm + "toks"].append(tok)
    if rope:
        W["wq"] = st.wg
        W["wqB"] = st.wgB
        weng, wdep = wsrc(st, aps["L0_attn_w_qTP"])
        W["wqtoks"] = []
        for c in range(4):
            def fn(e, ap=aps["L0_attn_w_qTP"], wg=st.wg, c=c):
                return e.dma_start(
                    wg[:, c * 2048:(c + 1) * 2048],
                    bass.AP(ap.tensor, c * 2 * 131072,
                            [[1024, 128], [131072, 2], [1, 1024]]))
            tok = pr.emit(weng, fn, wr=[st.wgB] if c == 0 else (),
                          rdall=wdep, dma=True)
            if c > 0:
                st.wgB.r.append(tok)
            W["wqtoks"].append(tok)
    woT = aps["L0_attn_w_oT"] if layer == "l0" else aps["L3_attn_WoT"]

    Kd = nc.dram_tensor(pfx + "Kd", [KT * 128, 1024], BF16, kind="Internal").ap()
    KdB = [Buf() for _ in range(KT)]
    Vd = nc.dram_tensor(pfx + "Vd", [KT * 128, 1040], BF16, kind="Internal").ap()
    VdB = [Buf() for _ in range(KT)]
    kxl = st.xln_fm
    vxl = st.yfm
    kxlB = [Buf() for _ in range(KT)]
    vxlB = [Buf() for _ in range(KT)]

    def load_kv_slab(kt, hg):
        pr.emit("q", lambda e, kt=kt, hg=hg: e.dma_start(
            kxl[:, kt * 256:(kt + 1) * 256],
            bass.AP(Kd.tensor, kt * 128 * 1024 + hg * 256,
                    [[1024, 128], [1, 256]])),
            rdall=[KdB[kt]] + list(st.xlnfB), wr=[kxlB[kt]], dma=True)
        pr.emit("q", lambda e, kt=kt, hg=hg: e.dma_start(
            vxl[:, kt * 260:(kt + 1) * 260],
            bass.AP(Vd.tensor, kt * 128 * 1040 + hg * 260,
                    [[1040, 128], [1, 260]])),
            rdall=[VdB[kt]], wr=[vxlB[kt]], dma=True)
    attn_d = nc.dram_tensor(pfx + "attn", [8 * 128, T], BF16, kind="Internal").ap()
    attn_dB = Buf()

    if rope:
        rT1 = pr.sb(pfx + "rT1", [128, 2 * 1024], BF16)
        rT1B = [Buf(), Buf()]
        rT2 = pr.sb(pfx + "rT2", [128, 2 * 1024], BF16)
        rT2B = [Buf(), Buf()]
    if layer == "l3":
        diag = pr.sb(pfx + "diag", [16, 512], BF16)
        diagB = Buf()
        diag_d = nc.dram_tensor(pfx + "diagd", [16, 8191], BF16, kind="Internal").ap()
        diag_dB = Buf()
        oh = pr.sb(pfx + "oh", [32, 512], BF16)
        ohB = [Buf(), Buf()]
        rel = pr.sb(pfx + "rel", [32, 16], BF16)
        relB = Buf()
        st.l3_strip = pr.sb(pfx + "strip", [128, 4 * 2432], BF16)
        st.l3_stripB = Buf()
        st.l3_fcol = pr.sb(pfx + "fcol", [128, 12], F32)
        st.l3_fcolB = Buf()
        st.l3_f2 = pr.sb(pfx + "f2", [1, 12], BF16)
        st.l3_f2B = Buf()
        onesr128 = pr.sb(pfx + "o128", [1, 128], BF16)
        onesr128B = Buf()
        pr.emit("v", lambda e: e.memset(onesr128[:, :], 1.0), wr=[onesr128B])
        pr.emit("g", lambda e: e.dma_start(rel[:, :], aps["L3_attn_rel_emb"]), wr=[relB], dma=True)
        for ci, c0 in enumerate(range(0, 8191, 512)):
            cn = min(512, 8191 - c0)
            pr.emit("q" if aps["l3_ohT"].dtype == BF16 else "g",
                    lambda e, c0=c0, cn=cn: e.dma_start(
                oh[:, 0:cn],
                bass.AP(aps["l3_ohT"].tensor, c0, [[8191, 32], [1, cn]])),
                wr=[ohB[0]], dma=True)
            bank, bankB = st.bank4()
            pr.group("t", [lambda e, cn=cn, bank=bank: e.matmul(
                bank[0:16, 0:cn], rel[:, :], oh[:, 0:cn],
                start=True, stop=True)], rd=[relB, ohB[0]], wr=[bankB])
            pr.emit("s", lambda e, cn=cn, bank=bank: e.copy(
                diag[:, 0:cn], bank[0:16, 0:cn]), rd=[bankB], wr=[diagB])
            tok = pr.emit("q", lambda e, c0=c0, cn=cn: e.dma_start(
                bass.AP(diag_d.tensor, c0, [[8191, 16], [1, cn]]), diag[:, 0:cn]),
                rd=[diagB], dma=True)
            diag_dB.r.append(tok)

        def load_hg_bias(heads):
            for hi, h in enumerate(heads):
                for (woff, C, Wn) in ((0, 3456, 1152), (1152, 5504, 1280)):
                    def fn(e, hi=hi, h=h, woff=woff, C=C, Wn=Wn):
                        return e.dma_start(
                            st.l3_strip[:, hi * 2432 + woff:hi * 2432 + woff + Wn],
                            bass.AP(diag_d.tensor, h * 8191 + C, [[1, 128], [1, Wn]]))
                    tok = pr.emit("q", fn, rdall=[diag_dB], wr=[st.l3_stripB], dma=True)
            for hi, h in enumerate(heads):
                def fn(e, hi=hi, h=h):
                    return e.dma_start(
                        st.l3_f2[0:1, hi * 3:hi * 3 + 3],
                        bass.AP(diag_d.tensor, h * 8191, [[1, 1], [5375, 2], [1, 2]])
                    )
                pass
            for hi, h in enumerate(heads):
                for ci, idx in enumerate((0, 5375, 8190)):
                    def fn(e, hi=hi, h=h, ci=ci, idx=idx):
                        return e.dma_start(
                            st.l3_f2[0:1, hi * 3 + ci:hi * 3 + ci + 1],
                            bass.AP(diag_d.tensor, h * 8191 + idx, [[1, 1], [1, 1]]))
                    pr.emit("g", fn, rdall=[diag_dB], wr=[st.l3_f2B], dma=True)
            bank, bankB = st.bank4()
            pr.group("t", [lambda e, bank=bank: e.matmul(
                bank[:, 0:12], onesr128[0:1, :], st.l3_f2[0:1, :],
                start=True, stop=True)], rd=[onesr128B, st.l3_f2B], wr=[bankB])
            pr.emit("s", lambda e, bank=bank: e.copy(st.l3_fcol[:, :], bank[:, 0:12]),
                    rd=[bankB], wr=[st.l3_fcolB])
        st.l3_load_hg_bias = load_hg_bias
    scale = 0.125

    qfm = pr.sb(pfx + "qfm", [128, 8 * T], BF16)
    qfmB = [Buf() for _ in range(NT)]
    if rope:
        qw = W["wq"]
        for i in range(NT):
            r = i % 2
            for half in range(2):
                bank, bankB = st.bank4()
                fns = []
                for ki in range(8):
                    def fn(e, i=i, ki=ki, half=half, bank=bank, xf=st.xln_fm):
                        return e.matmul(
                            bank[:, 0:512],
                            xf[:, ki * T + i * 128:ki * T + (i + 1) * 128],
                            qw[:, ki * 1024 + half * 512:ki * 1024 + (half + 1) * 512],
                            start=(ki == 0), stop=(ki == 7))
                    fns.append(fn)
                pr.group("t", fns, rd=[W["wqB"], st.xlnfB[i]], wr=[bankB],
                         extra=W["wqtoks"])
                pr.emit("s", lambda e, r=r, half=half, bank=bank: e.copy(
                    st.ytm[:, r * D + half * 512:r * D + (half + 1) * 512],
                    bank[:, 0:512]), rd=[bankB], wr=[st.ytmB[r]])
            reng = "q" if aps["ropeT1"].dtype == BF16 else "g"
            pr.emit(reng, lambda e, i=i, r=r: e.dma_start(
                rT1[:, r * 1024:(r + 1) * 1024], aps["ropeT1"][i * 128:(i + 1) * 128, :]),
                wr=[rT1B[r]], dma=True)
            pr.emit(reng, lambda e, i=i, r=r: e.dma_start(
                rT2[:, r * 1024:(r + 1) * 1024], aps["ropeT2"][i * 128:(i + 1) * 128, :]),
                wr=[rT2B[r]], dma=True)
            self_rope(st, st.ytm, st.ytmB, r, rT1, rT1B, rT2, rT2B)
            st.tm_to_fm(st.ytm, st.ytmB, r, qfm, qfmB, i)
    else:
        ffn_layer(st, aps["L3_attn_WqT"], 8, 8, ("sbuf", st.xln_fm, st.xlnfB),
                  ("sbuf", qfm, qfmB), None, OBG=4)

    kvxf = pr.sb(pfx + "kvxf", [128, 2 * 2048], BF16)
    kvxfB4 = {}
    kstg = st.slab
    kstgB = [[Buf(), Buf()], [Buf(), Buf()]]
    vstg = st.slab
    VOFF = 2048
    vstgB = Buf()
    for h in range(16):
        pr.emit("v", lambda e, h=h: e.memset(
            vstg[:, VOFF + h * 65 + 64:VOFF + h * 65 + 65], 1.0), wr=[vstgB])
    xfull = aps["xfull"]
    for g0 in range(0, KT, 2):
        s2 = (g0 // 2) % 2
        for gi in range(2):
            kt = g0 + gi
            r = kt % 2
            xt = st.xt[:, r * D:(r + 1) * D]
            pr.emit("q", lambda e, kt=kt, xt=xt: e.dma_start(
                xt, xfull[kt * 128:(kt + 1) * 128, :]), wr=[st.xtB[r]], dma=True)
            st._ln_stats(xt, [st.xtB[r]])
            yt = st.ytm[:, r * D:(r + 1) * D]
            pr.emit("v", lambda e, xt=xt, yt=yt: e.tensor_scalar(
                yt, xt, st.stat[:, 2:3], st.stat[:, 6:7], ALU.subtract, ALU.mult),
                rd=[st.xtB[r], st.statB], wr=[st.ytmB[r]])
            fi = s2 * 16 + gi
            kvxfB4.setdefault(fi, Buf())
            st.tm_to_fm(st.ytm, st.ytmB, r, kvxf, kvxfB4, fi, fm_stride=256)
        for gi in range(2):
            kt = g0 + gi
            r = gi % 2
            sl = kt % 2
            for half in range(2):
                bank, bankB = st.bank4()
                fns = []
                for ki in range(8):
                    def fn(e, s2=s2, gi=gi, ki=ki, half=half, bank=bank):
                        return e.matmul(
                            bank[:, 0:512],
                            kvxf[:, s2 * 2048 + ki * 256 + gi * 128:
                                 s2 * 2048 + ki * 256 + (gi + 1) * 128],
                            W["wk"][:, ki * 1024 + half * 512:ki * 1024 + (half + 1) * 512],
                            start=(ki == 0), stop=(ki == 7))
                    fns.append(fn)
                pr.group("t", fns, rd=[W["wkB"], kvxfB4[s2 * 16 + gi]],
                         wr=[bankB], extra=W["wktoks"])
                pr.emit("s", lambda e, r=r, half=half, bank=bank: e.copy(
                    st.ytm[:, r * D + half * 512:r * D + (half + 1) * 512],
                    bank[:, 0:512]), rd=[bankB], wr=[st.ytmB[r]])
            if rope:
                reng = "q" if aps["ropeT1F"].dtype == BF16 else "g"
                pr.emit(reng, lambda e, kt=kt, r=r: e.dma_start(
                    rT1[:, r * 1024:(r + 1) * 1024],
                    aps["ropeT1F"][kt * 128:(kt + 1) * 128, :]), wr=[rT1B[r]], dma=True)
                pr.emit(reng, lambda e, kt=kt, r=r: e.dma_start(
                    rT2[:, r * 1024:(r + 1) * 1024],
                    aps["ropeT2F"][kt * 128:(kt + 1) * 128, :]), wr=[rT2B[r]], dma=True)
                self_rope(st, st.ytm, st.ytmB, r, rT1, rT1B, rT2, rT2B)
            for j0 in range(0, 8, 4):
                fns = []
                for j in range(j0, j0 + 4):
                    def fn(e, r=r, j=j):
                        return e.transpose(
                            st.ptb[:, (j % 4) * 256:(j % 4) * 256 + 128],
                            st.ytm[:, r * D + j * 128:r * D + (j + 1) * 128],
                            st.ident[:, :])
                    fns.append(fn)
                pr.group("t", fns, rd=[st.ytmB[r], st.identB], wr=[st.ptbB])
                for j in range(j0, j0 + 4):
                    def fn(e, sl=sl, j=j):
                        return e.tensor_copy(kstg[:, sl * 1024 + j * 128:
                                                  sl * 1024 + (j + 1) * 128],
                                      st.ptb[:, (j % 4) * 256:(j % 4) * 256 + 128])
                    pr.emit("v", fn, rd=[st.ptbB], wr=[kstgB[sl][j0 // 4]])
                for j in range(j0, j0 + 4):
                    def fn(e, kt=kt, sl=sl, j=j):
                        return e.dma_start(
                            bass.AP(Kd.tensor, kt * 128 * 1024 + j * 128,
                                    [[1024, 128], [1, 128]]),
                            kstg[:, sl * 1024 + j * 128:sl * 1024 + (j + 1) * 128])
                    tok = pr.emit("q", fn, rd=[kstgB[sl][j0 // 4]], dma=True)
                    KdB[kt].r.append(tok)
            for half in range(2):
                bank, bankB = st.bank4()
                fns = []
                for ki in range(8):
                    def fn(e, s2=s2, gi=gi, ki=ki, half=half, bank=bank):
                        return e.matmul(
                            bank[:, 0:512],
                            kvxf[:, s2 * 2048 + ki * 256 + gi * 128:
                                 s2 * 2048 + ki * 256 + (gi + 1) * 128],
                            W["wv"][:, ki * 1024 + half * 512:ki * 1024 + (half + 1) * 512],
                            start=(ki == 0), stop=(ki == 7))
                    fns.append(fn)
                pr.group("t", fns, rd=[W["wvB"], kvxfB4[s2 * 16 + gi]],
                         wr=[bankB], extra=W["wvtoks"])
                for hh in range(8):
                    h = half * 8 + hh
                    def fn(e, hh=hh, h=h, bank=bank):
                        return e.tensor_copy(
                            vstg[:, VOFF + h * 65:VOFF + h * 65 + 64],
                            bank[:, hh * 64:hh * 64 + 64])
                    pr.emit("v", fn, rd=[bankB], wr=[vstgB])
            tok = pr.emit("q", lambda e, kt=kt: e.dma_start(
                bass.AP(Vd.tensor, kt * 128 * 1040, [[1040, 128], [1, 1040]]),
                vstg[:, VOFF:VOFF + 1040]), rd=[vstgB], dma=True)
            VdB[kt].r.append(tok)
            load_kv_slab(kt, 0)

    for hg in range(4):
        heads = range(hg * 4, hg * 4 + 4)
        if hg > 0:
            for kt in range(KT):
                load_kv_slab(kt, hg)
        if layer == "l3":
            st.l3_load_hg_bias(heads)
        for qb in range(T // 512):
            qrd = [qfmB[t] for t in range(qb * 4, qb * 4 + 4)]
            for hp in range(2):
                pvs = [(st.pb[4 + i2], st.pbB[4 + i2]) for i2 in range(2)]
                pend = None
                for j, kt in enumerate(range(KT)):
                    ba = (j % 2) * 2
                    fns = []
                    for i2 in range(2):
                        hi = 2 * hp + i2
                        h = hg * 4 + hi
                        bank = st.pb[ba + i2]
                        ks = kxl[64 * (h % 2):64 * (h % 2) + 64,
                                 kt * 256 + (hi // 2) * 128:
                                 kt * 256 + (hi // 2) * 128 + 128]
                        qs = qfm[64 * (h % 2):64 * (h % 2) + 64,
                                 (h // 2) * T + qb * 512:
                                 (h // 2) * T + (qb + 1) * 512]
                        def fn(e, bank=bank, ks=ks, qs=qs):
                            return e.matmul(bank[:, 0:512], ks, qs,
                                            start=True, stop=True)
                        fns.append(fn)
                    pr.group("t", fns, rd=[kxlB[kt]] + qrd,
                             wr=[st.pbB[ba], st.pbB[ba + 1]])
                    for i2 in range(2):
                        hi = 2 * hp + i2
                        r = ba + i2
                        bank, bankB = st.pb[r], st.pbB[r]
                        pro = st.aprobs[:, r * 512:(r + 1) * 512]
                        proB = st.aprobsB[r]
                        if layer == "l3":
                            bias = l3_bias(st, kt, qb, hi)
                        else:
                            bias = None
                        if bias is not None and bias[0] == "tile":
                            sco = st.asco[:, i2 * 512:(i2 + 1) * 512]
                            scoB = st.ascoB[i2]
                            pr.emit("v", lambda e, bank=bank, sco=sco, bias=bias:
                                    e.scalar_tensor_tensor(
                                sco, bank[:, 0:512], scale, bias[1],
                                ALU.mult, ALU.add),
                                rd=[bankB] + bias[2], wr=[scoB])
                            pr.emit("s", lambda e, sco=sco, pro=pro: e.activation(
                                pro, sco, AF.Exp), rd=[scoB], wr=[proB])
                        elif bias is not None and bias[0] == "const":
                            pr.emit("s", lambda e, bank=bank, pro=pro, bias=bias:
                                    e.activation(
                                pro, bank[:, 0:512], AF.Exp, bias=bias[1],
                                scale=scale), rd=[bankB] + bias[2], wr=[proB])
                        else:
                            pr.emit("s", lambda e, bank=bank, pro=pro: e.activation(
                                pro, bank[:, 0:512], AF.Exp, scale=scale),
                                rd=[bankB], wr=[proB])
                    if pend is not None:
                        pr.group("t", pend[0], rd=pend[1], wr=pend[2])
                    fns = []
                    rds = [vxlB[kt]]
                    wrs = []
                    for i2 in range(2):
                        hi = 2 * hp + i2
                        r = ba + i2
                        pro = st.aprobs[:, r * 512:(r + 1) * 512]
                        vs = vxl[:, kt * 260 + hi * 65:kt * 260 + (hi + 1) * 65]
                        pvb, pvbB = pvs[i2]
                        def fn(e, vs=vs, pro=pro, pvb=pvb,
                               first=(j == 0), last=(j == KT - 1)):
                            return e.matmul(pvb[0:65, 0:512], vs, pro,
                                            start=first, stop=last)
                        fns.append(fn)
                        rds.append(st.aprobsB[r])
                        wrs.append(pvbB)
                    pend = (fns, rds, wrs)
                pr.group("t", pend[0], rd=pend[1], wr=pend[2])
                for i2 in range(2):
                    hi = 2 * hp + i2
                    h = hg * 4 + hi
                    pvb, pvbB = pvs[i2]
                    pr.emit("s", lambda e, pvb=pvb: e.copy(st.aznrm[64:65, 0:512],
                                                           pvb[64:65, 0:512]),
                            rd=[pvbB], wr=[st.azrowB])
                    pr.emit("v", lambda e: e.reciprocal(st.aznrm[64:65, 512:1024],
                                                        st.aznrm[64:65, 0:512]),
                            rd=[st.azrowB], wr=[st.azrecB])
                    bank, bankB = st.pb[0], st.pbB[0]
                    pr.group("t", [lambda e, bank=bank: e.matmul(
                        bank[0:64, 0:512], st.aznrm[64:65, 1024:1088], st.aznrm[64:65, 512:1024],
                        start=True, stop=True)], rd=[st.aones64B, st.azrecB],
                        wr=[bankB])
                    pr.emit("s", lambda e, bank=bank: e.copy(st.aznrm[0:64, 0:512],
                                                             bank[0:64, 0:512]),
                            rd=[bankB], wr=[st.azbB])
                    r2 = hi % 2
                    stg = st.astg[:, r2 * 512:(r2 + 1) * 512]
                    pr.emit("v", lambda e, pvb=pvb, stg=stg: e.tensor_tensor(
                        stg[0:64, :], pvb[0:64, 0:512], st.aznrm[0:64, 0:512], ALU.mult),
                        rd=[pvbB, st.azbB], wr=[st.astgB[r2]])
                    ch = h // 2
                    off = (ch * 128 + 64 * (h % 2)) * T + qb * 512
                    tok = pr.emit("q", lambda e, stg=stg, off=off: e.dma_start(
                        bass.AP(attn_d.tensor, off, [[T, 64], [1, 512]]),
                        stg[0:64, :]),
                        rd=[st.astgB[r2]], dma=True)
                    attn_dB.r.append(tok)

    st.dbg_qfm = (qfm, qfmB)
    st.dbg_Kd = (Kd, KdB)
    st.dbg_Vd = (Vd, VdB)
    st.dbg_attnd = (attn_d, attn_dB)
    ffn_layer(st, woT, 8, 8, ("dram", attn_d, attn_dB),
              ("sbuf", st.yfm, st.yfB), None, OBG=8 if rope else 4)
    st.ln_out_add_fm(st.yfm, st.yfB)


def self_rope(st, tm, tmB, r, rT1, rT1B, rT2, rT2B):
    pr = st.pr
    u = st.arp_u[:, 0:1024]
    v = st.arp_v[:, 0:1024]
    t = tm[:, r * D:(r + 1) * D]
    pr.emit("v", lambda e, t=t: e.tensor_tensor(u, t, rT1[:, r * D:(r + 1) * D],
                                                ALU.mult),
            rd=[tmB[r], rT1B[r]], wr=[st.arp_uB])
    pr.emit("v", lambda e, t=t: e.tensor_tensor(v, t, rT2[:, r * D:(r + 1) * D],
                                                ALU.mult),
            rd=[tmB[r], rT2B[r]], wr=[st.arp_vB])
    def sv(base, off):
        return bass.AP(base.tensor if hasattr(base, "tensor") else base,
                       off, [[1024, 128], [64, 16], [1, 32]])
    ur = bass.AP(st.arp_u, 0, [[1024, 128], [64, 16], [1, 32]])
    ui = bass.AP(st.arp_u, 32, [[1024, 128], [64, 16], [1, 32]])
    vr = bass.AP(st.arp_v, 0, [[1024, 128], [64, 16], [1, 32]])
    vi = bass.AP(st.arp_v, 32, [[1024, 128], [64, 16], [1, 32]])
    tr = bass.AP(tm, r * D + 0, [[tm.shape[1], 128], [64, 16], [1, 32]])
    ti = bass.AP(tm, r * D + 32, [[tm.shape[1], 128], [64, 16], [1, 32]])
    pr.emit("v", lambda e: e.tensor_tensor(tr, ur, vi, ALU.subtract),
            rd=[st.arp_uB, st.arp_vB], wr=[tmB[r]])
    pr.emit("v", lambda e: e.tensor_tensor(ti, ui, vr, ALU.add),
            rd=[st.arp_uB, st.arp_vB], wr=[tmB[r]])


def l3_bias(st, kt, qb, hi):
    delta = kt * 128 - qb * 512
    if delta <= -256:
        return ("const", st.l3_fcol[:, hi * 3 + 0:hi * 3 + 1], [st.l3_fcolB])
    if 640 <= delta <= 1919:
        return ("const", st.l3_fcol[:, hi * 3 + 1:hi * 3 + 2], [st.l3_fcolB])
    if delta >= 2688:
        return ("const", st.l3_fcol[:, hi * 3 + 2:hi * 3 + 3], [st.l3_fcolB])
    if delta <= 512:
        woff, C = 0, 3456
    else:
        woff, C = 1152, 5504
    w0 = 4095 + delta - C
    ap = bass.AP(st.l3_strip, hi * 2432 + woff + w0,
                 [[4 * 2432, 128], [-1, 512]])
    return ("tile", ap, [st.l3_stripB])


L0A_WSPECS = {
    "L0_attn_w_qTP": (1024, 1024), "L0_attn_w_kTP": (1024, 1024),
    "L0_attn_w_vT": (1024, 1024), "L0_attn_w_oT": (1024, 1024),
    "ropeT1": (HALF, 1024), "ropeT2": (HALF, 1024),
    "ropeT1F": (S, 1024), "ropeT2F": (S, 1024),
}

L3A_WSPECS = {
    "L3_attn_WqT": (1024, 1024), "L3_attn_WkT": (1024, 1024),
    "L3_attn_WvT": (1024, 1024), "L3_attn_WoT": (1024, 1024),
    "L3_attn_rel_emb": (32, 16), "l3_ohT": (32, 8191),
}


def _rope_tables_jax():
    import jax
    import jax.numpy as jnp
    cpu = jax.local_devices(backend="cpu")[0]
    with jax.default_device(cpu):
        i = jnp.arange(32, dtype=jnp.float32)
        angles = jnp.power(jnp.asarray(10000.0, jnp.float32), 2.0 * i / 64)
        theta = jnp.arange(S, dtype=jnp.float32)[:, None] * angles[None, :]
        c = np.asarray(jnp.cos(theta))
        s = np.asarray(jnp.sin(theta))
    return c, s


def _perm_cols():
    idx = np.zeros(1024, np.int64)
    for h in range(16):
        for j in range(64):
            old = h * 64 + (2 * j if j < 32 else 2 * (j - 32) + 1)
            idx[h * 64 + j] = old
    return idx


def host_l0a_weights(I):
    cos, sin = _rope_tables_jax()
    C = np.zeros((S, 1024), np.float32)
    Sg = np.zeros((S, 1024), np.float32)
    for h in range(16):
        C[:, h * 64:h * 64 + 32] = cos
        C[:, h * 64 + 32:h * 64 + 64] = cos
        Sg[:, h * 64:h * 64 + 32] = -sin
        Sg[:, h * 64 + 32:h * 64 + 64] = sin
    T1 = C
    T2 = np.zeros((S, 1024), np.float32)
    for h in range(16):
        T2[:, h * 64:h * 64 + 32] = sin
        T2[:, h * 64 + 32:h * 64 + 64] = sin
    perm = _perm_cols()
    return {
        "L0_attn_w_qTP": np.ascontiguousarray(I["L0_attn_w_q"].T[:, perm]),
        "L0_attn_w_kTP": np.ascontiguousarray(I["L0_attn_w_k"].T[:, perm]),
        "L0_attn_w_vT": np.ascontiguousarray(I["L0_attn_w_v"].T),
        "L0_attn_w_oT": np.ascontiguousarray(I["L0_attn_w_o"].T),
        "ropeT1F": T1.astype(ml_dtypes.bfloat16),
        "ropeT2F": T2.astype(ml_dtypes.bfloat16),
    }


def host_l0a_percore(q_tok0):
    cos, sin = _rope_tables_jax()
    T1 = np.zeros((HALF, 1024), np.float32)
    T2 = np.zeros((HALF, 1024), np.float32)
    c = cos[q_tok0:q_tok0 + HALF]
    s = sin[q_tok0:q_tok0 + HALF]
    for h in range(16):
        T1[:, h * 64:h * 64 + 64] = np.concatenate([c, c], axis=1)
        T2[:, h * 64:h * 64 + 64] = np.concatenate([s, s], axis=1)
    return {"ropeT1": T1.astype(ml_dtypes.bfloat16),
            "ropeT2": T2.astype(ml_dtypes.bfloat16)}


def _bucket_np(rp):
    import math
    nb = 16
    rb = (rp > 0).astype(np.int64) * nb
    rp = np.abs(rp)
    me = nb // 2
    small = rp < me
    rpf = np.maximum(rp, 1).astype(np.float32)
    lg = me + (np.log(rpf / me) / math.log(128 / me) * (nb - me)).astype(np.int64)
    lg = np.minimum(lg, nb - 1)
    return rb + np.where(small, rp, lg)


def host_l3a_weights(I):
    return {
        "L3_attn_WqT": np.ascontiguousarray(I["L3_attn_Wq"].T),
        "L3_attn_WkT": np.ascontiguousarray(I["L3_attn_Wk"].T),
        "L3_attn_WvT": np.ascontiguousarray(I["L3_attn_Wv"].T),
        "L3_attn_WoT": np.ascontiguousarray(I["L3_attn_Wo"].T),
        "L3_attn_rel_emb": I["L3_attn_rel_emb"],
    }


def host_l3a_percore(q_tok0):
    i = np.arange(8191)
    bk = _bucket_np(i - 4095 - q_tok0)
    oh = np.zeros((32, 8191), ml_dtypes.bfloat16)
    oh[bk, i] = 1.0
    return {"l3_ohT": oh}


def build_stage_fullattn(layer, NT=16):
    nc = bass.Bass("TRN2", target_bir_lowering=False)
    T = NT * 128
    xin = nc.dram_tensor("xin", [T, D], F32, kind="ExternalInput")
    xfull = nc.dram_tensor("xfull", [S, D], F32, kind="ExternalInput")
    ident = nc.dram_tensor("ident", [128, 128], F32, kind="ExternalInput")
    xout = nc.dram_tensor("xout", [T, D], F32, kind="ExternalOutput")
    aps = {"xin": xin.ap(), "xfull": xfull.ap(), "ident": ident.ap(),
           "xout": xout.ap()}
    wspecs = L0A_WSPECS if layer == "l0" else L3A_WSPECS
    for name, shape in wspecs.items():
        aps[name] = nc.dram_tensor(name, list(shape), F32, kind="ExternalInput").ap()
    pr = PR(nc)
    st = Stage(pr, NT, xin=aps["xin"], xout=aps["xout"], arenas=("ffn",),
               wg_cols=8192 if layer == "l0" else 4096, slab_cols=4096)
    if layer == "l0":
        st.arp_u = pr.sb("arp_u", [128, 1024], BF16)
        st.arp_uB = Buf()
        st.arp_v = pr.sb("arp_v", [128, 1024], BF16)
        st.arp_vB = Buf()
    st.load_consts(aps["ident"])
    st.ln_in()
    emit_full_attn(st, aps, layer)
    toks = [(sm, v) for sm, v in pr.counts.items() if v > 0]
    pr.emit("g", lambda e: e.engine_nop(), extra=toks)
    pr.build_block()
    pr.close()
    return nc



L2FRONT_WSPECS = {"L2_attn_in_projT": (1024, 4096)}


def emit_l2front(st, aps, xz_d, xz_dB):
    ffn_layer(st, aps["L2_attn_in_projT"], 8, 32, ("sbuf", st.xln_fm, st.xlnfB),
              ("dram", xz_d, xz_dB), None, OBG=8)


def build_stage_l2front(NT=16):
    nc = bass.Bass("TRN2", target_bir_lowering=False)
    T = NT * 128
    xin = nc.dram_tensor("xin", [T, D], F32, kind="ExternalInput")
    ident = nc.dram_tensor("ident", [128, 128], F32, kind="ExternalInput")
    xz_out = nc.dram_tensor("xz_out", [4096, T], BF16, kind="ExternalOutput")
    aps = {"xin": xin.ap(), "ident": ident.ap()}
    for name, shape in L2FRONT_WSPECS.items():
        aps[name] = nc.dram_tensor(name, list(shape), F32, kind="ExternalInput").ap()
    pr = PR(nc)
    st = Stage(pr, NT, xin=aps["xin"], xout=None, arenas=("ffn",))
    st.load_consts(aps["ident"])
    st.ln_in()
    xzB = Buf()
    emit_l2front(st, aps, xz_out.ap(), xzB)
    toks = [(sm, v) for sm, v in pr.counts.items() if v > 0]
    pr.emit("g", lambda e: e.engine_nop(), extra=toks)
    pr.build_block()
    pr.close()
    return nc


L2SCAN_WSPECS = {
    "xcT": (2048, S), "zT": (1024, S),
    "conv_w": (2048, 4), "conv_b": (2048,),
    "x_projT": (2048, 96),
    "dt_projT_sl": (64, 1024), "dt_proj_b_sl": (1024,),
    "A_log_sl": (1024, 16), "D_sl": (1024,),
}


def build_stage_l2scan():
    L = S
    nc = bass.Bass("TRN2", target_bir_lowering=False)
    aps = {}
    for name, shape in L2SCAN_WSPECS.items():
        dt_ = BF16 if name in ("xcT",) else F32
        aps[name] = nc.dram_tensor(name, list(shape), dt_, kind="ExternalInput").ap()
    yg_out = nc.dram_tensor("yg_out", [1024, L], BF16, kind="ExternalOutput")
    pr = PR(nc)
    xs_d = nc.dram_tensor("xs_d", [1024, L], BF16, kind="Internal").ap()
    xs_dB = Buf()
    bc_d = nc.dram_tensor("bc_d", [32, L], F32, kind="Internal").ap()
    bc_dB = Buf()

    xc = pr.sb("xc", [128, L + 8], BF16); xcB = Buf()
    acc = pr.sb("acc", [128, L], F32); accB = Buf()
    xs = pr.sb("xs", [128, L], BF16); xsB = Buf()
    cw = pr.sb("cw", [128, 16 * 4], F32); cwB = Buf()
    cb = pr.sb("cb", [128, 16], F32); cbB = Buf()
    xdt = pr.sb("xdt", [64, L], F32); xdtB = Buf()
    xbc = pr.sb("xbc", [32, L], F32); xbcB = Buf()
    dtc = pr.sb("dtc", [128, L], F32); dtcB = Buf()
    uc = pr.sb("uc", [128, L], BF16); ucB = Buf()
    dA = pr.sb("dA", [128, L], F32); dAB = Buf()
    dbx = pr.sb("dbx", [128, L], BF16); dbxB = Buf()
    hh = pr.sb("hh", [128, L], BF16); hhB = Buf()
    yac = pr.sb("yac", [128, L], F32); yacB = Buf()
    zc = pr.sb("zc", [128, L], F32); zcB = Buf()
    brow = pr.sb("brow", [1, 2 * L], BF16); browB = Buf()
    xdtb = pr.sb("xdtb", [64, L], BF16); xdtbB = Buf()
    dtw = pr.sb("dtw", [64, 8 * 128], BF16); dtwB = Buf()
    dtb = pr.sb("dtb", [128, 8], F32); dtbB = Buf()
    Asb = pr.sb("Asb", [128, 8 * 16], F32); AsbB = Buf()
    Dsb = pr.sb("Dsb", [128, 8], F32); DsbB = Buf()
    xpw = pr.sb("xpw", [128, 16 * 96], BF16); xpwB = Buf()
    onesc = pr.sb("onesc", [128, 1], F32); onescB = Buf()
    onesr = pr.sb("onesr", [1, 128], BF16); onesrB = Buf()
    pb = [pr.ps(f"pb{i}", [128, 512], F32) for i in range(6)]
    pbB = [Buf(f"pb{i}") for i in range(6)]
    _rot = [0]

    def bank():
        i = _rot[0] % 6
        _rot[0] += 1
        return pb[i], pbB[i]

    pr.emit("v", lambda e: e.memset(onesc[:, :], 1.0), wr=[onescB])
    pr.emit("v", lambda e: e.memset(onesr[:, :], 1.0), wr=[onesrB])
    pr.emit("q", lambda e: e.dma_start(
        cw[:, :], bass.AP(aps["conv_w"].tensor, 0, [[4, 128], [512, 16], [1, 4]])),
        wr=[cwB], dma=True)
    col_vec_load(pr, aps["conv_b"], cb, cbB, 16)
    weng, wdep = wsrc(st, aps["x_projT"])
    pr.emit(weng, lambda e: e.dma_start(
        xpw[:, :], bass.AP(aps["x_projT"].tensor, 0, [[96, 128], [12288, 16], [1, 96]])),
        wr=[xpwB], rdall=wdep, dma=True)
    weng, wdep = wsrc(st, aps["dt_projT_sl"])
    pr.emit(weng, lambda e: e.dma_start(
        dtw[:, :], bass.AP(aps["dt_projT_sl"].tensor, 0, [[1024, 64], [1, 1024]])),
        wr=[dtwB], rdall=wdep, dma=True)
    col_vec_load(pr, aps["dt_proj_b_sl"], dtb, dtbB, 8)
    pr.emit("q", lambda e: e.dma_start(
        Asb[:, :], bass.AP(aps["A_log_sl"].tensor, 0, [[16, 128], [2048, 8], [1, 16]])),
        wr=[AsbB], dma=True)
    pr.emit("s", lambda e: e.activation(Asb[:, :], Asb[:, :], AF.Exp, scale=1.0),
            rd=[AsbB], wr=[AsbB])
    pr.emit("v", lambda e: e.tensor_scalar(Asb[:, :], Asb[:, :], -1.0, None, ALU.mult),
            rd=[AsbB], wr=[AsbB])
    col_vec_load(pr, aps["D_sl"], Dsb, DsbB, 8)

    first = [True]
    for ch in range(16):
        pr.emit("v", lambda e: e.memset(xc[:, 0:8], 0.0), wr=[xcB])
        pr.emit("q", lambda e, ch=ch: e.dma_start(
            xc[:, 8:8 + L], aps["xcT"][ch * 128:(ch + 1) * 128, :]),
            wr=[xcB], dma=True)
        pr.emit("v", lambda e, ch=ch: e.tensor_scalar(
            acc[:, :], xc[:, 5:5 + L], cw[:, ch * 4 + 0:ch * 4 + 1], None, ALU.mult),
            rd=[xcB, cwB], wr=[accB])
        for t in range(1, 4):
            def fn(e, ch=ch, t=t):
                return e.scalar_tensor_tensor(
                    acc[:, :], xc[:, 5 + t:5 + t + L], cw[:, ch * 4 + t:ch * 4 + t + 1],
                    acc[:, :], ALU.mult, ALU.add)
            pr.emit("v", fn, rd=[xcB, cwB, accB], wr=[accB])
        pr.emit("s", lambda e, ch=ch: e.activation(
            xs[:, :], acc[:, :], AF.Silu, bias=cb[:, ch:ch + 1], scale=1.0),
            rd=[accB, cbB], wr=[xsB])
        if ch < 8:
            tok = pr.emit_sw("q", lambda e, cc, ch=ch: e.dma_start(
                bass.AP(xs_d.tensor, ch * 128 * L, [[L, 128], [1, L]]), xs[:, :]),
                rd=[xsB], dma=True)
            xs_dB.r.append(tok)
        for lb in range(8):
            bank_, bankB_ = bank()
            pr.group("t", [lambda e, ch=ch, lb=lb, bank_=bank_: e.matmul(
                bank_[0:64, 0:512], xpw[:, ch * 96:ch * 96 + 64],
                xs[:, lb * 512:(lb + 1) * 512], start=True, stop=True)],
                rd=[xpwB, xsB], wr=[bankB_])
            if ch == 0:
                pr.emit("s", lambda e, lb=lb, bank_=bank_: e.copy(
                    xdt[:, lb * 512:(lb + 1) * 512], bank_[0:64, 0:512]),
                    rd=[bankB_], wr=[xdtB])
            else:
                pr.emit("v", lambda e, lb=lb, bank_=bank_: e.tensor_tensor(
                    xdt[:, lb * 512:(lb + 1) * 512], xdt[:, lb * 512:(lb + 1) * 512],
                    bank_[0:64, 0:512], ALU.add), rd=[bankB_, xdtB], wr=[xdtB])
            bank_, bankB_ = bank()
            pr.group("t", [lambda e, ch=ch, lb=lb, bank_=bank_: e.matmul(
                bank_[0:32, 0:512], xpw[:, ch * 96 + 64:ch * 96 + 96],
                xs[:, lb * 512:(lb + 1) * 512], start=True, stop=True)],
                rd=[xpwB, xsB], wr=[bankB_])
            if ch == 0:
                pr.emit("s", lambda e, lb=lb, bank_=bank_: e.copy(
                    xbc[:, lb * 512:(lb + 1) * 512], bank_[0:32, 0:512]),
                    rd=[bankB_], wr=[xbcB])
            else:
                pr.emit("v", lambda e, lb=lb, bank_=bank_: e.tensor_tensor(
                    xbc[:, lb * 512:(lb + 1) * 512], xbc[:, lb * 512:(lb + 1) * 512],
                    bank_[0:32, 0:512], ALU.add), rd=[bankB_, xbcB], wr=[xbcB])
    tok = pr.emit("q", lambda e: e.dma_start(bc_d, xbc[0:32, :]), rd=[xbcB], dma=True)
    bc_dB.r.append(tok)
    pr.emit("v", lambda e: e.tensor_copy(xdtb[:, :], xdt[:, :]), rd=[xdtB], wr=[xdtbB])

    for dch in range(8):
        for lb in range(8):
            bank_, bankB_ = bank()
            pr.group("t", [lambda e, dch=dch, lb=lb, bank_=bank_: e.matmul(
                bank_[:, 0:512], dtw[:, dch * 128:(dch + 1) * 128],
                xdtb[:, lb * 512:(lb + 1) * 512], start=True, stop=True)],
                rd=[dtwB, xdtbB], wr=[bankB_])
            pr.emit("s", lambda e, dch=dch, lb=lb, bank_=bank_: e.activation(
                dtc[:, lb * 512:(lb + 1) * 512], bank_[:, 0:512], AF.Exp,
                bias=dtb[:, dch:dch + 1], scale=1.0), rd=[bankB_, dtbB], wr=[dtcB])
        pr.emit("s", lambda e: e.activation(dtc[:, :], dtc[:, :], AF.Ln,
                                            bias=onesc[:, 0:1], scale=1.0),
                rd=[dtcB, onescB], wr=[dtcB])
        pr.emit_sw("q", lambda e, cc, dch=dch: e.dma_start(
            xs[:, :], bass.AP(xs_d.tensor, dch * 128 * L, [[L, 128], [1, L]])),
            rdall=[xs_dB], wr=[xsB], dma=True)
        pr.emit("v", lambda e: e.tensor_tensor(uc[:, :], dtc[:, :], xs[:, :], ALU.mult),
                rd=[dtcB, xsB], wr=[ucB])
        for n in range(16):
            pr.emit("g", lambda e, n=n: e.dma_start(
                brow[0:1, 0:L], bass.AP(bc_d.tensor, n * L, [[1, 1], [1, L]])),
                rdall=[bc_dB], wr=[browB], dma=True)
            pr.emit("g", lambda e, n=n: e.dma_start(
                brow[0:1, L:2 * L], bass.AP(bc_d.tensor, (16 + n) * L, [[1, 1], [1, L]])),
                rdall=[bc_dB], wr=[browB], dma=True)
            pr.emit("s", lambda e, dch=dch, n=n: e.activation(
                dA[:, :], dtc[:, :], AF.Exp,
                scale=Asb[:, dch * 16 + n:dch * 16 + n + 1]),
                rd=[dtcB, AsbB], wr=[dAB])
            for lb in range(8):
                bank_, bankB_ = bank()
                pr.group("t", [lambda e, lb=lb, bank_=bank_: e.matmul(
                    bank_[:, 0:512], onesr[0:1, :], brow[0:1, lb * 512:(lb + 1) * 512],
                    start=True, stop=True)], rd=[onesrB, browB], wr=[bankB_])
                pr.emit("v", lambda e, lb=lb, bank_=bank_: e.tensor_tensor(
                    dbx[:, lb * 512:(lb + 1) * 512], uc[:, lb * 512:(lb + 1) * 512],
                    bank_[:, 0:512], ALU.mult), rd=[ucB, bankB_], wr=[dbxB])
            pr.emit("v", lambda e: e.tensor_tensor_scan(
                hh[:, 0:2048], dA[:, 0:2048], dbx[:, 0:2048], 0.0,
                ALU.mult, ALU.add), rd=[dAB, dbxB], wr=[hhB])
            pr.emit("v", lambda e: e.tensor_tensor_scan(
                hh[:, 2048:4096], dA[:, 2048:4096], dbx[:, 2048:4096],
                hh[:, 2047:2048], ALU.mult, ALU.add), rd=[dAB, dbxB, hhB], wr=[hhB])
            for lb in range(8):
                bank_, bankB_ = bank()
                pr.group("t", [lambda e, lb=lb, bank_=bank_: e.matmul(
                    bank_[:, 0:512], onesr[0:1, :],
                    brow[0:1, L + lb * 512:L + (lb + 1) * 512],
                    start=True, stop=True)], rd=[onesrB, browB], wr=[bankB_])
                dst = yac if n == 0 else acc
                dstB = yacB if n == 0 else accB
                pr.emit("v", lambda e, lb=lb, bank_=bank_, dst=dst: e.tensor_tensor(
                    dst[:, lb * 512:(lb + 1) * 512], hh[:, lb * 512:(lb + 1) * 512],
                    bank_[:, 0:512], ALU.mult), rd=[hhB, bankB_], wr=[dstB])
            if n > 0:
                pr.emit("v", lambda e: e.tensor_tensor(yac[:, :], yac[:, :],
                                                       acc[:, :], ALU.add),
                        rd=[accB, yacB], wr=[yacB])
        pr.emit("v", lambda e, dch=dch: e.scalar_tensor_tensor(
            yac[:, :], xs[:, :], Dsb[:, dch:dch + 1], yac[:, :], ALU.mult, ALU.add),
            rd=[xsB, DsbB, yacB], wr=[yacB])
        pr.emit("q", lambda e, dch=dch: e.dma_start(
            zc[:, :], aps["zT"][dch * 128:(dch + 1) * 128, :]), wr=[zcB], dma=True)
        pr.emit("s", lambda e: e.activation(zc[:, :], zc[:, :], AF.Silu, scale=1.0),
                rd=[zcB], wr=[zcB])
        pr.emit("v", lambda e: e.tensor_tensor(yac[:, :], yac[:, :], zc[:, :],
                                               ALU.mult), rd=[zcB, yacB], wr=[yacB])
        tok = pr.emit("g", lambda e, dch=dch: e.dma_start(
            yg_out.ap()[dch * 128:(dch + 1) * 128, :], yac[:, :]),
            rd=[yacB], dma=True)
    toks = [(sm, v) for sm, v in pr.counts.items() if v > 0]
    pr.emit("g", lambda e: e.engine_nop(), extra=toks)
    pr.build_block()
    pr.close()
    return nc


def host_l2scan_percore(xz_row, dh, I):
    return {
        "xcT": np.ascontiguousarray(xz_row[0:2048][np.r_[dh * 1024:(dh + 1) * 1024, (1 - dh) * 1024:(2 - dh) * 1024]]),
        "zT": np.ascontiguousarray(xz_row[2048 + dh * 1024: 2048 + (dh + 1) * 1024]).astype(np.float32),
        "conv_w": np.ascontiguousarray(I["L2_attn_conv_w"][:, 0, :][np.r_[dh * 1024:(dh + 1) * 1024, (1 - dh) * 1024:(2 - dh) * 1024]]),
        "conv_b": np.ascontiguousarray(I["L2_attn_conv_b"][np.r_[dh * 1024:(dh + 1) * 1024, (1 - dh) * 1024:(2 - dh) * 1024]]),
        "x_projT": np.ascontiguousarray(I["L2_attn_x_proj_w"].T[np.r_[dh * 1024:(dh + 1) * 1024, (1 - dh) * 1024:(2 - dh) * 1024]]),
        "dt_projT_sl": np.ascontiguousarray(I["L2_attn_dt_proj_w"].T[:, dh * 1024:(dh + 1) * 1024]),
        "dt_proj_b_sl": I["L2_attn_dt_proj_b"][dh * 1024:(dh + 1) * 1024],
        "A_log_sl": I["L2_attn_A_log"][dh * 1024:(dh + 1) * 1024],
        "D_sl": I["L2_attn_D"][dh * 1024:(dh + 1) * 1024],
    }


L2OUT_WSPECS = {"L2_attn_out_projT": (2048, 1024), "ygT": (2048, 2048)}


def build_stage_l2out(NT=16):
    nc = bass.Bass("TRN2", target_bir_lowering=False)
    T = NT * 128
    xin = nc.dram_tensor("xin", [T, D], F32, kind="ExternalInput")
    ident = nc.dram_tensor("ident", [128, 128], F32, kind="ExternalInput")
    xout = nc.dram_tensor("xout", [T, D], F32, kind="ExternalOutput")
    aps = {"xin": xin.ap(), "ident": ident.ap(), "xout": xout.ap()}
    aps["L2_attn_out_projT"] = nc.dram_tensor("L2_attn_out_projT", [2048, 1024],
                                              F32, kind="ExternalInput").ap()
    ygT = nc.dram_tensor("ygT", [2048, T], BF16, kind="ExternalInput")
    pr = PR(nc)
    st = Stage(pr, NT, xin=aps["xin"], xout=aps["xout"], arenas=("ffn",),
               wg_cols=8192, slab_cols=8192)
    st.load_consts(aps["ident"])
    ygB = Buf()
    ffn_layer(st, aps["L2_attn_out_projT"], 16, 8, ("dram", ygT.ap(), ygB),
              ("sbuf", st.yfm, st.yfB), None, OBG=4)
    st.ln_out_add_fm(st.yfm, st.yfB)
    toks = [(sm, v) for sm, v in pr.counts.items() if v > 0]
    pr.emit("g", lambda e: e.engine_nop(), extra=toks)
    pr.build_block()
    pr.close()
    return nc



_PROGS = {}


def _prog(name, builder):
    if name not in _PROGS:
        _PROGS[name] = builder()
    return _PROGS[name]


def _run(nc, in_maps):
    return run_bass_kernel_spmd(nc, in_maps, core_ids=list(range(8)))


def _gather_x(res):
    out = np.empty((B, S, D), np.float32)
    for c in range(8):
        b, h = c // 2, c % 2
        out[b, h * HALF:(h + 1) * HALF] = res.results[c]["xout"]
    return out


def kernel_unfused(**inputs):
    import ml_dtypes
    I = {k: np.asarray(v) for k, v in inputs.items()}
    x = np.asarray(I["L0_attn_x"], np.float32)
    ident = np.eye(128, dtype=np.float32)

    def maps_tl(x, w, percore=None):
        ms = []
        for c in range(8):
            b, h = c // 2, c % 2
            m = {"xin": np.ascontiguousarray(x[b, h * HALF:(h + 1) * HALF]),
                 "ident": ident}
            if percore is not None:
                m.update(percore(b, h))
            m.update(w)
            ms.append(m)
        return ms

    w = host_l0a_weights(I)
    xc_ = x
    res = _run(_prog("l0a", lambda: build_stage_fullattn("l0", 16)),
               maps_tl(x, w, lambda b, h: dict(
                   xfull=np.ascontiguousarray(xc_[b]), **host_l0a_percore(h * HALF))))
    x = _gather_x(res)
    res = _run(_prog("l0f", lambda: build_stage_tl(emit_l0f, L0F_WSPECS, 16)),
               maps_tl(x, host_l0f_weights(I)))
    x = _gather_x(res)
    w = host_l1a_weights(I)
    xc_ = x
    res = _run(_prog("l1a", lambda: build_stage_l1a(16)),
               maps_tl(x, w, lambda b, h: host_l1a_percore(xc_[b], h * HALF)))
    x = _gather_x(res)
    res = _run(_prog("l1f", lambda: build_stage_l1f(16)),
               maps_tl(x, host_l1f_weights(I)))
    x = _gather_x(res)
    w = {"L2_attn_in_projT": np.ascontiguousarray(I["L2_attn_in_proj_w"].T)}
    res = _run(_prog("l2front", lambda: build_stage_l2front(16)), maps_tl(x, w))
    xz = np.empty((B, 4096, S), ml_dtypes.bfloat16)
    for c in range(8):
        b, h = c // 2, c % 2
        xz[b, :, h * HALF:(h + 1) * HALF] = res.results[c]["xz_out"]
    ms = []
    for c in range(8):
        b, dh = c // 2, c % 2
        ms.append(host_l2scan_percore(xz[b], dh, I))
    res = _run(_prog("l2scan", build_stage_l2scan), ms)
    yg = np.empty((B, 2048, S), ml_dtypes.bfloat16)
    for c in range(8):
        b, dh = c // 2, c % 2
        yg[b, dh * 1024:(dh + 1) * 1024] = res.results[c]["yg_out"]
    w = {"L2_attn_out_projT": np.ascontiguousarray(I["L2_attn_out_proj_w"].T)}
    res = _run(_prog("l2out", lambda: build_stage_l2out(16)),
               maps_tl(x, w, lambda b, h: {
                   "ygT": np.ascontiguousarray(yg[b, :, h * HALF:(h + 1) * HALF])}))
    x = _gather_x(res)
    res = _run(_prog("l2f", lambda: build_stage_tl(emit_l2f, L2F_WSPECS, 16)),
               maps_tl(x, host_l2f_weights(I)))
    x = _gather_x(res)
    w = host_l3a_weights(I)
    xc_ = x
    res = _run(_prog("l3a", lambda: build_stage_fullattn("l3", 16)),
               maps_tl(x, w, lambda b, h: dict(
                   xfull=np.ascontiguousarray(xc_[b]), **host_l3a_percore(h * HALF))))
    x = _gather_x(res)
    res = _run(_prog("l3f", lambda: build_stage_tl(emit_l3f, L3F_WSPECS, 16)),
               maps_tl(x, host_l3f_weights(I)))
    x = _gather_x(res)
    return x



def emit_l2scan_fused(st, aps, XZ, XZB, yg_local, yg_localB):
    pr = st.pr
    nc = pr.nc
    L = S
    xs_d = nc.dram_tensor("f_xs_d", [1024, L], BF16, kind="Internal").ap()
    xs_dB = Buf()
    xc = pr.sb("s_xc", [128, L + 8], BF16); xcB = Buf()
    acc = pr.sb("s_acc", [128, L], F32); accB = Buf()
    xs = pr.sb("s_xs", [128, L], BF16); xsB = Buf()
    cw = pr.sb("s_cw", [128, 16 * 4], F32); cwB = Buf()
    cb = pr.sb("s_cb", [128, 16], F32); cbB = Buf()
    xdt = pr.sb("s_xdt", [64, L], F32); xdtB = Buf()
    xbc = pr.sb("s_xbc", [32, L], F32); xbcB = Buf()
    xdtb = pr.sb("s_xdtb", [64, L], BF16); xdtbB = Buf()
    dtc = pr.sb("s_dtc", [128, L], F32); dtcB = Buf()
    uc = pr.sb("s_uc", [128, L], BF16); ucB = Buf()
    dA = pr.sb("s_dA", [128, L], F32); dAB = Buf()
    dbx = pr.sb("s_dbx", [128, L], BF16); dbxB = Buf()
    hh = pr.sb("s_hh", [128, L], BF16); hhB = Buf()
    yac = pr.sb("s_yac", [128, L], F32); yacB = Buf()
    zc = pr.sb("s_zc", [128, L], BF16); zcB = Buf()
    xbcb = pr.sb("s_xbcb", [32, L], BF16); xbcbB = Buf()
    bcsel = pr.sb("s_bcsel", [32, 4096], BF16); bcselB = Buf()
    dtw = pr.sb("s_dtw", [64, 8 * 128], BF16); dtwB = Buf()
    dtb = pr.sb("s_dtb", [128, 8], F32); dtbB = Buf()
    Asb = pr.sb("s_Asb", [128, 8 * 16], F32); AsbB = Buf()
    Dsb = pr.sb("s_Dsb", [128, 8], F32); DsbB = Buf()
    xpw = pr.sb("s_xpw", [128, 16 * 96], BF16); xpwB = Buf()
    onesc = pr.sb("s_onesc", [128, 1], F32); onescB = Buf()
    onesr = pr.sb("s_onesr", [1, 128], BF16); onesrB = Buf()

    pr.emit("v", lambda e: e.memset(onesc[:, :], 1.0), wr=[onescB])
    pr.emit("v", lambda e: e.memset(onesr[:, :], 1.0), wr=[onesrB])
    pr.emit("q" if aps["l2_bcsel"].dtype == BF16 else "g",
            lambda e: e.dma_start(bcsel[:, :], aps["l2_bcsel"]),
            wr=[bcselB], dma=True)
    pr.emit("q", lambda e: e.dma_start(
        cw[:, :], bass.AP(aps["conv_w"].tensor, 0, [[4, 128], [512, 16], [1, 4]])),
        wr=[cwB], dma=True)
    col_vec_load(pr, aps["conv_b"], cb, cbB, 16)
    weng, wdep = wsrc(st, aps["x_projT"])
    pr.emit(weng, lambda e: e.dma_start(
        xpw[:, :], bass.AP(aps["x_projT"].tensor, 0, [[96, 128], [12288, 16], [1, 96]])),
        wr=[xpwB], rdall=wdep, dma=True)
    weng, wdep = wsrc(st, aps["dt_projT_sl"])
    pr.emit(weng, lambda e: e.dma_start(
        dtw[:, :], bass.AP(aps["dt_projT_sl"].tensor, 0, [[1024, 64], [1, 1024]])),
        wr=[dtwB], rdall=wdep, dma=True)
    col_vec_load(pr, aps["dt_proj_b_sl"], dtb, dtbB, 8)
    pr.emit("q", lambda e: e.dma_start(
        Asb[:, :], bass.AP(aps["A_log_sl"].tensor, 0, [[16, 128], [2048, 8], [1, 16]])),
        wr=[AsbB], dma=True)
    pr.emit("s", lambda e: e.activation(Asb[:, :], Asb[:, :], AF.Exp, scale=1.0),
            rd=[AsbB], wr=[AsbB])
    pr.emit("v", lambda e: e.tensor_scalar(Asb[:, :], Asb[:, :], -1.0, None, ALU.mult),
            rd=[AsbB], wr=[AsbB])
    col_vec_load(pr, aps["D_sl"], Dsb, DsbB, 8)

    pr.sw_begin("q")
    for ch in range(16):
        pr.emit("v", lambda e: e.memset(xc[:, 0:8], 0.0), wr=[xcB])

        def fnv(e, cc, ch=ch):
            dh = cc % 2
            row = (dh * 1024 + ch * 128) if ch < 8 else ((1 - dh) * 1024 + (ch - 8) * 128)
            return e.dma_start(xc[:, 8:8 + L],
                               bass.AP(XZ.tensor, row * L, [[L, 128], [1, L]]))
        pr.emit_sw("q", fnv, rdall=[XZB], wr=[xcB], dma=True)
        pr.emit("v", lambda e, ch=ch: e.tensor_scalar(
            acc[:, :], xc[:, 5:5 + L], cw[:, ch * 4 + 0:ch * 4 + 1], None, ALU.mult),
            rd=[xcB, cwB], wr=[accB])
        for t in range(1, 4):
            def fn(e, ch=ch, t=t):
                return e.scalar_tensor_tensor(
                    acc[:, :], xc[:, 5 + t:5 + t + L], cw[:, ch * 4 + t:ch * 4 + t + 1],
                    acc[:, :], ALU.mult, ALU.add)
            pr.emit("v", fn, rd=[xcB, cwB, accB], wr=[accB])
        pr.emit("s", lambda e, ch=ch: e.activation(
            xs[:, :], acc[:, :], AF.Silu, bias=cb[:, ch:ch + 1], scale=1.0),
            rd=[accB, cbB], wr=[xsB])
        if ch < 8:
            tok = pr.emit_sw("q", lambda e, cc, ch=ch: e.dma_start(
                bass.AP(xs_d.tensor, ch * 128 * L, [[L, 128], [1, L]]), xs[:, :]),
                rd=[xsB], dma=True)
            xs_dB.r.append(tok)
        for lb in range(8):
            bank_, bankB_ = st.bank4()
            pr.group("t", [lambda e, ch=ch, lb=lb, bank_=bank_: e.matmul(
                bank_[0:64, 0:512], xpw[:, ch * 96:ch * 96 + 64],
                xs[:, lb * 512:(lb + 1) * 512], start=True, stop=True)],
                rd=[xpwB, xsB], wr=[bankB_])
            if ch == 0:
                pr.emit("s", lambda e, lb=lb, bank_=bank_: e.copy(
                    xdt[:, lb * 512:(lb + 1) * 512], bank_[0:64, 0:512]),
                    rd=[bankB_], wr=[xdtB])
            else:
                pr.emit("v", lambda e, lb=lb, bank_=bank_: e.tensor_tensor(
                    xdt[:, lb * 512:(lb + 1) * 512], xdt[:, lb * 512:(lb + 1) * 512],
                    bank_[0:64, 0:512], ALU.add), rd=[bankB_, xdtB], wr=[xdtB])
            bank_, bankB_ = st.bank4()
            pr.group("t", [lambda e, ch=ch, lb=lb, bank_=bank_: e.matmul(
                bank_[0:32, 0:512], xpw[:, ch * 96 + 64:ch * 96 + 96],
                xs[:, lb * 512:(lb + 1) * 512], start=True, stop=True)],
                rd=[xpwB, xsB], wr=[bankB_])
            if ch == 0:
                pr.emit("s", lambda e, lb=lb, bank_=bank_: e.copy(
                    xbc[:, lb * 512:(lb + 1) * 512], bank_[0:32, 0:512]),
                    rd=[bankB_], wr=[xbcB])
            else:
                pr.emit("v", lambda e, lb=lb, bank_=bank_: e.tensor_tensor(
                    xbc[:, lb * 512:(lb + 1) * 512], xbc[:, lb * 512:(lb + 1) * 512],
                    bank_[0:32, 0:512], ALU.add), rd=[bankB_, xbcB], wr=[xbcB])
    pr.sw_end()
    pr.emit("v", lambda e: e.tensor_copy(xdtb[:, :], xdt[:, :]), rd=[xdtB], wr=[xdtbB])
    pr.emit("v", lambda e: e.tensor_copy(xbcb[:, :], xbc[0:32, :]), rd=[xbcB], wr=[xbcbB])

    pr.sw_begin("q")
    for dch in range(8):
        for lb in range(8):
            bank_, bankB_ = st.bank4()
            pr.group("t", [lambda e, dch=dch, lb=lb, bank_=bank_: e.matmul(
                bank_[:, 0:512], dtw[:, dch * 128:(dch + 1) * 128],
                xdtb[:, lb * 512:(lb + 1) * 512], start=True, stop=True)],
                rd=[dtwB, xdtbB], wr=[bankB_])
            pr.emit("s", lambda e, dch=dch, lb=lb, bank_=bank_: e.activation(
                dtc[:, lb * 512:(lb + 1) * 512], bank_[:, 0:512], AF.Exp,
                bias=dtb[:, dch:dch + 1], scale=1.0), rd=[bankB_, dtbB], wr=[dtcB])
        pr.emit("s", lambda e: e.activation(dtc[:, :], dtc[:, :], AF.Ln,
                                            bias=onesc[:, 0:1], scale=1.0),
                rd=[dtcB, onescB], wr=[dtcB])
        pr.emit_sw("q", lambda e, cc, dch=dch: e.dma_start(
            xs[:, :], bass.AP(xs_d.tensor, dch * 128 * L, [[L, 128], [1, L]])),
            rdall=[xs_dB], wr=[xsB], dma=True)
        pr.emit("v", lambda e: e.tensor_tensor(uc[:, :], dtc[:, :], xs[:, :], ALU.mult),
                rd=[dtcB, xsB], wr=[ucB])
        for n in range(16):
            pr.emit("s", lambda e, dch=dch, n=n: e.activation(
                dA[:, :], dtc[:, :], AF.Exp,
                scale=Asb[:, dch * 16 + n:dch * 16 + n + 1]),
                rd=[dtcB, AsbB], wr=[dAB])
            for lb in range(8):
                bank_, bankB_ = st.bank4()
                pr.group("t", [lambda e, n=n, lb=lb, bank_=bank_: e.matmul(
                    bank_[:, 0:512], bcsel[:, n * 128:(n + 1) * 128],
                    xbcb[:, lb * 512:(lb + 1) * 512],
                    start=True, stop=True)], rd=[bcselB, xbcbB], wr=[bankB_])
                pr.emit("v", lambda e, lb=lb, bank_=bank_: e.tensor_tensor(
                    dbx[:, lb * 512:(lb + 1) * 512], uc[:, lb * 512:(lb + 1) * 512],
                    bank_[:, 0:512], ALU.mult), rd=[ucB, bankB_], wr=[dbxB])
            pr.emit("v", lambda e: e.tensor_tensor_scan(
                hh[:, 0:2048], dA[:, 0:2048], dbx[:, 0:2048], 0.0,
                ALU.mult, ALU.add), rd=[dAB, dbxB], wr=[hhB])
            pr.emit("v", lambda e: e.tensor_tensor_scan(
                hh[:, 2048:4096], dA[:, 2048:4096], dbx[:, 2048:4096],
                hh[:, 2047:2048], ALU.mult, ALU.add), rd=[dAB, dbxB, hhB], wr=[hhB])
            for lb in range(8):
                bank_, bankB_ = st.bank4()
                pr.group("t", [lambda e, n=n, lb=lb, bank_=bank_: e.matmul(
                    bank_[:, 0:512], bcsel[:, (16 + n) * 128:(17 + n) * 128],
                    xbcb[:, lb * 512:(lb + 1) * 512],
                    start=True, stop=True)], rd=[bcselB, xbcbB], wr=[bankB_])
                dst = yac if n == 0 else acc
                dstB = yacB if n == 0 else accB
                pr.emit("v", lambda e, lb=lb, bank_=bank_, dst=dst: e.tensor_tensor(
                    dst[:, lb * 512:(lb + 1) * 512], hh[:, lb * 512:(lb + 1) * 512],
                    bank_[:, 0:512], ALU.mult), rd=[hhB, bankB_], wr=[dstB])
            if n > 0:
                pr.emit("v", lambda e: e.tensor_tensor(yac[:, :], yac[:, :],
                                                       acc[:, :], ALU.add),
                        rd=[accB, yacB], wr=[yacB])
        pr.emit("v", lambda e, dch=dch: e.scalar_tensor_tensor(
            yac[:, :], xs[:, :], Dsb[:, dch:dch + 1], yac[:, :], ALU.mult, ALU.add),
            rd=[xsB, DsbB, yacB], wr=[yacB])

        def fnv(e, cc, dch=dch):
            dh = cc % 2
            row = 2048 + dh * 1024 + dch * 128
            return e.dma_start(zc[:, :], bass.AP(XZ.tensor, row * L, [[L, 128], [1, L]]))
        pr.emit_sw("q", fnv, rdall=[XZB], wr=[zcB], dma=True)
        pr.emit("s", lambda e: e.activation(zc[:, :], zc[:, :], AF.Silu, scale=1.0),
                rd=[zcB], wr=[zcB])
        pr.emit("v", lambda e: e.tensor_tensor(yac[:, :], yac[:, :], zc[:, :],
                                               ALU.mult), rd=[zcB, yacB], wr=[yacB])
        tok = pr.emit("g", lambda e, dch=dch: e.dma_start(
            bass.AP(yg_local.tensor, dch * 128 * L, [[L, 128], [1, L]]), yac[:, :]),
            rd=[yacB], dma=True)
        yg_localB.r.append(tok)
    pr.sw_end()


MEGA_NT = 16


def build_megakernel():
    nc = bass.Bass("TRN2", target_bir_lowering=False)
    T = MEGA_NT * 128
    xin = nc.dram_tensor("xin", [T, D], F32, kind="ExternalInput")
    xfull0 = nc.dram_tensor("xfull0", [S, D], F32, kind="ExternalInput")
    ident = nc.dram_tensor("ident", [128, 128], F32, kind="ExternalInput")
    xout = nc.dram_tensor("xout", [T, D], F32, kind="ExternalOutput")
    aps = {"ident": ident.ap()}
    allw = {}
    for spec in (L0A_WSPECS, L0F_WSPECS, L1A_WSPECS, L1F_WSPECS, L2FRONT_WSPECS,
                 L2F_WSPECS, L3A_WSPECS, L3F_WSPECS):
        allw.update(spec)
    allw.pop("ropeT1")
    allw.pop("ropeT2")
    BF16_IN = {"ropeT1F", "ropeT2F", "ropeT1", "ropeT2", "l1_masks", "l3_ohT",
               "l2_bcsel"}
    for name, shape in allw.items():
        dt_ = BF16 if name in BF16_IN else F32
        aps[name] = nc.dram_tensor(name, list(shape), dt_, kind="ExternalInput").ap()
    for name, shape in (("ropeT1", (HALF, 1024)), ("ropeT2", (HALF, 1024)),
                        ("l1_kvzero", (128, MEGA_NT + 2)),
                        ("conv_w", (2048, 4)), ("conv_b", (2048,)),
                        ("x_projT", (2048, 96)), ("dt_projT_sl", (64, 1024)),
                        ("dt_proj_b_sl", (1024,)), ("A_log_sl", (1024, 16)),
                        ("D_sl", (1024,)), ("l2_bcsel", (32, 4096)),
                        ("L2_attn_out_projT", (2048, 1024))):
        dt_ = BF16 if name in BF16_IN else F32
        aps[name] = nc.dram_tensor(name, list(shape), dt_, kind="ExternalInput").ap()

    xA = nc.dram_tensor("m_xA", [T, D], F32, kind="Internal").ap()
    xB = nc.dram_tensor("m_xB", [T, D], F32, kind="Internal").ap()
    xz_local = nc.dram_tensor("m_xzl", [4096, T], BF16, kind="Internal").ap()
    yg_local = nc.dram_tensor("m_ygl", [1024, S], BF16, kind="Internal").ap()
    R1 = nc.dram_tensor("m_R1", [S + 256, D], F32, kind="Internal", addr_space="Shared").ap()
    R2 = nc.dram_tensor("m_R2", [S + 256, D], F32, kind="Internal", addr_space="Shared").ap()
    XZ = nc.dram_tensor("m_XZ", [4096, S], BF16, kind="Internal", addr_space="Shared").ap()
    YG = nc.dram_tensor("m_YG", [2048, S], BF16, kind="Internal", addr_space="Shared").ap()
    FLG = nc.dram_tensor("m_FLG", [8, 256], U32, kind="Internal", addr_space="Shared").ap()

    pr = PR(nc)
    pr.need_polls = True
    st = Stage(pr, MEGA_NT, xin=xin.ap(), xout=xA, arenas=(), fm_scoped=True)
    evrow = pr.sb("m_evrow", [1, 8], U32)
    evrowB = Buf()
    fb = pr.sb("m_fb", [1, 256], U32)
    fbB = Buf()
    zrow, zrowB = st.scr, st.scrB
    for ev in range(8):
        pr.emit("v", lambda e, ev=ev: e.memset(evrow[0:1, ev:ev + 1],
                                               MAGIC + ev + 1), wr=[evrowB])
    pr.emit("v", lambda e: e.memset(zrow[:, :], 0.0), wr=[zrowB])
    st.load_consts(aps["ident"])
    attn_arenas(st, need_sco=True)
    st.wdeps = {}
    CACHE_W = ["L0_attn_w_qTP", "L0_attn_w_kTP", "L0_attn_w_vT", "L0_attn_w_oT",
               "L0_ffn_WT",
               "L1_attn_Wq", "L1_attn_Wk", "L1_attn_WkS", "L1_attn_Wv",
               "L1_attn_Wo",
               "L1_ffn_experts_wg", "L1_ffn_experts_wu", "L1_ffn_experts_wd",
               "L1_ffn_shared_wg", "L1_ffn_shared_wu", "L1_ffn_shared_wd",
               "L2_attn_in_projT", "x_projT", "dt_projT_sl",
               "L2_attn_out_projT",
               "L2_ffn_w1_wT", "L2_ffn_w2_wT", "L2_ffn_h1_wT", "L2_ffn_h2_wT",
               "L2_ffn_w3_wT",
               "L3_attn_WqT", "L3_attn_WkT", "L3_attn_WvT", "L3_attn_WoT"]
    _cl = [2]
    for name in CACHE_W:
        apf = aps[name]
        ct = nc.dram_tensor("wc_" + name, list(apf.tensor.shape), BF16,
                            kind="Internal").ap()
        cb = Buf()
        ln = _cl[0]
        _cl[0] = 5 - ln
        pr.emit("g", lambda e, ct=ct, apf=apf: e.dma_start(ct, apf),
                wr=[cb], dma=True, lane=ln)
        st.wdeps[ct.tensor.name] = cb
        aps[name] = ct
    for Rx in (R1, R2):
        for off in (0, (S + 128) * D):
            pr.emit("q", lambda e, Rx=Rx, off=off: e.dma_start(
                bass.AP(Rx.tensor, off, [[D, 128], [1, D]]), zrow[:, :]),
                rd=[zrowB], dma=True)

    R1B, R2B, XZB, YGB = Buf(), Buf(), Buf(), Buf()

    def d2d_xhalf(src_ap, dst, dstB, ev):
        C4 = T // 4
        toks = []
        pr.sw_begin("q")
        for c4 in range(4):
            def fnv(e, cc, c4=c4):
                off = (128 + (cc % 2) * HALF + c4 * C4) * D
                return e.dma_start(
                    bass.AP(dst.tensor, off, [[D, C4], [1, D]]),
                    bass.AP(src_ap.tensor, src_ap.offset + c4 * C4 * D,
                            [[D, C4], [1, D]]))
            toks.append(pr.emit_sw("q", fnv, dma=True))
        pr.sw_end()
        for tok in toks:
            dstB.r.append(tok)
        pr.barrier_event(ev, FLG, evrow, evrowB, toks)

    pr.mark("S1_l0a")
    pr.push_scope()
    st.alloc_fm()
    st.alloc_ffn(8192, 4096)
    st.arp_u = pr.sb("arp_u", [128, 1024], BF16); st.arp_uB = Buf()
    st.arp_v = pr.sb("arp_v", [128, 1024], BF16); st.arp_vB = Buf()
    a0 = dict(aps)
    a0["xfull"] = xfull0.ap()
    st.xin, st.xout = xin.ap(), xA
    st.ln_in()
    emit_full_attn(st, a0, "l0")
    pr.pop_scope()
    pr.mark("S2_l0f")
    pr.push_scope()
    st.alloc_fm()
    st.alloc_ffn(8192, 4096)
    st.xin, st.xout = xA, xB
    st.ln_in()
    emit_l0f(st, aps)
    pr.pop_scope()
    d2d_xhalf(xB, R1, R1B, 0)
    pr.mark("S3_l1a")
    pr.push_scope()
    st.alloc_fm()
    st.alloc_ffn(8192, 4096)
    st.xin, st.xout = xB, xA
    pr.poll_event(0, FLG, fb, fbB)
    pr.set_fence()
    st.ln_in()
    a1 = dict(aps)
    a1["xfull_sw"] = R1
    emit_l1a(st, a1)
    pr.pop_scope()
    pr.mark("S4_l1f")
    pr.push_scope()
    st.alloc_fm(y=False)
    acc = pr.sb("m_acc", [128, MEGA_NT * D], F32)
    accB = [Buf(f"macc{i}") for i in range(MEGA_NT)]
    st.xin, st.xout = xA, xB
    st.ln_in(keep_tm=(acc, accB))
    emit_l1f(st, aps, acc, accB)
    st.ln_out_add_tm(acc, accB)
    pr.pop_scope()
    pr.mark("S5_l2front")
    pr.push_scope()
    st.alloc_fm(y=False)
    st.alloc_ffn(8192, 4096)
    st.xin = xB
    st.ln_in()
    xzlB = Buf()
    ffn_layer(st, aps["L2_attn_in_projT"], 8, 32, ("sbuf", st.xln_fm, st.xlnfB),
              ("dram", xz_local, xzlB), None, OBG=8)
    toks = []
    pr.sw_begin("q")
    for c4 in range(4):
        def fnv(e, cc, c4=c4):
            return e.dma_start(
                bass.AP(XZ.tensor, (cc % 2) * HALF + c4 * 1024 * S,
                        [[S, 1024], [1, T]]),
                bass.AP(xz_local.tensor, c4 * 1024 * T, [[T, 1024], [1, T]]))
        toks.append(pr.emit_sw("q", fnv, rdall=[xzlB], dma=True))
    pr.sw_end()
    for tok in toks:
        XZB.r.append(tok)
    pr.barrier_event(1, FLG, evrow, evrowB, toks)
    pr.pop_scope()
    pr.mark("S6_l2scan")
    pr.push_scope()
    pr.poll_event(1, FLG, fb, fbB)
    pr.set_fence()
    yglB = Buf()
    emit_l2scan_fused(st, aps, XZ, XZB, yg_local, yglB)
    toks = []
    pr.sw_begin("q")
    for c4 in range(4):
        def fnv(e, cc, c4=c4):
            off = (cc % 2) * 1024 * S + c4 * 256 * S
            return e.dma_start(bass.AP(YG.tensor, off, [[S, 256], [1, S]]),
                               bass.AP(yg_local.tensor, c4 * 256 * S,
                                       [[S, 256], [1, S]]))
        toks.append(pr.emit_sw("q", fnv, rdall=[yglB], dma=True))
    pr.sw_end()
    for tok in toks:
        YGB.r.append(tok)
    pr.barrier_event(2, FLG, evrow, evrowB, toks)
    pr.pop_scope()
    pr.mark("S7_l2out")
    pr.push_scope()
    st.alloc_fm(xln=False)
    st.alloc_ffn(8192, 8192)
    st.xin, st.xout = xB, xA
    pr.poll_event(2, FLG, fb, fbB)
    pr.set_fence()
    ffn_layer(st, aps["L2_attn_out_projT"], 16, 8, ("dram_sw", YG, YGB, S),
              ("sbuf", st.yfm, st.yfB), None, OBG=4)
    st.ln_out_add_fm(st.yfm, st.yfB)
    pr.pop_scope()
    pr.mark("S8_l2f")
    pr.push_scope()
    st.alloc_fm()
    st.alloc_ffn(16384, 16384)
    st.alloc_src2()
    st.xin, st.xout = xA, xB
    st.ln_in()
    emit_l2f(st, aps)
    pr.pop_scope()
    d2d_xhalf(xB, R2, R2B, 3)
    pr.mark("S9_l3a")
    pr.push_scope()
    st.alloc_fm()
    st.alloc_ffn(4096, 4096)
    st.xin, st.xout = xB, xA
    pr.poll_event(3, FLG, fb, fbB)
    pr.set_fence()
    st.ln_in()
    a3 = dict(aps)
    a3["xfull"] = R1
    a3["xfull"] = bass.AP(R2.tensor, 128 * D, [[D, S], [1, D]])
    emit_full_attn(st, a3, "l3")
    pr.pop_scope()
    pr.mark("S10_l3f")
    pr.push_scope()
    st.alloc_fm()
    st.alloc_ffn(16384, 16384)
    st.alloc_tern()
    st.xin, st.xout = xA, xout.ap()
    st.ln_in()
    emit_l3f(st, aps)
    pr.pop_scope()
    toks = [(sm, v) for sm, v in pr.counts.items() if v > 0]
    pr.emit("g", lambda e: e.engine_nop(), extra=toks)
    pr.build_block()
    pr.close()
    return nc


def kernel_fused(**inputs):
    import ml_dtypes
    I = {k: np.asarray(v) for k, v in inputs.items()}
    x = np.asarray(I["L0_attn_x"], np.float32)
    ident = np.eye(128, dtype=np.float32)
    nc = _prog("mega", build_megakernel)
    w = {}
    w.update(host_l0a_weights(I))
    w.update(host_l0f_weights(I))
    w.update(host_l1a_weights(I))
    w.update(host_l1f_weights(I))
    w["L2_attn_in_projT"] = np.ascontiguousarray(I["L2_attn_in_proj_w"].T)
    w["L2_attn_out_projT"] = np.ascontiguousarray(I["L2_attn_out_proj_w"].T)
    w.update(host_l2f_weights(I))
    w.update(host_l3a_weights(I))
    w.update(host_l3f_weights(I))
    in_maps = []
    for c in range(8):
        b, h = c // 2, c % 2
        m = {"xin": np.ascontiguousarray(x[b, h * HALF:(h + 1) * HALF]),
             "xfull0": np.ascontiguousarray(x[b]), "ident": ident}
        m.update(host_l0a_percore(h * HALF))
        m.update({"l1_kvzero": host_l1a_percore(x[b], h * HALF)["l1_kvzero"]})
        m.update({k: v for k, v in host_l2scan_percore(
            np.zeros((4096, 1), ml_dtypes.bfloat16), h, I).items()
            if k not in ("xcT", "zT")})
        bcsel = np.zeros((32, 4096), ml_dtypes.bfloat16)
        for n_ in range(32):
            bcsel[n_, n_ * 128:(n_ + 1) * 128] = 1.0
        m["l2_bcsel"] = bcsel
        m.update(host_l3a_percore(h * HALF))
        m.update(w)
        in_maps.append(m)
    res = _run(nc, in_maps)
    out = np.empty((B, S, D), np.float32)
    for c in range(8):
        b, h = c // 2, c % 2
        out[b, h * HALF:(h + 1) * HALF] = res.results[c]["xout"]
    return out


kernel = kernel_fused
